# version 09
# baseline (speedup 1.0000x reference)
import numpy as np
import ml_dtypes
import concourse.bass as bass
import concourse.mybir as mybir
from concourse.bass_utils import run_bass_kernel_spmd
from contextlib import ExitStack

F32 = mybir.dt.float32
BF16 = mybir.dt.bfloat16
FP8 = mybir.dt.float8e4
AF = mybir.ActivationFunctionType
ALU = mybir.AluOpType
AX = mybir.AxisListType

NCORE = 8
T = 1024
TW = T + 2
D = 2048
EPS = 1e-5
I0 = 5632
I1 = 8192

bf = ml_dtypes.bfloat16
f8 = ml_dtypes.float8_e4m3fn

PIECES_T = ((0, 512), (512, 512))
UV_COPY_ONLY = False
SIM1 = False


class Op:
    __slots__ = ("engine", "fn", "waits", "async_inc", "seq", "sig_idx", "tag")

    def __init__(self, engine, fn, waits, async_inc, seq, tag):
        self.engine = engine
        self.fn = fn
        self.waits = waits
        self.async_inc = async_inc
        self.seq = seq
        self.sig_idx = None
        self.tag = tag


ENGINES = ("sync", "scalar", "vector", "tensor", "gpsimd")


def _flat(waits):
    out = []
    for x in waits:
        if x is None:
            continue
        if isinstance(x, (list, tuple)):
            out.extend(_flat(x))
        else:
            out.append(x)
    return out


class Sched:
    def __init__(self, nc):
        self.nc = nc
        self.ops = {e: [] for e in ENGINES}
        self.n = 0
        self._fence_cb = None

    def op(self, engine, fn, waits=(), async_inc=None, tag=""):
        o = Op(engine, fn, _flat(waits), async_inc, self.n, tag)
        self.n += 1
        self.ops[engine].append(o)
        return o

    def dma(self, engine, out, in_, waits=(), tag="", fence=True, **kw):
        w = _flat(waits)
        if fence and self._fence_cb is not None:
            eng_ops = {}
            other = []
            for x in w:
                if x.async_inc is None and x.engine in ("vector", "scalar", "gpsimd"):
                    eng_ops.setdefault(x.engine, []).append(x)
                else:
                    other.append(x)
            for e2, grp in eng_ops.items():
                other.append(self._fence_cb(self, e2, grp))
            w = other
        return self.op(engine, lambda eng: eng.dma_start(out, in_, **kw),
                       waits=w, async_inc=16, tag=tag)

    @staticmethod
    def _needs_sig(consumer_engine, producer):
        return True

    def emit(self, stack):
        K_SLOTS = {"sync": 6, "gpsimd": 6, "scalar": 2, "vector": 2,
                   "tensor": 1}
        need = set()
        for e in ENGINES:
            for o in self.ops[e]:
                if o.async_inc is not None:
                    need.add(o)
                for w in o.waits:
                    if self._needs_sig(e, w):
                        need.add(w)
        sig_total = {}
        self._slot = {}
        for e in ENGINES:
            cnt = {"c": 0}
            for k in range(K_SLOTS[e]):
                cnt[("a", k)] = 0
            aidx = 0
            cc_i = 0
            for o in self.ops[e]:
                if o in need:
                    if o.async_inc is not None:
                        if o.async_inc == 16:
                            sl = aidx % K_SLOTS[e]
                            aidx += 1
                        else:
                            sl = f"cc{cc_i}"
                            cc_i += 1
                            cnt.setdefault(("a", sl), 0)
                        self._slot[o] = sl
                        cnt[("a", sl)] += o.async_inc
                        o.sig_idx = cnt[("a", sl)]
                    else:
                        cnt["c"] += 1
                        o.sig_idx = cnt["c"]
            sig_total[(e, "c")] = cnt["c"]
            for k, v in cnt.items():
                if isinstance(k, tuple) and k[0] == "a":
                    sig_total[(e, "a", k[1])] = v
        idx = {e: 0 for e in ENGINES}
        done = set()
        total = sum(len(v) for v in self.ops.values())
        progressed = True
        while progressed:
            progressed = False
            for e in ENGINES:
                while idx[e] < len(self.ops[e]):
                    o = self.ops[e][idx[e]]
                    ok = all(w in done for w in o.waits)
                    if ok:
                        done.add(o)
                        idx[e] += 1
                        progressed = True
                    else:
                        break
        if len(done) != total:
            stuck = {e: (idx[e], len(self.ops[e]),
                         self.ops[e][idx[e]].tag if idx[e] < len(self.ops[e]) else "")
                     for e in ENGINES}
            raise RuntimeError(f"Sched deadlock: {stuck}")
        sems = {k: stack.enter_context(
                    self.nc.semaphore("sch_" + "_".join(str(y) for y in k)))
                for k in sig_total if sig_total[k] > 0}
        block = stack.enter_context(self.nc.Block())

        def make_body(e):
            def body(eng):
                cum = {}
                for o in self.ops[e]:
                    byeng = {}
                    for w in o.waits:
                        if self._needs_sig(e, w):
                            if w.async_inc is not None:
                                k = (w.engine, "a", self._slot[w])
                            else:
                                k = (w.engine, "c")
                            byeng[k] = max(byeng.get(k, 0), w.sig_idx)
                    if o.sig_idx is not None and o.async_inc is not None:
                        sl = self._slot[o]
                        k = (e, "a", sl)
                        byeng[k] = max(byeng.get(k, 0), cum.get(sl, 0))
                    for pe, v in byeng.items():
                        if v > 0:
                            eng.wait_ge(sems[pe], v)
                    inst = o.fn(eng)
                    if o.sig_idx is not None:
                        if o.async_inc is not None:
                            sl = self._slot[o]
                            inst.then_inc(sems[(e, "a", sl)], o.async_inc)
                            cum[sl] = o.sig_idx
                        else:
                            inst.then_inc(sems[(e, "c")], 1)
            return body

        for e in ENGINES:
            if self.ops[e]:
                getattr(block, e)(make_body(e))


class Ctx:
    def __init__(self):
        self.nc = bass.Bass("TRN2", target_bir_lowering=False)
        self.s = Sched(self.nc)
        self.stack = ExitStack()
        self._fence_sb = None

        self._last_fence = {}

        def fence_cb(s, engine, grp):
            if self._fence_sb is None:
                self._fence_sb = self.sb("fence_sb", [1, 16], F32)
            t = self._fence_sb
            grp = list(grp) + [self._last_fence.get(engine)]
            if engine == "scalar":
                o = s.op("scalar", lambda e: e.memzero(t[0:1, 0:4]),
                         waits=grp, tag="fenceS")
            elif engine == "gpsimd":
                o = s.op("gpsimd", lambda e: e.memset(t[0:1, 4:8], 0.0),
                         waits=grp, tag="fenceG")
            else:
                o = s.op("vector", lambda e: e.memset(t[0:1, 8:12], 0.0),
                         waits=grp, tag="fenceV")
            self._last_fence[engine] = o
            return o

        self.s._fence_cb = fence_cb

    def inp(self, name, shape, dtype=F32):
        return self.nc.dram_tensor(name, list(shape), dtype, kind="ExternalInput").ap()

    def out(self, name, shape, dtype=F32):
        return self.nc.dram_tensor(name, list(shape), dtype, kind="ExternalOutput").ap()

    cur = None

    def sb(self, name, shape, dtype=F32, stack=None):
        return (stack or self.cur or self.stack).enter_context(
            self.nc.sbuf_tensor(name, list(shape), dtype))

    def psum(self, name, shape, dtype=F32, stack=None):
        return (stack or self.cur or self.stack).enter_context(
            self.nc.psum_tensor(name, list(shape), dtype))

    def const_sb(self, name, arr):
        dt_map = {np.dtype(np.float32): F32, np.dtype(bf): BF16, np.dtype(f8): FP8}
        t = self.sb(name, arr.shape, dt_map[arr.dtype])
        dram = self.nc.inline_tensor(np.ascontiguousarray(arr), name=name + "_c").ap()
        op = self.s.dma("sync", t[:, :], dram, tag=f"ld_{name}")
        return t, op

    def smalls(self, specs):
        tiles, lds = {}, []
        for nm, ap_ in specs:
            t = self.sb(nm + "_sb", list(ap_.tensor.shape), F32)
            tiles[nm] = t
            lds.append(self.s.dma("sync", t[:, :], ap_, tag=f"ld{nm}"))
        return tiles, lds


def std_consts(cx, ident=False, mask128=False):
    c = {}
    ld = []
    if cx._fence_sb is None:
        cx._fence_sb = cx.sb("fence_sb", [1, 16], F32)
    t, o = cx.const_sb("ones_col", np.ones((128, 1), dtype=bf))
    c["ones_col"] = t; ld.append(o)
    t, o = cx.const_sb("ones_row", np.ones((1, 128), dtype=np.float32))
    c["ones_row"] = t; ld.append(o)
    t, o = cx.const_sb("onesrb", np.ones((1, 128), dtype=bf))
    c["ones_row_b"] = t; ld.append(o)
    t, o = cx.const_sb("eps_col", np.full((128, 1), EPS, dtype=np.float32))
    c["eps_col"] = t; ld.append(o)
    if ident:
        t, o = cx.const_sb("identb", np.eye(128, dtype=bf))
        c["ident"] = t; ld.append(o)
        t, o = cx.const_sb("identf2", np.eye(128, dtype=np.float32))
        c["identf"] = t; ld.append(o)
    if mask128:
        m = np.ones((128, T), dtype=bf)
        m[:, ::128] = 0
        t, o = cx.const_sb("mask128", m)
        c["mask128"] = t; ld.append(o)
    c["ld"] = ld
    return c


def build_ln(cx, c, xsb, xbsb, ntok, x_ready, pieces, scope, tagp=""):
    s = cx.s
    brow = cx.sb(f"brow{tagp}", [1, 4 * ntok], F32, stack=scope)
    bc = cx.sb(f"bc{tagp}", [128, 2 * ntok], F32, stack=scope)
    sq = cx.sb(f"sq{tagp}", [128, ntok], BF16, stack=scope)
    scope2 = ExitStack()
    stat_a = cx.psum(f"statA{tagp}", [1, ntok], F32, stack=scope2)
    stat_b = cx.psum(f"statB{tagp}", [1, ntok], F32, stack=scope2)

    cast_ops = []
    mm_sum = []
    mm_sq_last = None
    prev_sq_mm = None
    for ci in range(16):
        xc = xsb[:, ci * ntok:(ci + 1) * ntok]
        xbc = xbsb[:, ci * ntok:(ci + 1) * ntok]
        cop = s.op("vector", lambda v, a=xbc, b=xc: v.tensor_copy(a, b),
                   waits=[x_ready], tag=f"lncast{tagp}{ci}")
        cast_ops.append(cop)
        sop = s.op("scalar", lambda sc, b=xbc: sc.activation(sq[:, :], b, AF.Square),
                   waits=[cop, prev_sq_mm], tag=f"lnsq{tagp}{ci}")
        for h0, hw in pieces:
            mm_sum.append(s.op("tensor", lambda t, ci=ci, h0=h0, hw=hw, xbc=xbc:
                               t.matmul(stat_a[0:1, h0:h0 + hw],
                                        c["ones_col"][:, :], xbc[:, h0:h0 + hw],
                                        start=(ci == 0), stop=(ci == 15)),
                               waits=[cop, c["ld"]], tag=f"lnmmA{tagp}"))
        for h0, hw in pieces:
            prev_sq_mm = s.op("tensor", lambda t, ci=ci, h0=h0, hw=hw: t.matmul(
                stat_b[0:1, h0:h0 + hw], c["ones_col"][:, :], sq[:, h0:h0 + hw],
                start=(ci == 0), stop=(ci == 15)), waits=[sop], tag=f"lnmmB{tagp}")
        mm_sq_last = prev_sq_mm
    inv_d = 1.0 / D
    MEAN = brow[0:1, 0:ntok]
    RSTD = brow[0:1, ntok:2 * ntok]
    MB = brow[0:1, 2 * ntok:3 * ntok]
    SCR = brow[0:1, 3 * ntok:4 * ntok]
    o_mean = s.op("vector", lambda v: v.tensor_scalar_mul(
        MEAN, stat_a[0:1, :], inv_d), waits=[mm_sum[-1]], tag=f"mean{tagp}")
    o_m2 = s.op("scalar", lambda sc: sc.activation(
        SCR, MEAN, AF.Square), waits=[o_mean], tag=f"m2{tagp}")
    o_var = s.op("vector", lambda v: v.scalar_tensor_tensor(
        RSTD, stat_b[0:1, :], inv_d, SCR,
        ALU.mult, ALU.subtract), waits=[o_m2, mm_sq_last], tag=f"var{tagp}")
    scope2.close()
    bc_ps = cx.psum(f"bcps{tagp}", [128, ntok], F32, stack=scope)
    o_sqrt = s.op("scalar", lambda sc: sc.activation(
        RSTD, RSTD, AF.Sqrt, bias=c["eps_col"][0:1, :]),
        waits=[o_var], tag=f"sqrt{tagp}")
    o_rstd = s.op("vector", lambda v: v.reciprocal(
        RSTD, RSTD), waits=[o_sqrt], tag=f"rstd{tagp}")
    o_mb = s.op("vector", lambda v: v.scalar_tensor_tensor(
        MB, MEAN, -1.0, RSTD,
        ALU.mult, ALU.mult), waits=[o_rstd], tag=f"mb{tagp}")
    bc1mm = [s.op("tensor", lambda t, h0=h0, hw=hw: t.matmul(
        bc_ps[:, h0:h0 + hw], c["ones_row"][0:1, :],
        brow[0:1, ntok + h0:ntok + h0 + hw],
        start=True, stop=True), waits=[o_rstd, mm_sq_last], tag=f"bc1mm{tagp}")
        for h0, hw in pieces]
    cp1 = s.op("vector", lambda v: v.tensor_copy(bc[:, 0:ntok], bc_ps[:, :]),
               waits=bc1mm, tag=f"bccp1{tagp}")
    bc2mm = [s.op("tensor", lambda t, h0=h0, hw=hw: t.matmul(
        bc_ps[:, h0:h0 + hw], c["ones_row"][0:1, :],
        brow[0:1, 2 * ntok + h0:2 * ntok + h0 + hw],
        start=True, stop=True), waits=[o_mb, cp1], tag=f"bc2mm{tagp}")
        for h0, hw in pieces]
    cp2 = s.op("vector", lambda v: v.tensor_copy(bc[:, ntok:], bc_ps[:, :]),
               waits=bc2mm, tag=f"bccp2{tagp}")
    return bc, [cp1, cp2]


def ln_full(cx, c, xsb, xbsb, ntok, x_ready, pieces, tagp=""):
    scr = cx.sb(f"lnscr{tagp}", [128, ntok], F32)
    scope = ExitStack()
    bc, bc_ops = build_ln(cx, c, xsb, xbsb, ntok, x_ready, pieces, scope, tagp)
    apps = []
    prev = None
    for ci in range(16):
        xc = xsb[:, ci * ntok:(ci + 1) * ntok]
        oc = xbsb[:, ci * ntok:(ci + 1) * ntok]
        o1 = cx.s.op("vector", lambda v, xc=xc: v.tensor_mul(
            scr[:, :], xc, bc[:, 0:ntok]), waits=[bc_ops, prev],
            tag=f"app1{tagp}{ci}")
        o2 = cx.s.op("vector", lambda v, oc=oc: v.tensor_add(
            oc, scr[:, :], bc[:, ntok:]), waits=[o1], tag=f"app2{tagp}{ci}")
        apps.append(o2)
        prev = o2
    scope.close()
    return apps


def stream_matmul(cx, name, dram_tiled, nslab, K, rhs_fn, rhs_ready, evac_fn,
                  pieces, pst, sb_dtype=BF16, psum_prev=None, slab_pair=None,
                  first_waits=None, rhs_cover=None):
    s = cx.s
    nkc = K // 128
    if slab_pair is None:
        slab_pair = [cx.sb(f"{name}_sl{i}", [128, K], sb_dtype) for i in range(2)]
    if isinstance(slab_pair, dict):
        sp_tiles, sp_last = slab_pair["tiles"], slab_pair["lastmm"]
    else:
        sp_tiles, sp_last = slab_pair, [None, None]
    cast = dram_tiled.tensor.dtype != sb_dtype
    evacs = []
    for m in range(nslab):
        i = m % 2
        sl = sp_tiles[i]
        w = [sp_last[i]] if sp_last[i] is not None else (
            [first_waits] if first_waits else [])
        if dram_tiled.tensor.shape[1] == 128 and K != 128:
            ld = s.dma("gpsimd" if cast else "sync", sl[:, :], dram_tiled[m],
                       waits=w, tag=f"{name}_ld{m}")
        else:
            ld = s.dma("gpsimd" if cast else "sync",
                       sl[:, :].rearrange("p (kc m) -> p kc m", kc=nkc),
                       dram_tiled[m].rearrange("(kc p) m -> p kc m", p=128),
                       waits=w, tag=f"{name}_ld{m}")
        ps = pst[i]
        ev_prev = evacs[m - 2] if m >= 2 else (psum_prev[i] if psum_prev else [])
        col = 0
        mlast = None
        for h0, hw in pieces:
            for kc in range(nkc):
                if rhs_cover is not None:
                    wl = ([ld, rhs_cover, ev_prev]
                          if (kc == 0 and col == 0) else [])
                else:
                    wl = [ld, rhs_ready(kc)]
                    if kc == 0 and col == 0:
                        wl.append(ev_prev)
                mlast = s.op("tensor", lambda t, kc=kc, h0=h0, hw=hw, col=col,
                             sl=sl, ps=ps: t.matmul(
                                 ps[:, col:col + hw],
                                 sl[:, kc * 128:(kc + 1) * 128],
                                 rhs_fn(kc, h0, hw), start=(kc == 0),
                                 stop=(kc == nkc - 1)),
                             waits=wl, tag=f"{name}_mm{m}")
            col += hw
        sp_last[i] = mlast
        evacs.append(_flat([evac_fn(m, ps, pieces, [mlast])]))
    return evacs


def pieces_cols(pieces):
    cols = []
    col = 0
    for h0, hw in pieces:
        cols.append((col, h0, hw))
        col += hw
    return cols


def tile_lhsT(w_eff):
    K, M = w_eff.shape
    assert M % 128 == 0 and K % 128 == 0
    return np.ascontiguousarray(w_eff.reshape(K, M // 128, 128).transpose(1, 0, 2))


def tile_lhsT_p(w_eff):
    K, M = w_eff.shape
    assert M % 128 == 0 and K % 128 == 0
    a = w_eff.reshape(K // 128, 128, M // 128, 128)
    return np.ascontiguousarray(
        a.transpose(2, 1, 0, 3).reshape(M // 128, 128, K))


def col16(v):
    return np.ascontiguousarray(v.reshape(-1, 128).T)


def fm_layout(xt):
    n = xt.shape[1]
    return np.ascontiguousarray(
        xt.reshape(16, 128, n).transpose(1, 0, 2).reshape(128, 16 * n))


def fm_unlayout(a, n):
    return np.ascontiguousarray(
        a.reshape(128, 16, n).transpose(1, 0, 2).reshape(2048, n))


def shard_ranges():
    return [(c // 4, (c % 4) * T, (c % 4) * T + T) for c in range(NCORE)]


def prep_x_shards(x):
    shards = []
    for b, t0, t1 in shard_ranges():
        ext = np.zeros((TW, D), dtype=np.float32)
        lo = max(t0 - 1, 0)
        hi = min(t1 + 1, 4096)
        ext[1 - (t0 - lo):1 + (hi - t0)] = x[b, lo:hi]
        shards.append(fm_layout(np.ascontiguousarray(ext.T)))
    return shards


def build_launch_q(gate_rows, down_rows):
    cx = Ctx()
    s = cx.s
    wg = cx.inp("wg", [gate_rows, D])
    wu = cx.inp("wu", [gate_rows, D])
    wd = cx.inp("wd", [down_rows, I0])
    qg = cx.out("qg", [gate_rows, D], FP8)
    qu = cx.out("qu", [gate_rows, D], FP8)
    qd = cx.out("qd", [down_rows, I0], FP8)
    wsg = cx.out("wsg", [gate_rows, 1])
    wsu = cx.out("wsu", [gate_rows, 1])
    wsd = cx.out("wsd", [down_rows, 1])

    with cx.stack:
        def quant_one(tag, src, qdst, wsdst, rows, cols):
            wt = [cx.sb(f"wt_{tag}{i}", [128, cols], F32) for i in range(2)]
            qt = [cx.sb(f"qt_{tag}{i}", [128, cols], FP8) for i in range(2)]
            mx = cx.sb(f"mx_{tag}", [128, 8], F32)
            prev = {}
            for g in range((rows + 127) // 128):
                r0 = g * 128
                nr = min(128, rows - r0)
                i = g % 2
                ld = s.dma("sync", wt[i][0:nr, :], src[r0:r0 + nr, :],
                           waits=[prev.get(("q", i))], tag=f"q{tag}ld{g}")
                red = s.op("vector", lambda v, nr=nr, i=i: v.reduce_max(
                    mx[0:nr, 4 * i:4 * i + 1], wt[i][0:nr, :], AX.X,
                    apply_absolute_value=True),
                    waits=[ld, prev.get(("s2", i))], tag=f"q{tag}red{g}")
                rec = s.op("vector", lambda v, nr=nr, i=i: v.reciprocal(
                    mx[0:nr, 4 * i + 1:4 * i + 2], mx[0:nr, 4 * i:4 * i + 1]),
                    waits=[red], tag=f"q{tag}rec{g}")
                scl = s.op("vector", lambda v, nr=nr, i=i: v.tensor_scalar_mul(
                    mx[0:nr, 4 * i + 2:4 * i + 3],
                    mx[0:nr, 4 * i + 1:4 * i + 2], 224.0),
                    waits=[rec], tag=f"q{tag}scl{g}")
                wsv = s.op("vector", lambda v, nr=nr, i=i: v.tensor_scalar_mul(
                    mx[0:nr, 4 * i + 3:4 * i + 4], mx[0:nr, 4 * i:4 * i + 1],
                    1.0 / 224.0), waits=[red], tag=f"q{tag}wsv{g}")
                qop = s.op("scalar", lambda sc, nr=nr, i=i: sc.activation(
                    qt[i][0:nr, :], wt[i][0:nr, :], AF.Copy,
                    scale=mx[0:nr, 4 * i + 2:4 * i + 3]),
                    waits=[scl, prev.get(("s1", i))], tag=f"q{tag}q{g}")
                st1 = s.dma("sync", qdst[r0:r0 + nr, :], qt[i][0:nr, :],
                            waits=[qop], tag=f"q{tag}st1{g}")
                st2 = s.dma("sync", wsdst[r0:r0 + nr, :],
                            mx[0:nr, 4 * i + 3:4 * i + 4], waits=[wsv],
                            tag=f"q{tag}st2{g}")
                prev[("q", i)] = qop
                prev[("s1", i)] = st1
                prev[("s2", i)] = st2

        quant_one("g", wg, qg, wsg, gate_rows, D)
        quant_one("u", wu, qu, wsu, gate_rows, D)
        quant_one("d", wd, qd, wsd, down_rows, I0)
        s.emit(cx.stack)
    return cx.nc


def build_launch_a():
    cx = Ctx()
    s = cx.s
    x_in = cx.inp("x", [128, 16 * TW])
    wg1t = cx.inp("wg1t", [4, D, 128])
    wg2t = cx.inp("wg2t", [16, 512, 128])
    wut = cx.inp("wut", [16, D, 128])
    bg1 = cx.inp("bg1", [128, 4])
    bg2 = cx.inp("bg2", [128, 16])
    bu = cx.inp("bu", [128, 16])
    cw = cx.inp("cw", [128, 48])
    cb = cx.inp("cb", [128, 16])
    o_stream = cx.out("stream1p", [128, 16 * T])
    o_xb = cx.out("xbo", [128, 16 * T], BF16)
    o_gates = cx.out("gateso", [128, 16 * T], BF16)
    o_ug = cx.out("ugo", [128, 16 * T], BF16)
    o_tot = cx.out("toto", [128, 16])

    with cx.stack:
        c = std_consts(cx)
        xsb = cx.sb("xsb", [128, 16 * TW], F32)
        xb = cx.sb("xbl", [128, 16 * TW], BF16)
        hb = cx.sb("hb", [128, 4 * T], BF16)
        gates = cx.sb("gat", [128, 16 * T], BF16)
        ugs = [cx.sb(f"ug{i}", [128, T], BF16) for i in range(2)]
        tot = cx.sb("tot_sb", [128, 16], F32)
        sm, small_ld = cx.smalls([("bg1", bg1), ("bg2", bg2), ("bu", bu),
                                  ("cw", cw), ("cb", cb)])
        ldx = s.dma("sync", xsb[:, :], x_in, tag="ldx")

        pieces_tw = ((0, 512), (512, 512), (1024, 2))
        apps = ln_full(cx, c, xsb, xb, TW, [ldx, c["ld"]], pieces_tw, tagp="a")

        t32 = cx.sb("t32", [128, T], F32)
        conv_done = []
        prev = None
        for ci in range(16):
            xc = xb[:, ci * TW:(ci + 1) * TW]
            sc_ = xsb[:, ci * TW:(ci + 1) * TW]
            o1 = s.op("vector", lambda v, xc=xc, ci=ci: v.tensor_scalar_mul(
                t32[:, :], xc[:, 0:T], sm["cw"][:, 3 * ci:3 * ci + 1]),
                waits=[apps[ci], prev, small_ld], tag=f"cv1_{ci}")
            o2 = s.op("vector", lambda v, xc=xc, ci=ci: v.scalar_tensor_tensor(
                t32[:, :], xc[:, 1:1 + T], sm["cw"][:, 3 * ci + 1:3 * ci + 2],
                t32[:, :], ALU.mult, ALU.add), waits=[o1], tag=f"cv2_{ci}")
            o3 = s.op("vector", lambda v, xc=xc, ci=ci: v.scalar_tensor_tensor(
                t32[:, :], xc[:, 2:2 + T], sm["cw"][:, 3 * ci + 2:3 * ci + 3],
                t32[:, :], ALU.mult, ALU.add), waits=[o2], tag=f"cv3_{ci}")
            o4 = s.op("vector", lambda v, sc_=sc_, ci=ci: v.scalar_tensor_tensor(
                sc_[:, 1:1 + T], t32[:, :], sm["cb"][:, ci:ci + 1],
                sc_[:, 1:1 + T], ALU.add, ALU.add), waits=[o3], tag=f"cv4_{ci}")
            prev = o4
            conv_done.append(o4)

        pst = [cx.psum(f"pp{i}", [128, 1024], F32) for i in range(2)]

        def rhs_x(kc, h0, hw):
            return xb[:, kc * TW + 1 + h0: kc * TW + 1 + h0 + hw]

        def rhs_x_ready(kc):
            return apps[kc]

        def evac_h(m, ps, pieces, waits):
            ops = []
            for col, h0, hw in pieces_cols(pieces):
                ops.append(s.op("scalar", lambda sc, m=m, ps=ps, col=col, h0=h0,
                                hw=hw: sc.activation(
                                    hb[:, m * T + h0:m * T + h0 + hw],
                                    ps[:, col:col + hw], AF.Relu,
                                    bias=sm["bg1"][:, m:m + 1]),
                                waits=waits + [small_ld], tag=f"evh{m}"))
            return ops

        ev_h = stream_matmul(cx, "h", wg1t, 4, D, rhs_x, rhs_x_ready, evac_h,
                             PIECES_T, pst, first_waits=apps)

        def rhs_h(kc, h0, hw):
            return hb[:, kc * T + h0:kc * T + h0 + hw]

        def evac_g(m, ps, pieces, waits):
            ops = []
            for col, h0, hw in pieces_cols(pieces):
                ops.append(s.op("scalar", lambda sc, m=m, ps=ps, col=col, h0=h0,
                                hw=hw: sc.activation(
                                    gates[:, m * T + h0:m * T + h0 + hw],
                                    ps[:, col:col + hw], AF.Sigmoid,
                                    bias=sm["bg2"][:, m:m + 1]),
                                waits=waits, tag=f"evg{m}"))
            return ops

        ev_g = stream_matmul(cx, "gt", wg2t, 16, 512, rhs_h,
                             lambda kc: ev_h[kc], evac_g, PIECES_T, pst,
                             psum_prev=[ev_h[2], ev_h[3]])

        ug_stores = []

        def evac_u(m, ps, pieces, waits):
            i = m % 2
            prev_store = ug_stores[m - 2] if m >= 2 else None
            ops = []
            for col, h0, hw in pieces_cols(pieces):
                ops.append(s.op("vector", lambda v, m=m, ps=ps, col=col, h0=h0,
                                hw=hw, i=i: v.scalar_tensor_tensor(
                                    ugs[i][:, h0:h0 + hw], ps[:, col:col + hw],
                                    sm["bu"][:, m:m + 1],
                                    gates[:, m * T + h0:m * T + h0 + hw],
                                    ALU.add, ALU.mult),
                                waits=waits + [ev_g[m], prev_store, small_ld],
                                tag=f"evu{m}"))
            red = s.op("vector", lambda v, m=m, i=i: v.reduce_sum(
                tot[:, m:m + 1], ugs[i][:, :], AX.X), waits=ops, tag=f"tot{m}")
            st = s.dma("sync", o_ug[:, m * T:(m + 1) * T], ugs[i][:, :],
                       waits=[red], tag=f"stug{m}")
            ug_stores.append(st)
            return ops + [red]

        stream_matmul(cx, "u", wut, 16, D, rhs_x, rhs_x_ready, evac_u,
                      PIECES_T, pst, psum_prev=[ev_g[14], ev_g[15]])

        st1 = s.dma("sync",
                    o_stream[:, :].rearrange("p (c t) -> p c t", c=16),
                    xsb[:, :].rearrange("p (c t) -> p c t", c=16)[:, :, 1:1 + T],
                    waits=conv_done, tag="st_stream")
        st2 = s.dma("sync",
                    o_xb[:, :].rearrange("p (c t) -> p c t", c=16),
                    xb[:, :].rearrange("p (c t) -> p c t", c=16)[:, :, 1:1 + T],
                    waits=apps, tag="st_xb")
        st3 = s.dma("sync", o_gates, gates[:, :], waits=ev_g, tag="st_gates")
        st4 = s.dma("sync", o_tot, tot[:, :], waits=ug_stores, tag="st_tot")
        s.op("sync", lambda e: e.nop(), waits=[st1, st2, st3, st4] + ug_stores,
             tag="fin")
        s.emit(cx.stack)
    return cx.nc


def build_launch_b():
    cx = Ctx()
    s = cx.s
    streamp = cx.inp("streamp", [128, 16 * T])
    xb_in = cx.inp("xbi", [128, 16 * T], BF16)
    gates_in = cx.inp("gatesi", [128, 16 * T], BF16)
    ug_in = cx.inp("ugi", [128, 16 * T], BF16)
    totall = cx.inp("totall", [512, 16])
    pmask = cx.inp("pmask", [128, 4])
    wvt = cx.inp("wvt", [16, D, 128])
    wut = cx.inp("wut", [16, D, 128])
    bv = cx.inp("bv", [128, 16])
    o_stream = cx.out("stream1", [128, 16 * T])
    o_go = cx.out("godbg", [128, 16 * T], BF16)
    o_kc = cx.out("kcdbg", [128, T], BF16)
    o_vg = cx.out("vgdbg", [128, T], BF16)
    o_pfx = cx.out("pfxdbg", [128, 16])

    with cx.stack:
        c = std_consts(cx)
        xsb = cx.sb("xsb", [128, 16 * T], F32)
        xb = cx.sb("xbl", [128, 16 * T], BF16)
        gates = cx.sb("gat", [128, 16 * T], BF16)
        ug = cx.sb("ug", [128, 16 * T], BF16)
        go = gates
        vg = cx.sb("vg", [128, T], BF16)
        kc_t = cx.sb("kc_t", [128, T], BF16)
        zero_b = cx.sb("zerob", [128, T], BF16)
        tots = cx.sb("tots", [128, 64], F32)
        pfx = cx.sb("pfx", [128, 16], F32)
        pm = cx.sb("pm", [128, 4], F32)
        sm, small_ld = cx.smalls([("bv", bv)])

        ld_s = s.dma("sync", xsb[:, :], streamp, tag="ldS")
        ld_x = s.dma("sync", xb[:, :], xb_in, tag="ldxb")
        ld_g = s.dma("sync", gates[:, :], gates_in, tag="ldg")
        ld_u = s.dma("sync", ug[:, :], ug_in, tag="ldug")
        ld_t = s.dma("sync", tots[:, :].rearrange("p (g m) -> p g m", g=4),
                     totall.rearrange("(g p) m -> p g m", p=128), tag="ldtot")
        ld_p = s.dma("sync", pm[:, :], pmask, tag="ldpm")
        z0 = s.op("vector", lambda v: v.memset(zero_b[:, :], 0.0), tag="z0")

        pfx_ops = []
        o = s.op("vector", lambda v: v.memset(pfx[:, :], 0.0), tag="pfx0")
        for j in range(4):
            o = s.op("vector", lambda v, j=j, o=o: v.scalar_tensor_tensor(
                pfx[:, :], tots[:, j * 16:(j + 1) * 16], pm[:, j:j + 1],
                pfx[:, :], ALU.mult, ALU.add),
                waits=[ld_t, ld_p, o], tag=f"pfx{j}")
        pfx_ops.append(o)

        pst = [cx.psum(f"pp{i}", [128, 1024], F32) for i in range(2)]

        go_ops = [None] * 16

        def rhs_x(kc, h0, hw):
            return xb[:, kc * T + h0: kc * T + h0 + hw]

        def evac_v(m, ps, pieces, waits):
            ops = []
            for col, h0, hw in pieces_cols(pieces):
                ops.append(s.op("vector", lambda v, m=m, ps=ps, col=col, h0=h0,
                                hw=hw: v.scalar_tensor_tensor(
                                    vg[:, h0:h0 + hw], ps[:, col:col + hw],
                                    sm["bv"][:, m:m + 1],
                                    gates[:, m * T + h0:m * T + h0 + hw],
                                    ALU.add, ALU.mult),
                                waits=waits + [small_ld, ld_g, go_ops[m - 1] if m >= 1 else None],
                                tag=f"evv{m}"))
            scn = s.op("vector", lambda v, m=m: v.tensor_tensor_scan(
                kc_t[:, :], ug[:, m * T:(m + 1) * T], zero_b[:, :],
                0.0, ALU.add, ALU.add),
                waits=[ld_u, z0, pfx_ops, ops, go_ops[m - 1] if m >= 1 else None],
                tag=f"scan{m}")
            scn2 = s.op("vector", lambda v, m=m: v.tensor_scalar_add(
                kc_t[:, :], kc_t[:, :], pfx[:, m:m + 1]), waits=[scn],
                tag=f"scanp{m}")
            gop = s.op("vector", lambda v, m=m: v.tensor_mul(
                go[:, m * T:(m + 1) * T], kc_t[:, :], vg[:, :]),
                waits=[scn2], tag=f"go{m}")
            go_ops[m] = gop
            return ops + [scn, gop]

        ev_v = stream_matmul(cx, "v", wvt, 16, D, rhs_x, lambda kc: ld_x,
                             evac_v, PIECES_T, pst)
        sdb = [s.dma("sync", o_go, go[:, :], waits=go_ops, tag="dbgo"),
               s.dma("sync", o_kc, kc_t[:, :], waits=go_ops, tag="dbkc"),
               s.dma("sync", o_vg, vg[:, :], waits=go_ops, tag="dbvg"),
               s.dma("sync", o_pfx, pfx[:, :], waits=go_ops, tag="dbpf")]

        st_ops = []

        def evac_uv(m, ps, pieces, waits):
            ops = []
            for col, h0, hw in pieces_cols(pieces):
                if UV_COPY_ONLY:
                    ops.append(s.op("vector", lambda v, m=m, ps=ps, col=col,
                                    h0=h0, hw=hw: v.tensor_copy(
                                        xsb[:, m * T + h0:m * T + h0 + hw],
                                        ps[:, col:col + hw]),
                                    waits=waits + [ld_s], tag=f"evuv{m}"))
                else:
                    ops.append(s.op("vector", lambda v, m=m, ps=ps, col=col,
                                    h0=h0, hw=hw: v.tensor_add(
                                        xsb[:, m * T + h0:m * T + h0 + hw],
                                        xsb[:, m * T + h0:m * T + h0 + hw],
                                        ps[:, col:col + hw]),
                                    waits=waits + [ld_s], tag=f"evuv{m}"))

            st_ops.append(s.dma("sync", o_stream[:, m * T:(m + 1) * T],
                                xsb[:, m * T:(m + 1) * T], waits=ops,
                                tag=f"stuv{m}"))
            return ops

        stream_matmul(cx, "uv", wut, 16, D,
                      lambda kc, h0, hw: go[:, kc * T + h0:kc * T + h0 + hw],
                      lambda kc: go_ops[kc], evac_uv, PIECES_T, pst,
                      psum_prev=[ev_v[14], ev_v[15]])

        s.op("sync", lambda e: e.nop(), waits=st_ops + sdb, tag="fin")
        s.emit(cx.stack)
    return cx.nc




def rowmax_to_bcs(cx, c, M, m_ready, identf, tagp="", ntok=T, pre=None):
    s = cx.s
    nt = ntok // 128
    scope = ExitStack()
    if pre is None:
        tm = cx.sb(f"rm_tm{tagp}", [128, 4 * nt], F32)
        s448 = cx.sb(f"s448bc{tagp}", [128, ntok], F32)
        r448 = cx.sb(f"r448bc{tagp}", [128, ntok], F32)
    else:
        tm, s448, r448 = pre
    rowsb = cx.sb(f"rmrow{tagp}", [1, 2 * ntok], F32, stack=scope)
    pst = cx.psum(f"rm_ps{tagp}", [128, 128], F32, stack=scope)
    psr = cx.psum(f"rm_psr{tagp}", [1, 1024], F32, stack=scope)
    psb = cx.psum(f"rm_psb{tagp}", [128, 512], F32, stack=scope)
    prev_red = None
    tm_ops = []
    for tc in range(nt):
        tr = s.op("tensor", lambda t, tc=tc: t.transpose(
            pst[:, :], M[:, tc * 128:(tc + 1) * 128], identf[:, :]),
            waits=[m_ready, prev_red], tag=f"rmT{tagp}{tc}")
        red = s.op("vector", lambda v, tc=tc: v.reduce_max(
            tm[:, tc:tc + 1], pst[:, :], AX.X), waits=[tr], tag=f"rmR{tagp}{tc}")
        prev_red = red
        tm_ops.append(red)
    o_rec = s.op("vector", lambda v: v.reciprocal(tm[:, nt:2 * nt], tm[:, 0:nt]),
                 waits=tm_ops, tag=f"rmrec{tagp}")
    o_s448 = s.op("vector", lambda v: v.tensor_scalar_mul(
        tm[:, 2 * nt:3 * nt], tm[:, nt:2 * nt], 224.0), waits=[o_rec],
        tag=f"rms{tagp}")
    o_r448 = s.op("vector", lambda v: v.tensor_scalar_mul(
        tm[:, 3 * nt:4 * nt], tm[:, 0:nt], 1.0 / 224.0), waits=tm_ops,
        tag=f"rmr{tagp}")
    prev_cp = None
    row_ops = []
    for which, (src_c, dst0) in enumerate(((2 * nt, 0), (3 * nt, ntok))):
        for tc in range(nt):
            tr = s.op("tensor", lambda t, tc=tc, src_c=src_c: t.transpose(
                psr[0:1, tc * 128:(tc + 1) * 128],
                tm[:, src_c + tc:src_c + tc + 1], identf[:, :]),
                waits=[o_s448, o_r448, prev_cp], tag=f"rmTT{tagp}")
        cp = s.op("vector", lambda v, which=which, dst0=dst0: v.tensor_copy(
            rowsb[0:1, dst0:dst0 + ntok], psr[0:1, 0:ntok]),
            waits=[], tag=f"rmcp{tagp}{which}")
        cp.waits.extend(cx.s.ops["tensor"][-nt:])
        prev_cp = cp
        row_ops.append(cp)
    ready = []
    prev = None
    for which, (dst, r0) in enumerate(((s448, 0), (r448, ntok))):
        for h0 in range(0, ntok, 512):
            mm = s.op("tensor", lambda t, h0=h0, r0=r0: t.matmul(
                psb[:, 0:512], c["ones_row"][0:1, :],
                rowsb[0:1, r0 + h0:r0 + h0 + 512], start=True, stop=True),
                waits=[row_ops[which], prev], tag=f"rmbc{tagp}")
            cp = s.op("vector", lambda v, dst=dst, h0=h0: v.tensor_copy(
                dst[:, h0:h0 + 512], psb[:, 0:512]), waits=[mm],
                tag=f"rmbcc{tagp}")
            prev = cp
            ready.append(cp)
    scope.close()
    return s448, r448, ready


def ln_full32(cx, c, xsb, xout32, ntok, x_ready, pieces, tagp=""):
    scope = ExitStack()
    rot = [cx.sb(f"lncr{tagp}{i}", [128, ntok], BF16, stack=scope) for i in range(2)]
    scr = cx.sb(f"lnscr{tagp}", [128, ntok], F32, stack=scope)
    bc, bc_ops = build_ln_rot(cx, c, xsb, rot, ntok, x_ready, pieces, scope, tagp)
    apps = []
    prev = None
    for ci in range(16):
        xc = xsb[:, ci * ntok:(ci + 1) * ntok]
        oc = xout32[:, ci * ntok:(ci + 1) * ntok]
        o1 = cx.s.op("vector", lambda v, xc=xc: v.tensor_mul(
            scr[:, :], xc, bc[:, 0:ntok]), waits=[bc_ops, prev],
            tag=f"a321{tagp}{ci}")
        o2 = cx.s.op("vector", lambda v, oc=oc: v.tensor_add(
            oc, scr[:, :], bc[:, ntok:]), waits=[o1], tag=f"a322{tagp}{ci}")
        apps.append(o2)
        prev = o2
    scope.close()
    return apps


def build_ln_rot(cx, c, xsb, rot, ntok, x_ready, pieces, scope, tagp=""):
    s = cx.s
    brow = cx.sb(f"brow{tagp}", [1, 4 * ntok], F32, stack=scope)
    bc = cx.sb(f"bc{tagp}", [128, 2 * ntok], F32, stack=scope)
    sq = cx.sb(f"sq{tagp}", [128, ntok], BF16, stack=scope)
    scope2 = ExitStack()
    stat_a = cx.psum(f"statA{tagp}", [1, ntok], F32, stack=scope2)
    stat_b = cx.psum(f"statB{tagp}", [1, ntok], F32, stack=scope2)
    mm_sum = []
    mm_sq_last = None
    prev_sq_mm = None
    prev_sum_mm = {}
    for ci in range(16):
        xc = xsb[:, ci * ntok:(ci + 1) * ntok]
        xbc = rot[ci % 2]
        cop = s.op("vector", lambda v, a=xbc, b=xc: v.tensor_copy(a[:, :], b),
                   waits=[x_ready, prev_sum_mm.get(ci % 2)],
                   tag=f"lncast{tagp}{ci}")
        sop = s.op("scalar", lambda sc, b=xbc: sc.activation(
            sq[:, :], b[:, :], AF.Square), waits=[cop, prev_sq_mm],
            tag=f"lnsq{tagp}{ci}")
        last_sum = None
        for h0, hw in pieces:
            last_sum = s.op("tensor", lambda t, ci=ci, h0=h0, hw=hw, xbc=xbc:
                            t.matmul(stat_a[0:1, h0:h0 + hw],
                                     c["ones_col"][:, :], xbc[:, h0:h0 + hw],
                                     start=(ci == 0), stop=(ci == 15)),
                            waits=[cop], tag=f"lnmmA{tagp}")
            mm_sum.append(last_sum)
        prev_sum_mm[ci % 2] = last_sum
        for h0, hw in pieces:
            prev_sq_mm = s.op("tensor", lambda t, ci=ci, h0=h0, hw=hw: t.matmul(
                stat_b[0:1, h0:h0 + hw], c["ones_col"][:, :], sq[:, h0:h0 + hw],
                start=(ci == 0), stop=(ci == 15)), waits=[sop],
                tag=f"lnmmB{tagp}")
        mm_sq_last = prev_sq_mm
    inv_d = 1.0 / D
    MEAN = brow[0:1, 0:ntok]
    RSTD = brow[0:1, ntok:2 * ntok]
    MB = brow[0:1, 2 * ntok:3 * ntok]
    SCR = brow[0:1, 3 * ntok:4 * ntok]
    o_mean = s.op("vector", lambda v: v.tensor_scalar_mul(
        MEAN, stat_a[0:1, :], inv_d), waits=[mm_sum[-1]], tag=f"mean{tagp}")
    o_m2 = s.op("scalar", lambda sc: sc.activation(
        SCR, MEAN, AF.Square), waits=[o_mean], tag=f"m2{tagp}")
    o_var = s.op("vector", lambda v: v.scalar_tensor_tensor(
        RSTD, stat_b[0:1, :], inv_d, SCR,
        ALU.mult, ALU.subtract), waits=[o_m2, mm_sq_last], tag=f"var{tagp}")
    scope2.close()
    bc_ps = cx.psum(f"bcps{tagp}", [128, ntok], F32, stack=scope)
    o_sqrt = s.op("scalar", lambda sc: sc.activation(
        RSTD, RSTD, AF.Sqrt, bias=c["eps_col"][0:1, :]),
        waits=[o_var], tag=f"sqrt{tagp}")
    o_rstd = s.op("vector", lambda v: v.reciprocal(
        RSTD, RSTD), waits=[o_sqrt], tag=f"rstd{tagp}")
    o_mb = s.op("vector", lambda v: v.scalar_tensor_tensor(
        MB, MEAN, -1.0, RSTD, ALU.mult, ALU.mult), waits=[o_rstd],
        tag=f"mb{tagp}")
    bc1mm = [s.op("tensor", lambda t, h0=h0, hw=hw: t.matmul(
        bc_ps[:, h0:h0 + hw], c["ones_row"][0:1, :],
        brow[0:1, ntok + h0:ntok + h0 + hw],
        start=True, stop=True), waits=[o_rstd, mm_sq_last], tag=f"bc1mm{tagp}")
        for h0, hw in pieces]
    cp1 = s.op("vector", lambda v: v.tensor_copy(bc[:, 0:ntok], bc_ps[:, :]),
               waits=bc1mm, tag=f"bccp1{tagp}")
    bc2mm = [s.op("tensor", lambda t, h0=h0, hw=hw: t.matmul(
        bc_ps[:, h0:h0 + hw], c["ones_row"][0:1, :],
        brow[0:1, 2 * ntok + h0:2 * ntok + h0 + hw],
        start=True, stop=True), waits=[o_mb, cp1], tag=f"bc2mm{tagp}")
        for h0, hw in pieces]
    cp2 = s.op("vector", lambda v: v.tensor_copy(bc[:, ntok:], bc_ps[:, :]),
               waits=bc2mm, tag=f"bccp2{tagp}")
    return bc, [cp1, cp2]


def build_launch_c1():
    cx = Ctx()
    s = cx.s
    streami = cx.inp("streami", [128, 16 * T])
    qgt = cx.inp("qgt", [44, D, 128], FP8)
    qut = cx.inp("qut", [44, D, 128], FP8)
    wsg = cx.inp("wsg", [128, 44])
    wsu = cx.inp("wsu", [128, 44])
    o_t = cx.out("t_out", [128, 44 * T], BF16)
    o_mt = cx.out("mt", [128, T])

    with cx.stack:
        c = std_consts(cx)
        identf, idl = cx.const_sb("identf", np.eye(128, dtype=np.float32))
        c["ld"].append(idl)
        xsb = cx.sb("xsb", [128, 16 * T], F32)
        xf = cx.sb("xf", [128, 16 * T], F32)
        qx = cx.sb("qx", [128, 16 * T], FP8)
        Mx = cx.sb("Mx", [128, T], F32)
        Mt = cx.sb("Mt", [128, T], F32)
        smv, small_ld = cx.smalls([("wsg", wsg), ("wsu", wsu)])

        ld_s = s.dma("sync", xsb[:, :], streami, tag="ldS")
        apps = ln_full32(cx, c, xsb, xf, T, [ld_s, c["ld"]], PIECES_T, tagp="c")

        absx = cx.sb("absx", [128, T], F32)
        mops = []
        prev = None
        prev_abs = None
        for ci in range(16):
            ab = s.op("scalar", lambda sc, ci=ci: sc.activation(
                absx[:, :], xf[:, ci * T:(ci + 1) * T], AF.Abs),
                waits=[apps[ci], prev], tag=f"abx{ci}")
            if ci == 0:
                prev = s.op("vector", lambda v: v.tensor_copy(
                    Mx[:, :], absx[:, :]), waits=[ab], tag="mx0")
            else:
                prev = s.op("vector", lambda v: v.tensor_max(
                    Mx[:, :], Mx[:, :], absx[:, :]), waits=[ab, prev],
                    tag=f"mx{ci}")
            mops.append(prev)
        s448x, r448x, bc_rdy = rowmax_to_bcs(cx, c, Mx, mops[-1], identf,
                                             tagp="x")
        q_ops = []
        for ci in range(16):
            q_ops.append(s.op("vector", lambda v, ci=ci: v.tensor_mul(
                qx[:, ci * T:(ci + 1) * T], xf[:, ci * T:(ci + 1) * T],
                s448x[:, :]), waits=[apps[ci], bc_rdy], tag=f"qx{ci}"))

        slg = [cx.sb(f"slg{i}", [128, D], FP8) for i in range(2)]
        slu = [cx.sb(f"slu{i}", [128, D], FP8) for i in range(2)]
        psG = [cx.psum(f"psG{i}", [128, 1024], F32) for i in range(2)]
        psU = [cx.psum(f"psU{i}", [128, 1024], F32) for i in range(2)]
        g32 = cx.sb("g32", [128, T], F32)
        abst = cx.sb("abst", [128, T], BF16)
        sg32 = cx.sb("sg32", [128, T], F32)
        u32 = cx.sb("u32", [128, T], F32)
        tb = [cx.sb(f"tb{i}", [128, T], BF16) for i in range(2)]

        ev_hist = []
        mm_hist = []
        st_hist = []
        mt_ops = []
        cx._c1_sil = []
        for m in range(44):
            i = m % 2
            wprev = [mm_hist[m - 2]] if m >= 2 else [q_ops[-1]]
            ldg = s.dma("sync", slg[i][:, :].rearrange("p (kc m) -> p kc m", kc=16),
                        qgt[m].rearrange("(kc p) m -> p kc m", p=128),
                        waits=wprev, tag=f"ldg{m}")
            ldu = s.dma("sync", slu[i][:, :].rearrange("p (kc m) -> p kc m", kc=16),
                        qut[m].rearrange("(kc p) m -> p kc m", p=128),
                        waits=wprev, tag=f"ldu{m}")
            evp = ev_hist[m - 2] if m >= 2 else []
            lastG = lastU = None
            for ps, sl, ldx_ in ((psG[i], slg[i], ldg), (psU[i], slu[i], ldu)):
                for col, h0, hw in pieces_cols(PIECES_T):
                    for kc in range(16):
                        wl = [ldx_, q_ops[kc]]
                        if kc == 0 and col == 0:
                            wl.append(evp)
                        last = s.op("tensor", lambda t, ps=ps, sl=sl, kc=kc,
                                    col=col, h0=h0, hw=hw: t.matmul(
                                        ps[:, col:col + hw],
                                        sl[:, kc * 128:(kc + 1) * 128],
                                        qx[:, kc * T + h0:kc * T + h0 + hw],
                                        start=(kc == 0), stop=(kc == 15)),
                                    waits=wl, tag=f"c1mm{m}")
                if ps is psG[i]:
                    lastG = last
                else:
                    lastU = last
            mm_hist.append(lastU)
            eprev = ev_hist[m - 1][-1] if m >= 1 else None
            sprev = cx._c1_sil[m - 1] if m >= 1 else None
            e1 = s.op("vector", lambda v, m=m, i=i: v.scalar_tensor_tensor(
                g32[:, :], psG[i][:, :], smv["wsg"][:, m:m + 1], r448x[:, :],
                ALU.mult, ALU.mult), waits=[lastG, small_ld, bc_rdy, eprev, sprev],
                tag=f"evg{m}")
            e2 = s.op("scalar", lambda sc: sc.activation(
                sg32[:, :], g32[:, :], AF.Silu), waits=[e1, eprev], tag=f"sil{m}")
            cx._c1_sil.append(e2)
            e3 = s.op("vector", lambda v, m=m, i=i: v.scalar_tensor_tensor(
                u32[:, :], psU[i][:, :], smv["wsu"][:, m:m + 1], r448x[:, :],
                ALU.mult, ALU.mult), waits=[lastU, eprev], tag=f"evu{m}")
            e4 = s.op("vector", lambda v, m=m, i=i: v.tensor_mul(
                tb[i][:, :], sg32[:, :], u32[:, :]),
                waits=[e2, e3, st_hist[m - 2] if m >= 2 else None],
                tag=f"tmul{m}")
            ab = s.op("scalar", lambda sc, i=i: sc.activation(
                abst[:, :], tb[i][:, :], AF.Abs),
                waits=[e4, mt_ops[-1] if mt_ops else None], tag=f"abt{m}")
            if m == 0:
                e5 = s.op("vector", lambda v: v.tensor_copy(
                    Mt[:, :], abst[:, :]), waits=[ab], tag=f"mt{m}")
            else:
                e5 = s.op("vector", lambda v: v.tensor_max(
                    Mt[:, :], Mt[:, :], abst[:, :]), waits=[ab, mt_ops[-1]],
                    tag=f"mt{m}")
            mt_ops.append(e5)
            st = s.dma("sync", o_t[:, m * T:(m + 1) * T], tb[i][:, :],
                       waits=[e4], tag=f"stt{m}")
            st_hist.append(st)
            ev_hist.append([e1, e3, e4])
        stm = s.dma("sync", o_mt, Mt[:, :], waits=[mt_ops[-1]], tag="stmt")
        s.op("sync", lambda e: e.nop(), waits=[stm] + st_hist, tag="fin")
        s.emit(cx.stack)
    return cx.nc


def build_launch_c2():
    cx = Ctx()
    s = cx.s
    streami = cx.inp("streami", [128, 16 * T])
    t_in = cx.inp("t_in", [128, 44 * T], BF16)
    mt_in = cx.inp("mt_in", [128, T])
    qdt = cx.inp("qdt", [16, I0, 128], FP8)
    wsd = cx.inp("wsd", [128, 16])
    o_stream = cx.out("stream2", [128, 16 * T])

    with cx.stack:
        c = std_consts(cx)
        identf, idl = cx.const_sb("identf", np.eye(128, dtype=np.float32))
        c["ld"].append(idl)
        xsb = cx.sb("xsb", [128, 16 * T], F32)
        qt = cx.sb("qt", [128, 44 * T], FP8)
        Mt = cx.sb("Mt", [128, T], F32)
        trot = [cx.sb(f"trot{i}", [128, T], BF16) for i in range(2)]
        smv, small_ld = cx.smalls([("wsd", wsd)])

        ld_s = s.dma("sync", xsb[:, :], streami, tag="ldS")
        ld_m = s.dma("sync", Mt[:, :], mt_in, tag="ldM")
        s448t, r448t, bc_rdy = rowmax_to_bcs(cx, c, Mt, ld_m, identf, tagp="t")

        q_ops = []
        prev_q = {}
        for mi in range(44):
            i = mi % 2
            ldt = s.dma("sync", trot[i][:, :], t_in[:, mi * T:(mi + 1) * T],
                        waits=[prev_q.get(i)], tag=f"ldt{mi}")
            qo = s.op("vector", lambda v, mi=mi, i=i: v.tensor_mul(
                qt[:, mi * T:(mi + 1) * T], trot[i][:, :], s448t[:, :]),
                waits=[ldt, bc_rdy], tag=f"qt{mi}")
            prev_q[i] = qo
            q_ops.append(qo)

        pst = [cx.psum(f"pp{i}", [128, 1024], F32) for i in range(2)]
        st_ops = []

        dscr = cx.sb("dscr", [128, 1024], F32)

        def evac_d(m, ps, pieces, waits):
            ops = []
            for col, h0, hw in pieces_cols(pieces):
                o1 = s.op("vector", lambda v, m=m, ps=ps, col=col, hw=hw:
                          v.scalar_tensor_tensor(
                              dscr[:, col:col + hw],
                              ps[:, col:col + hw], smv["wsd"][:, m:m + 1],
                              r448t[:, col:col + hw], ALU.mult, ALU.mult),
                          waits=waits + [small_ld, bc_rdy], tag=f"evd{m}")
                o2 = s.op("vector", lambda v, m=m, h0=h0, hw=hw, col=col:
                          v.tensor_add(
                              xsb[:, m * T + h0:m * T + h0 + hw],
                              xsb[:, m * T + h0:m * T + h0 + hw],
                              dscr[:, col:col + hw]),
                          waits=[o1, ld_s], tag=f"evd2{m}")
                ops.append(o2)
            st_ops.append(s.dma("sync", o_stream[:, m * T:(m + 1) * T],
                                xsb[:, m * T:(m + 1) * T], waits=ops,
                                tag=f"std{m}"))
            return ops

        stream_matmul(cx, "dn", qdt, 16, I0,
                      lambda kc, h0, hw: qt[:, kc * T + h0:kc * T + h0 + hw],
                      lambda kc: q_ops[kc], evac_d, PIECES_T, pst,
                      sb_dtype=FP8, first_waits=bc_rdy)
        s.op("sync", lambda e: e.nop(), waits=st_ops, tag="fin")
        s.emit(cx.stack)
    return cx.nc




LNSCALE_DK = -3.1192868061458647


def ln_hbm(cx, c, x_dram, xbsb, ntok, pieces, tagp=""):
    s = cx.s
    scope = ExitStack()
    brow = cx.sb(f"brow{tagp}", [1, 4 * ntok], F32, stack=scope)
    bc = cx.sb(f"bc{tagp}", [128, 2 * ntok], F32, stack=scope)
    rotb = [cx.sb(f"lhb{tagp}{i}", [128, ntok], BF16, stack=scope) for i in range(2)]
    rotf = [cx.sb(f"lhf{tagp}{i}", [128, ntok], F32, stack=scope) for i in range(2)]
    sq = cx.sb(f"sq{tagp}", [128, ntok], BF16, stack=scope)
    scr = cx.sb(f"lnscr{tagp}", [128, ntok], F32, stack=scope)
    scope2 = ExitStack()
    stat_a = cx.psum(f"statA{tagp}", [1, ntok], F32, stack=scope2)
    stat_b = cx.psum(f"statB{tagp}", [1, ntok], F32, stack=scope2)
    mm_sum = []
    mm_sq_last = None
    prev_sq = None
    prev_mm = {}
    for ci in range(16):
        xbc = rotb[ci % 2]
        ld = s.dma("gpsimd", xbc[:, :], x_dram[:, ci * ntok:(ci + 1) * ntok],
                   waits=[prev_mm.get(ci % 2), c["ld"]], tag=f"lhld{tagp}{ci}")
        sop = s.op("scalar", lambda sc, b=xbc: sc.activation(
            sq[:, :], b[:, :], AF.Square), waits=[ld, prev_sq],
            tag=f"lnsq{tagp}{ci}")
        last = None
        for h0, hw in pieces:
            last = s.op("tensor", lambda t, ci=ci, h0=h0, hw=hw, xbc=xbc:
                        t.matmul(stat_a[0:1, h0:h0 + hw], c["ones_col"][:, :],
                                 xbc[:, h0:h0 + hw], start=(ci == 0),
                                 stop=(ci == 15)), waits=[ld], tag=f"lnA{tagp}")
            mm_sum.append(last)
        for h0, hw in pieces:
            prev_sq = s.op("tensor", lambda t, ci=ci, h0=h0, hw=hw: t.matmul(
                stat_b[0:1, h0:h0 + hw], c["ones_col"][:, :], sq[:, h0:h0 + hw],
                start=(ci == 0), stop=(ci == 15)), waits=[sop], tag=f"lnB{tagp}")
        prev_mm[ci % 2] = [last, prev_sq]
        mm_sq_last = prev_sq
    inv_d = 1.0 / D
    MEAN = brow[0:1, 0:ntok]
    RSTD = brow[0:1, ntok:2 * ntok]
    MB = brow[0:1, 2 * ntok:3 * ntok]
    SCR = brow[0:1, 3 * ntok:4 * ntok]
    o_mean = s.op("vector", lambda v: v.tensor_scalar_mul(
        MEAN, stat_a[0:1, :], inv_d), waits=[mm_sum[-1]], tag=f"mean{tagp}")
    o_m2 = s.op("scalar", lambda sc: sc.activation(SCR, MEAN, AF.Square),
                waits=[o_mean], tag=f"m2{tagp}")
    o_var = s.op("vector", lambda v: v.scalar_tensor_tensor(
        RSTD, stat_b[0:1, :], inv_d, SCR, ALU.mult, ALU.subtract),
        waits=[o_m2, mm_sq_last], tag=f"var{tagp}")
    scope2.close()
    bc_ps = cx.psum(f"bcps{tagp}", [128, ntok], F32, stack=scope)
    o_sqrt = s.op("scalar", lambda sc: sc.activation(
        RSTD, RSTD, AF.Sqrt, bias=c["eps_col"][0:1, :]), waits=[o_var],
        tag=f"sqrt{tagp}")
    o_rstd = s.op("vector", lambda v: v.reciprocal(RSTD, RSTD), waits=[o_sqrt],
                  tag=f"rstd{tagp}")
    o_mb = s.op("vector", lambda v: v.scalar_tensor_tensor(
        MB, MEAN, -1.0, RSTD, ALU.mult, ALU.mult), waits=[o_rstd], tag=f"mb{tagp}")
    bc1 = [s.op("tensor", lambda t, h0=h0, hw=hw: t.matmul(
        bc_ps[:, h0:h0 + hw], c["ones_row"][0:1, :],
        brow[0:1, ntok + h0:ntok + h0 + hw], start=True, stop=True),
        waits=[o_rstd, mm_sq_last], tag=f"bc1{tagp}") for h0, hw in pieces]
    cp1 = s.op("vector", lambda v: v.tensor_copy(bc[:, 0:ntok], bc_ps[:, :]),
               waits=bc1, tag=f"bcc1{tagp}")
    bc2 = [s.op("tensor", lambda t, h0=h0, hw=hw: t.matmul(
        bc_ps[:, h0:h0 + hw], c["ones_row"][0:1, :],
        brow[0:1, 2 * ntok + h0:2 * ntok + h0 + hw], start=True, stop=True),
        waits=[o_mb, cp1], tag=f"bc2{tagp}") for h0, hw in pieces]
    cp2 = s.op("vector", lambda v: v.tensor_copy(bc[:, ntok:], bc_ps[:, :]),
               waits=bc2, tag=f"bcc2{tagp}")
    apps = []
    prev_o1 = {}
    prev = None
    for ci in range(16):
        xfc = rotf[ci % 2]
        ld = s.dma("sync", xfc[:, :], x_dram[:, ci * ntok:(ci + 1) * ntok],
                   waits=[prev_o1.get(ci % 2)], tag=f"lhf{tagp}{ci}")
        o1 = s.op("vector", lambda v, xfc=xfc: v.tensor_mul(
            scr[:, :], xfc[:, :], bc[:, 0:ntok]), waits=[ld, cp1, cp2, prev],
            tag=f"ap1{tagp}{ci}")
        o2 = s.op("vector", lambda v, ci=ci: v.tensor_add(
            xbsb[:, ci * ntok:(ci + 1) * ntok], scr[:, :], bc[:, ntok:]),
            waits=[o1], tag=f"ap2{tagp}{ci}")
        prev_o1[ci % 2] = o1
        prev = o2
        apps.append(o2)
    scope.close()
    return apps


def build_launch_d():
    cx = Ctx()
    s = cx.s
    streami = cx.inp("streami", [128, 16 * T])
    wq_t = cx.inp("wq_t", [16, D, 128])
    wk_t = cx.inp("wk_t", [16, D, 128])
    wv_nat = cx.inp("wv_nat", [D, 1024])
    wgk1p = cx.inp("wgk1p", [128, 256])
    wgk2 = cx.inp("wgk2", [16, D])
    bgk2 = cx.inp("bgk2", [128, 16])
    o_o = cx.out("o_tm", [128, 8 * T], BF16)
    o_qe = cx.out("qe_all", [128, 16 * T], BF16)
    o_sp = cx.out("spack", [128, 4096])
    o_a = cx.out("a_out", [128, 16])
    o_cc = cx.out("cc_out", [128, 128])
    o_xb = cx.out("xb_out", [128, 16 * T], BF16)

    with cx.stack:
        c = std_consts(cx, ident=True, mask128=True)
        tril, trl = cx.const_sb("trilc", np.triu(np.ones((128, 128), np.float32)))
        m8 = np.ones((128, 32), np.float32)
        m8[:, ::8] = 0.0
        mask8, m8l = cx.const_sb("mask8c", m8)
        onec, onl = cx.const_sb("onecol", np.ones((128, 1), np.float32))
        c["ld"] += [trl, m8l, onl]
        xb = cx.sb("xb", [128, 16 * T], BF16)
        v_tm = cx.sb("v_tm", [128, 8 * T], BF16)
        o_sb = cx.sb("o_sb", [128, 8 * T], BF16)
        spack = cx.sb("spack_sb", [128, 4096], F32)
        sb_S = cx.sb("sbS", [128, 1024], BF16)
        fm1 = cx.sb("fm1", [16, T], BF16)
        qe_h = cx.sb("qe_h", [128, 4 * T], BF16)
        kd_h = cx.sb("kd_h", [128, 4 * T], BF16)
        kdtm = cx.sb("kdtm", [128, 8 * 512], BF16)
        ebt = cx.sb("ebt", [128, T], BF16)
        embt = cx.sb("embt", [128, T], BF16)
        gkp = cx.sb("gkp", [128, T], F32)
        sA = cx.sb("sAd", [128, T], F32)
        sB = cx.sb("sBd", [128, T], F32)
        m0t = cx.sb("m0t", [128, T], F32)
        b_t = cx.sb("b_td", [128, T], F32)
        zs = cx.sb("zs", [128, 32], F32)
        bl = cx.sb("bl", [128, 32], F32)
        zsh = cx.sb("zsh", [128, 32], F32)
        wexp = cx.sb("wexp", [128, 32], F32)
        ebias = cx.sb("ebias", [128, 8], F32)
        asb = cx.sb("asbd", [128, 128], BF16)
        misc = cx.sb("miscd", [128, 64], F32)
        a_sb = cx.sb("a_sb", [128, 16], F32)
        cc_sb = cx.sb("cc_sb", [128, 128], F32)
        slq = [cx.sb(f"slq{i}", [128, D], BF16) for i in range(2)]
        wgk1_sb = cx.sb("wgk1sb", [128, 256], BF16)
        wgk2_sb = cx.sb("wgk2sb", [16, D], BF16)
        smv, small_ld = cx.smalls([("bgk2", bgk2)])

        apps = ln_hbm(cx, c, streami, xb, T, PIECES_T, tagp="d")
        st_xb = s.dma("sync", o_xb, xb[:, :], waits=apps, tag="stxb")
        ld1 = s.dma("gpsimd", wgk1_sb[:, :], wgk1p, tag="ldgk1")
        ld2 = s.dma("gpsimd", wgk2_sb[:, :], wgk2, tag="ldgk2")

        scope_v = ExitStack()
        wv_sb = cx.sb("wv_sb", [128, 16 * 1024], BF16, stack=scope_v)
        psv = [cx.psum(f"psv{i}", [128, 1024], F32, stack=scope_v) for i in range(2)]
        psf = cx.psum("psf", [16, T], F32, stack=scope_v)
        ldv = s.dma("gpsimd", wv_sb[:, :].rearrange("p (kc m) -> p kc m", kc=16),
                    wv_nat.rearrange("(kc p) m -> p kc m", p=128),
                    waits=[apps], tag="ldwv")
        v_ev = []
        for tc in range(8):
            i = tc % 2
            last = None
            for col, h0, hw in pieces_cols(PIECES_T):
                for kc in range(16):
                    wl = ([ldv, apps[-1], v_ev[tc - 2] if tc >= 2 else None]
                          if (kc == 0 and col == 0) else [])
                    last = s.op("tensor", lambda t, i=i, kc=kc, tc=tc, col=col,
                                h0=h0, hw=hw: t.matmul(
                                    psv[i][:, col:col + hw],
                                    xb[:, kc * T + tc * 128:kc * T + tc * 128 + 128],
                                    wv_sb[:, kc * 1024 + h0:kc * 1024 + h0 + hw],
                                    start=(kc == 0), stop=(kc == 15)),
                                waits=wl, tag=f"vmm{tc}")
            ev = s.op("vector", lambda v, i=i, tc=tc: v.tensor_copy(
                v_tm[:, tc * T:(tc + 1) * T], psv[i][:, :]), waits=[last],
                tag=f"vev{tc}")
            v_ev.append(ev)
        lastf = None
        for col, h0, hw in pieces_cols(PIECES_T):
            for kc in range(16):
                lastf = s.op("tensor", lambda t, kc=kc, col=col, h0=h0, hw=hw:
                             t.matmul(psf[0:16, col:col + hw],
                                      wgk1_sb[:, kc * 16:(kc + 1) * 16],
                                      xb[:, kc * T + h0:kc * T + h0 + hw],
                                      start=(kc == 0), stop=(kc == 15)),
                             waits=[ld1, apps[kc], v_ev[-1] if (kc == 0 and col == 0) else None],
                             tag="fmm")
        evf = s.op("vector", lambda v: v.tensor_copy(fm1[:, :], psf[0:16, :]),
                   waits=[lastf], tag="fev")
        scope_v.close()

        psqk = cx.psum("psqk", [128, 1024], F32)
        psA = cx.psum("psA", [128, 128], F32)
        psO = cx.psum("psO", [128, 256], F32)
        psOb = cx.psum("psOb", [128, 128], BF16)
        psT = cx.psum("psT", [128, 1024], F32)
        qk_last_ev = [evf]
        P = {}

        st_qe = [None] * 4
        o_cp_last = None
        final_ops = []
        for h in range(4):
            mq_done = []
            for mq in range(4):
                gq = 4 * h + mq
                lastg = None
                for col, h0, hw in pieces_cols(PIECES_T):
                    lastg = s.op("tensor", lambda t, gq=gq, col=col, h0=h0,
                                 hw=hw: t.matmul(
                                     psqk[:, col:col + hw],
                                     wgk2_sb[0:16, gq * 128:(gq + 1) * 128],
                                     fm1[:, h0:h0 + hw], start=True, stop=True),
                                 waits=[ld2, evf, qk_last_ev], tag=f"gkmm{gq}")
                e0 = s.op("vector", lambda v, gq=gq: v.tensor_scalar_add(
                    gkp[:, :], psqk[:, :], smv["bgk2"][:, gq:gq + 1]),
                    waits=[lastg, small_ld, P.get("gkp")], tag=f"gk0{gq}")
                qk_last_ev = [e0]
                e1 = s.op("vector", lambda v: v.tensor_scalar_min(
                    m0t[:, :], gkp[:, :], 0.0), waits=[e0, P.get("m0")],
                    tag=f"gk1{gq}")
                e2 = s.op("scalar", lambda sc: sc.activation(
                    sA[:, :], gkp[:, :], AF.Abs), waits=[e0, P.get("sA")],
                    tag=f"gk2{gq}")
                e3 = s.op("scalar", lambda sc: sc.activation(
                    sB[:, :], sA[:, :], AF.Exp, scale=-1.0),
                    waits=[e2, P.get("sB")], tag=f"gk3{gq}")
                e4 = s.op("scalar", lambda sc: sc.activation(
                    sA[:, :], sB[:, :], AF.Ln, bias=onec[:, 0:1]),
                    waits=[e3], tag=f"gk4{gq}")
                e5 = s.op("vector", lambda v: v.scalar_tensor_tensor(
                    m0t[:, :], sA[:, :], -1.0, m0t[:, :], ALU.mult, ALU.add),
                    waits=[e4, e1], tag=f"gk5{gq}")
                P["sA"] = e5
                P["sB"] = e4
                P["m0"] = e5
                scn = s.op("vector", lambda v: v.tensor_tensor_scan(
                    b_t[:, :], c["mask128"][:, :], m0t[:, :], 0.0,
                    ALU.mult, ALU.add), waits=[e5, P.get("b")], tag=f"gk6{gq}")
                P["gkp"] = [e1, e2]
                x1 = s.op("vector", lambda v, mq=mq: v.tensor_copy(
                    zs[:, mq * 8:(mq + 1) * 8].rearrange("p (c o) -> p c o", o=1),
                    b_t[:, :].rearrange("p (c t) -> p c t", c=8)[:, :, 64:65]),
                    waits=[scn, P.get("zsg")], tag=f"zs{gq}")
                x2 = s.op("vector", lambda v, mq=mq: v.tensor_copy(
                    bl[:, mq * 8:(mq + 1) * 8].rearrange("p (c o) -> p c o", o=1),
                    b_t[:, :].rearrange("p (c t) -> p c t", c=8)[:, :, 127:128]),
                    waits=[scn, P.get("blg")], tag=f"bl{gq}")
                x3 = s.op("vector", lambda v, mq=mq: v.tensor_scalar(
                    ebias[:, :], zs[:, mq * 8:(mq + 1) * 8], -1.0 / 16.0,
                    LNSCALE_DK, ALU.mult, ALU.add), waits=[x1, P.get("ebias")],
                    tag=f"ebias{gq}")
                ebo = None
                for ch in range(8):
                    ebo = s.op("scalar", lambda sc, ch=ch: sc.activation(
                        ebt[:, ch * 128:(ch + 1) * 128],
                        b_t[:, ch * 128:(ch + 1) * 128], AF.Exp,
                        scale=1.0 / 16.0, bias=ebias[:, ch:ch + 1]),
                        waits=[x3, scn, P.get("ebt")], tag=f"eb{gq}")
                x4 = s.op("vector", lambda v, mq=mq: v.tensor_scalar_mul(
                    ebias[:, :], zs[:, mq * 8:(mq + 1) * 8], 1.0 / 16.0),
                    waits=[ebo], tag=f"embias{gq}")
                emo = None
                for ch in range(8):
                    emo = s.op("scalar", lambda sc, ch=ch: sc.activation(
                        embt[:, ch * 128:(ch + 1) * 128],
                        b_t[:, ch * 128:(ch + 1) * 128], AF.Exp,
                        scale=-1.0 / 16.0, bias=ebias[:, ch:ch + 1]),
                        waits=[x4, P.get("embt")], tag=f"em{gq}")
                P["b"] = [emo, x1, x2]
                P["ebias"] = emo
                i2 = mq % 2
                ldq = s.dma("gpsimd",
                            slq[i2][:, :].rearrange("p (kc m) -> p kc m", kc=16),
                            wq_t[gq].rearrange("(kc p) m -> p kc m", p=128),
                            waits=[P.get(("sl", i2))], tag=f"ldq{gq}")
                lastq = None
                for col, h0, hw in pieces_cols(PIECES_T):
                    for kc in range(16):
                        wl = ([ldq, apps[-1], qk_last_ev]
                              if (kc == 0 and col == 0) else [])
                        lastq = s.op("tensor", lambda t, i2=i2, kc=kc, col=col,
                                     h0=h0, hw=hw: t.matmul(
                                         psqk[:, col:col + hw],
                                         slq[i2][:, kc * 128:(kc + 1) * 128],
                                         xb[:, kc * T + h0:kc * T + h0 + hw],
                                         start=(kc == 0), stop=(kc == 15)),
                                     waits=wl, tag=f"qmm{gq}")
                qev = s.op("vector", lambda v, mq=mq: v.tensor_mul(
                    qe_h[:, mq * T:(mq + 1) * T], psqk[:, :], ebt[:, :]),
                    waits=[lastq, ebo, st_qe[h - 1] if h > 0 else None,
                           P.get(("qe", mq))], tag=f"qev{gq}")
                qk_last_ev = [qev]
                ldk = s.dma("gpsimd",
                            slq[i2][:, :].rearrange("p (kc m) -> p kc m", kc=16),
                            wk_t[gq].rearrange("(kc p) m -> p kc m", p=128),
                            waits=[lastq], tag=f"ldk{gq}")
                lastk = None
                for col, h0, hw in pieces_cols(PIECES_T):
                    for kc in range(16):
                        wl = ([ldk, qk_last_ev]
                              if (kc == 0 and col == 0) else [])
                        lastk = s.op("tensor", lambda t, i2=i2, kc=kc, col=col,
                                     h0=h0, hw=hw: t.matmul(
                                         psqk[:, col:col + hw],
                                         slq[i2][:, kc * 128:(kc + 1) * 128],
                                         xb[:, kc * T + h0:kc * T + h0 + hw],
                                         start=(kc == 0), stop=(kc == 15)),
                                     waits=wl, tag=f"kmm{gq}")
                kev = s.op("vector", lambda v, mq=mq: v.tensor_mul(
                    kd_h[:, mq * T:(mq + 1) * T], psqk[:, :], embt[:, :]),
                    waits=[lastk, emo, P.get(("kd", mq))], tag=f"kev{gq}")
                qk_last_ev = [kev]
                P[("sl", i2)] = lastk
                P["ebt"] = qev
                P["embt"] = kev
                mq_done.append([qev, kev])
            w2 = s.op("vector", lambda v: v.tensor_sub(
                wexp[:, :], bl[:, :], zs[:, :]), waits=[mq_done, P.get("wexpg")],
                tag=f"wx1{h}")
            w0 = s.op("vector", lambda v: v.tensor_copy(
                zsh[:, 0:31], zs[:, 1:32]), waits=[mq_done, P.get("zshg")],
                tag=f"zsh{h}")
            w1 = s.op("vector", lambda v: v.memset(
                zsh[:, :].rearrange("p (m c) -> p m c", m=4)[:, :, 7:8], 0.0),
                waits=[w0], tag=f"zshz{h}")
            w3 = s.op("vector", lambda v: v.tensor_add(
                wexp[:, :], wexp[:, :], zsh[:, :]), waits=[w2, w1], tag=f"wx2{h}")
            w4 = s.op("scalar", lambda sc: sc.activation(
                wexp[:, :], wexp[:, :], AF.Exp, scale=1.0 / 16.0),
                waits=[w3], tag=f"wx3{h}")
            P["zshg"] = w3
            tr_cp = []
            prev_cp = P.get("kdtm")
            for ccc in range(8):
                for mq in range(4):
                    tr = s.op("tensor", lambda t, ccc=ccc, mq=mq: t.transpose(
                        psOb[:, :],
                        kd_h[:, mq * T + ccc * 128:mq * T + ccc * 128 + 128],
                        c["ident"][:, :]),
                        waits=[mq_done[mq], prev_cp], tag=f"ktr{h}")
                    cp = s.op("vector", lambda v, ccc=ccc, mq=mq: v.tensor_copy(
                        kdtm[:, ccc * 512 + mq * 128:ccc * 512 + mq * 128 + 128],
                        psOb[:, :]), waits=[tr], tag=f"ktc{h}")
                    prev_cp = cp
                    tr_cp.append(cp)
            z1 = s.op("vector", lambda v, h=h: v.memset(
                spack[:, h * 1024:(h + 1) * 1024], 0.0), waits=[], tag=f"sz{h}")
            z2 = s.op("vector", lambda v: v.memset(sb_S[:, :], 0.0),
                      waits=[P.get("sbS")], tag=f"sbz{h}")
            prev_asb = P.get("asbg")
            prev_ocp = None
            prev_upd = [z1] * 4
            prev_cast = [z2] * 4
            for ccc in range(8):
                lastA = None
                for mq in range(4):
                    lastA = s.op("tensor", lambda t, ccc=ccc, mq=mq: t.matmul(
                        psA[:, :],
                        kd_h[:, mq * T + ccc * 128:mq * T + ccc * 128 + 128],
                        qe_h[:, mq * T + ccc * 128:mq * T + ccc * 128 + 128],
                        start=(mq == 0), stop=(mq == 3)),
                        waits=[mq_done, prev_asb], tag=f"Amm{h}{ccc}")
                aev = s.op("vector", lambda v: v.tensor_mul(
                    asb[:, :], psA[:, :], tril[:, :]), waits=[lastA, trl],
                    tag=f"Aev{h}{ccc}")
                prev_asb = aev
                o_mms = [s.op("tensor", lambda t, ccc=ccc, h=h: t.matmul(
                    psO[:, :], asb[:, :],
                    v_tm[:, ccc * T + h * 256:ccc * T + h * 256 + 256],
                    start=True, stop=(ccc == 0)),
                    waits=[aev, v_ev[ccc], prev_ocp], tag=f"Omm{h}{ccc}")]
                if ccc > 0:
                    for mq in range(4):
                        o_mms.append(s.op("tensor", lambda t, ccc=ccc, mq=mq:
                                          t.matmul(
                                              psO[:, :],
                                              qe_h[:, mq * T + ccc * 128:mq * T + ccc * 128 + 128],
                                              sb_S[:, mq * 256:(mq + 1) * 256],
                                              start=False, stop=(mq == 3)),
                                          waits=[prev_cast[mq]],
                                          tag=f"O2mm{h}{ccc}"))
                ocp = s.op("vector", lambda v, ccc=ccc, h=h: v.tensor_copy(
                    o_sb[:, ccc * T + h * 256:ccc * T + h * 256 + 256],
                    psO[:, :]), waits=o_mms, tag=f"Ocp{h}{ccc}")
                prev_ocp = ocp
                o_cp_last = ocp
                t_mms = []
                for mq in range(4):
                    t_mms.append(s.op("tensor", lambda t, ccc=ccc, mq=mq, h=h:
                                      t.matmul(
                                          psT[:, mq * 256:(mq + 1) * 256],
                                          kdtm[:, ccc * 512 + mq * 128:ccc * 512 + mq * 128 + 128],
                                          v_tm[:, ccc * T + h * 256:ccc * T + h * 256 + 256],
                                          start=True, stop=True),
                                      waits=[tr_cp[ccc * 4 + mq], v_ev[ccc],
                                             prev_upd[mq]], tag=f"Tmm{h}{ccc}"))
                for mq in range(4):
                    sl_ = spack[:, h * 1024 + mq * 256:h * 1024 + (mq + 1) * 256]
                    u1 = s.op("vector", lambda v, sl_=sl_, mq=mq: v.tensor_add(
                        sl_, sl_, psT[:, mq * 256:(mq + 1) * 256]),
                        waits=[t_mms[mq], prev_upd[mq]], tag=f"su{h}{ccc}")
                    u2 = s.op("vector", lambda v, sl_=sl_, mq=mq, ccc=ccc:
                              v.tensor_scalar_mul(
                                  sl_, sl_, wexp[:, mq * 8 + ccc:mq * 8 + ccc + 1]),
                              waits=[u1, w4], tag=f"sm{h}{ccc}")
                    prev_upd[mq] = u2
                    if ccc < 7:
                        u3 = s.op("vector", lambda v, sl_=sl_, mq=mq:
                                  v.tensor_copy(sb_S[:, mq * 256:(mq + 1) * 256],
                                                sl_),
                                  waits=[u2, o_mms], tag=f"scast{h}{ccc}")
                        prev_cast[mq] = u3
            st_qe[h] = s.dma("sync", o_qe[:, h * 4096:(h + 1) * 4096],
                             qe_h[:, :], waits=[prev_asb, prev_ocp], tag=f"stqe{h}")
            P["kdtm"] = prev_upd
            for mq in range(4):
                P[("kd", mq)] = [prev_asb, tr_cp[-1]]
                P[("qe", mq)] = [prev_asb, prev_ocp]
            P["sbS"] = prev_cast
            P["asbg"] = prev_asb
            P["wexpg"] = prev_upd
            fo = []
            for mq in range(4):
                fo.append(s.op("vector", lambda v, mq=mq: v.reduce_sum(
                    misc[:, mq:mq + 1], bl[:, mq * 8:(mq + 1) * 8], AX.X),
                    waits=[mq_done, P.get("misc")], tag=f"bls{h}{mq}"))
            a1 = s.op("scalar", lambda sc, h=h: sc.activation(
                a_sb[:, h * 4:(h + 1) * 4], misc[:, 0:4], AF.Exp,
                scale=1.0 / 16.0), waits=fo, tag=f"aex{h}")
            P["misc"] = a1
            c1_ = s.op("vector", lambda v: v.tensor_tensor_scan(
                misc[:, 32:64], mask8[:, :], bl[:, 0:32], 0.0,
                ALU.mult, ALU.add), waits=[mq_done, P.get("misc2"), m8l],
                tag=f"ccs{h}")
            c2_ = s.op("vector", lambda v: v.tensor_sub(
                misc[:, 32:64], misc[:, 32:64], bl[:, 0:32]), waits=[c1_],
                tag=f"ccsub{h}")
            c3_ = s.op("vector", lambda v: v.tensor_add(
                misc[:, 32:64], misc[:, 32:64], zs[:, 0:32]), waits=[c2_],
                tag=f"ccadd{h}")
            c4_ = s.op("scalar", lambda sc, h=h: sc.activation(
                cc_sb[:, h * 32:(h + 1) * 32], misc[:, 32:64], AF.Exp,
                scale=1.0 / 16.0), waits=[c3_], tag=f"ccex{h}")
            P["misc2"] = c4_
            P["zsg"] = [w4, c4_]
            P["blg"] = [w4, a1, c4_]
            final_ops += [a1, c4_]
        st1 = s.dma("sync", o_o, o_sb[:, :], waits=[o_cp_last], tag="sto")
        st2 = s.dma("sync", o_sp, spack[:, :], waits=[P["kdtm"]], tag="stsp")
        st3 = s.dma("sync", o_a, a_sb[:, :], waits=final_ops, tag="sta")
        st4 = s.dma("sync", o_cc, cc_sb[:, :], waits=final_ops, tag="stcc")
        s.op("sync", lambda e: e.nop(),
             waits=[st1, st2, st3, st4, st_xb] + st_qe, tag="fin")
        s.emit(cx.stack)
    return cx.nc



def build_launch_e():
    cx = Ctx()
    s = cx.s
    streami = cx.inp("streami", [128, 16 * T])
    o_in = cx.inp("o_in", [128, 8 * T], BF16)
    qe_in = cx.inp("qe_in", [128, 16 * T], BF16)
    cc_in = cx.inp("cc_in", [128, 128])
    sp_all = cx.inp("sp_all", [512, 4096])
    a_all = cx.inp("a_all", [512, 16])
    selm = cx.inp("selm", [128, 4])
    xbi = cx.inp("xbi", [128, 16 * T], BF16)
    wg_nat = cx.inp("wg_nat", [D, 1024])
    wnorm = cx.inp("wnorm", [1, 1024])
    wo_t = cx.inp("wo_t", [16, 1024, 128])
    o_stream = cx.out("stream3", [128, 16 * T])

    with cx.stack:
        c = std_consts(cx, ident=True)
        o_sb = cx.sb("o_sbe", [128, 8 * T], BF16)
        sbin = cx.sb("sbin", [128, 4096], BF16)
        cc_sb = cx.sb("cc_sbe", [128, 128], F32)
        sel = cx.sb("sele", [128, 4], F32)
        rstd = cx.sb("rstde", [128, 32], F32)
        racc = cx.sb("racce", [128, 32], F32)
        wnb = cx.sb("wnbe", [128, T], BF16)
        wrow = cx.sb("wrowe", [1, T], BF16)

        ld_o = s.dma("sync", o_sb[:, :], o_in, tag="ldo")
        ld_cc = s.dma("sync", cc_sb[:, :], cc_in, tag="ldcc")
        ld_sel = s.dma("sync", sel[:, :], selm, tag="ldsel")
        ld_wr = s.dma("gpsimd", wrow[:, :], wnorm, tag="ldwr")

        sc1 = ExitStack()
        T_acc = cx.sb("T_acc", [128, 4096], F32, stack=sc1)
        S_in = cx.sb("S_in", [128, 4096], F32, stack=sc1)
        rotS = [cx.sb(f"rotS{i}", [128, 4096], F32, stack=sc1) for i in range(2)]
        a_sb = cx.sb("a_sbe", [128, 64], F32, stack=sc1)
        ld_a = s.dma("sync", a_sb[:, :].rearrange("p (g m) -> p g m", g=4),
                     a_all.rearrange("(g p) m -> p g m", p=128), tag="lda",
                     stack=None) if False else s.dma(
                         "sync", a_sb[:, :].rearrange("p (g m) -> p g m", g=4),
                         a_all.rearrange("(g p) m -> p g m", p=128), tag="lda")
        ld_S = []
        for j in range(4):
            ld_S.append(s.dma("sync", rotS[j % 2][:, :],
                              sp_all[j * 128:(j + 1) * 128, :],
                              waits=[None], tag=f"ldS{j}"))
        cp0 = s.op("vector", lambda v: v.tensor_copy(T_acc[:, :], rotS[0][:, :]),
                   waits=[ld_S[0]], tag="cp0")
        si0 = s.op("vector", lambda v: v.tensor_scalar_mul(
            S_in[:, :], T_acc[:, :], sel[:, 0:1]), waits=[cp0, ld_sel], tag="si0")
        ld_S[2].waits.append(cp0)
        prevsi = si0
        prevT = cp0
        for j in range(1, 4):
            tops = []
            for hm in range(16):
                tops.append(s.op("vector", lambda v, j=j, hm=hm:
                                 v.scalar_tensor_tensor(
                                     T_acc[:, hm * 256:(hm + 1) * 256],
                                     T_acc[:, hm * 256:(hm + 1) * 256],
                                     a_sb[:, j * 16 + hm:j * 16 + hm + 1],
                                     rotS[j % 2][:, hm * 256:(hm + 1) * 256],
                                     ALU.mult, ALU.add),
                                 waits=[ld_S[j], ld_a, prevT, prevsi],
                                 tag=f"tj{j}"))
            if j + 2 < 4:
                ld_S[j + 2].waits.extend(tops)
            prevT = tops
            prevsi = s.op("vector", lambda v, j=j: v.scalar_tensor_tensor(
                S_in[:, :], T_acc[:, :], sel[:, j:j + 1], S_in[:, :],
                ALU.mult, ALU.add), waits=[tops, prevsi], tag=f"si{j}")
        cbin = s.op("vector", lambda v: v.tensor_copy(sbin[:, :], S_in[:, :]),
                    waits=[prevsi], tag="cbin")
        sc1.close()

        sc2 = ExitStack()
        qrot = [cx.sb(f"qrot{i}", [128, 4096], BF16, stack=sc2) for i in range(2)]
        qs = cx.sb("qse", [128, 128], BF16, stack=sc2)
        psC = cx.psum("psC", [128, 256], F32, stack=sc2)
        prev_oadd = None
        prev_qs = None
        corr_ops = []
        h_last = {}
        for h in range(4):
            ldq_h = s.dma("sync", qrot[h % 2][:, :],
                          qe_in[:, h * 4096:(h + 1) * 4096],
                          waits=[h_last.get(h % 2), cbin if h < 2 else None],
                          tag=f"ldq{h}")
            for ccc in range(8):
                cmms = []
                for mq in range(4):
                    qso = s.op("vector", lambda v, h=h, ccc=ccc, mq=mq:
                               v.tensor_scalar_mul(
                                   qs[:, :],
                                   qrot[h % 2][:, mq * T + ccc * 128:mq * T + ccc * 128 + 128],
                                   cc_sb[:, h * 32 + mq * 8 + ccc:h * 32 + mq * 8 + ccc + 1]),
                               waits=[ldq_h, ld_cc, prev_qs], tag=f"qs{h}{ccc}")
                    cmms.append(s.op("tensor", lambda t, h=h, mq=mq: t.matmul(
                        psC[:, :], qs[:, :],
                        sbin[:, (h * 4 + mq) * 256:(h * 4 + mq + 1) * 256],
                        start=(mq == 0), stop=(mq == 3)),
                        waits=[qso, cbin, prev_oadd], tag=f"cmm{h}{ccc}"))
                    prev_qs = cmms[-1]
                oadd = s.op("vector", lambda v, h=h, ccc=ccc: v.tensor_add(
                    o_sb[:, ccc * T + h * 256:ccc * T + h * 256 + 256],
                    o_sb[:, ccc * T + h * 256:ccc * T + h * 256 + 256],
                    psC[:, :]), waits=[cmms, ld_o], tag=f"oadd{h}{ccc}")
                prev_oadd = oadd
                corr_ops.append(oadd)
            h_last[h % 2] = prev_qs
        sc2.close()

        dump = cx.sb("dumpe", [128, 256], F32)
        racc_ops = []
        prev_d = None
        for tc in range(8):
            for h in range(4):
                prev_d = s.op("scalar", lambda sc, tc=tc, h=h: sc.activation(
                    dump[:, :], o_sb[:, tc * T + h * 256:tc * T + h * 256 + 256],
                    AF.Square, accum_out=racc[:, tc * 4 + h:tc * 4 + h + 1]),
                    waits=[corr_ops, prev_d], tag=f"sq{tc}{h}")
                racc_ops.append(prev_d)
        r1 = s.op("scalar", lambda sc: sc.activation(
            rstd[:, :], racc[:, :], AF.Sqrt, scale=1.0 / 256.0,
            bias=c["eps_col"][:, 0:1]), waits=racc_ops, tag="rsq")
        r2 = s.op("vector", lambda v: v.reciprocal(rstd[:, :], rstd[:, :]),
                  waits=[r1], tag="rrec")
        psW = cx.psum("psW", [128, 512], F32)
        wb_ops = []
        prev = None
        for h0 in range(0, T, 512):
            mm = s.op("tensor", lambda t, h0=h0: t.matmul(
                psW[:, :], c["ones_row_b"][0:1, :], wrow[0:1, h0:h0 + 512],
                start=True, stop=True),
                waits=[ld_wr, prev, c["ld"], corr_ops[-1]], tag="wbc")
            prev = s.op("vector", lambda v, h0=h0: v.tensor_copy(
                wnb[:, h0:h0 + 512], psW[:, :]), waits=[mm], tag="wbcc")
            wb_ops.append(prev)

        sc4 = ExitStack()
        og_fm = cx.sb("og_fm", [128, 8 * T], BF16)
        xb = cx.sb("xbe", [128, 16 * T], BF16, stack=sc4)
        wg_sb = cx.sb("wg_sbe", [128, 16 * 1024], BF16, stack=sc4)
        sg_t = cx.sb("sg_te", [128, T], BF16, stack=sc4)
        psg = [cx.psum(f"psge{i}", [128, 1024], F32, stack=sc4) for i in range(2)]
        ld_xb = s.dma("sync", xb[:, :], xbi, waits=[corr_ops[-1]], tag="ldxb")
        ld_wg = s.dma("gpsimd", wg_sb[:, :].rearrange("p (kc m) -> p kc m", kc=16),
                      wg_nat.rearrange("(kc p) m -> p kc m", p=128),
                      waits=[corr_ops[-1]], tag="ldwg")
        og_ops = []
        prev_sg = None
        for tc in range(8):
            i = tc % 2
            last = None
            for col, h0, hw in pieces_cols(PIECES_T):
                for kc in range(16):
                    wl = [ld_xb, ld_wg]
                    if kc == 0 and col == 0:
                        wl.append(og_ops[-8:] if tc >= 2 else None)
                    last = s.op("tensor", lambda t, i=i, kc=kc, tc=tc, col=col,
                                h0=h0, hw=hw: t.matmul(
                                    psg[i][:, col:col + hw],
                                    xb[:, kc * T + tc * 128:kc * T + tc * 128 + 128],
                                    wg_sb[:, kc * 1024 + h0:kc * 1024 + h0 + hw],
                                    start=(kc == 0), stop=(kc == 15)),
                                waits=wl, tag=f"gmm{tc}")
            sg = s.op("scalar", lambda sc, i=i: sc.activation(
                sg_t[:, :], psg[i][:, :], AF.Silu), waits=[last, prev_sg],
                tag=f"gsil{tc}")
            gt = s.op("vector", lambda v: v.tensor_mul(
                sg_t[:, :], sg_t[:, :], wnb[:, :]), waits=[sg, wb_ops],
                tag=f"gwn{tc}")
            tc_ops = []
            for h in range(4):
                tc_ops.append(s.op("vector", lambda v, tc=tc, h=h:
                                   v.scalar_tensor_tensor(
                                       o_sb[:, tc * T + h * 256:tc * T + h * 256 + 256],
                                       o_sb[:, tc * T + h * 256:tc * T + h * 256 + 256],
                                       rstd[:, tc * 4 + h:tc * 4 + h + 1],
                                       sg_t[:, h * 256:(h + 1) * 256],
                                       ALU.mult, ALU.mult),
                                   waits=[gt, r2, corr_ops, racc_ops],
                                   tag=f"og{tc}{h}"))
            og_ops.extend(tc_ops)
            prev_sg = tc_ops
        psTr = cx.psum("psTre", [128, 128], BF16, stack=sc4)
        prev_cp = None
        tr_cps = {}
        for tc in range(8):
            for vc in range(8):
                tr = s.op("tensor", lambda t, tc=tc, vc=vc: t.transpose(
                    psTr[:, :], o_sb[:, tc * T + vc * 128:tc * T + vc * 128 + 128],
                    c["ident"][:, :]), waits=[og_ops, prev_cp], tag=f"ogtr{tc}")
                cp = s.op("vector", lambda v, tc=tc, vc=vc: v.tensor_copy(
                    og_fm[:, vc * T + tc * 128:vc * T + tc * 128 + 128],
                    psTr[:, :]), waits=[tr], tag=f"ogtc{tc}")
                prev_cp = cp
                tr_cps[vc] = cp
        sc4.close()

        xsb = cx.sb("xsbe", [128, 16 * T], F32)
        ld_s = s.dma("sync", xsb[:, :], streami, waits=[prev_cp, og_ops],
                     tag="ldS")
        pst = [cx.psum(f"ppe{i}", [128, 1024], F32) for i in range(2)]
        st_ops = []

        def evac_wo(m, ps, pieces, waits):
            ops = []
            for col, h0, hw in pieces_cols(pieces):
                ops.append(s.op("vector", lambda v, m=m, ps=ps, col=col, h0=h0,
                                hw=hw: v.tensor_add(
                                    xsb[:, m * T + h0:m * T + h0 + hw],
                                    xsb[:, m * T + h0:m * T + h0 + hw],
                                    ps[:, col:col + hw]),
                                waits=waits + [ld_s], tag=f"evwo{m}"))
            st_ops.append(s.dma("sync", o_stream[:, m * T:(m + 1) * T],
                                xsb[:, m * T:(m + 1) * T], waits=ops,
                                tag=f"stwo{m}"))
            return ops

        stream_matmul(cx, "wo", wo_t, 16, 1024,
                      lambda kc, h0, hw: og_fm[:, kc * T + h0:kc * T + h0 + hw],
                      lambda kc: tr_cps[kc], evac_wo, PIECES_T, pst,
                      first_waits=og_ops)
        s.op("sync", lambda e: e.nop(), waits=st_ops, tag="fin")
        s.emit(cx.stack)
    return cx.nc




def build_launch_f():
    cx = Ctx()
    s = cx.s
    streami = cx.inp("streami", [128, 16 * T])
    wg0_t = cx.inp("wg0_t", [64, D, 128])
    wg1_t = cx.inp("wg1_t", [64, D, 128])
    wl_t = cx.inp("wl_t", [16, I1, 128])
    o_out = cx.out("out", [128, 16 * T])

    with cx.stack:
        c = std_consts(cx)
        xsb = cx.sb("xsbf", [128, 16 * T], F32)
        qx = cx.sb("qxf", [128, 16 * T], FP8)
        qa = cx.sb("qaf", [128, 64 * T], FP8)
        gb = cx.sb("gbf", [128, T], BF16)
        gelb = cx.sb("gelbf", [128, T], BF16)
        ub = cx.sb("ubf", [128, T], BF16)
        actb = cx.sb("actbf", [128, T], BF16)
        dscr = cx.sb("dscrf", [128, T], BF16)

        ld_s = s.dma("sync", xsb[:, :], streami, tag="ldS")
        q_ops = ln_full32(cx, c, xsb, qx, T, [ld_s, c["ld"]], PIECES_T, tagp="f")

        slg0 = [cx.sb(f"slg0{i}", [128, D], FP8) for i in range(2)]
        slg1 = [cx.sb(f"slg1{i}", [128, D], FP8) for i in range(2)]
        scg = ExitStack()
        psG = [cx.psum(f"psGf{i}", [128, 1024], F32, stack=scg) for i in range(2)]
        psU = [cx.psum(f"psUf{i}", [128, 1024], F32, stack=scg) for i in range(2)]
        ev_hist = []
        mm_hist = []
        qa_ops = []
        P = {}
        for m in range(64):
            i = m % 2
            wprev = [mm_hist[m - 2]] if m >= 2 else [q_ops[-1]]
            ldg = s.dma("gpsimd",
                        slg0[i][:, :].rearrange("p (kc m) -> p kc m", kc=16),
                        wg0_t[m].rearrange("(kc p) m -> p kc m", p=128),
                        waits=wprev, tag=f"ldg0{m}")
            ldu = s.dma("gpsimd",
                        slg1[i][:, :].rearrange("p (kc m) -> p kc m", kc=16),
                        wg1_t[m].rearrange("(kc p) m -> p kc m", p=128),
                        waits=wprev, tag=f"ldg1{m}")
            evp = ev_hist[m - 2] if m >= 2 else []
            lastG = lastU = None
            for ps, sl, ldx_ in ((psG[i], slg0[i], ldg), (psU[i], slg1[i], ldu)):
                for col, h0, hw in pieces_cols(PIECES_T):
                    for kc in range(16):
                        wl = [ldx_, q_ops[kc]]
                        if kc == 0 and col == 0:
                            wl.append(evp)
                        last = s.op("tensor", lambda t, ps=ps, sl=sl, kc=kc,
                                    col=col, h0=h0, hw=hw: t.matmul(
                                        ps[:, col:col + hw],
                                        sl[:, kc * 128:(kc + 1) * 128],
                                        qx[:, kc * T + h0:kc * T + h0 + hw],
                                        start=(kc == 0), stop=(kc == 15)),
                                    waits=wl, tag=f"fmm{m}")
                if ps is psG[i]:
                    lastG = last
                else:
                    lastU = last
            mm_hist.append(lastU)
            e1 = s.op("vector", lambda v, i=i: v.tensor_copy(
                gb[:, :], psG[i][:, :]), waits=[lastG, P.get("gb")],
                tag=f"gcp{m}")
            e2 = s.op("scalar", lambda sc: sc.activation(
                gelb[:, :], gb[:, :], AF.Gelu_apprx_tanh),
                waits=[e1, P.get("gelb")], tag=f"gel{m}")
            e3 = s.op("scalar", lambda sc, i=i: sc.copy(
                ub[:, :], psU[i][:, :]), waits=[lastU, P.get("ub"), e2],
                tag=f"ucp{m}")
            e4 = s.op("vector", lambda v: v.tensor_mul(
                actb[:, :], gelb[:, :], ub[:, :]),
                waits=[e2, e3, P.get("actb")], tag=f"amul{m}")
            e5 = s.op("gpsimd", lambda g, m=m: g.tensor_copy(
                qa[:, m * T:(m + 1) * T], actb[:, :]), waits=[e4],
                tag=f"qa{m}")
            P["gb"] = e2
            P["gelb"] = e4
            P["ub"] = e4
            P["actb"] = e5
            ev_hist.append([e1, e3])
            qa_ops.append(e5)
        scg.close()

        pst = [cx.psum(f"ppf{i}", [128, 1024], F32) for i in range(2)]
        st_ops = []

        def evac_dn(m, ps, pieces, waits):
            ops = []
            for col, h0, hw in pieces_cols(pieces):
                o1 = s.op("vector", lambda v, ps=ps, col=col, hw=hw:
                          v.tensor_copy(dscr[:, col:col + hw],
                                        ps[:, col:col + hw]),
                          waits=waits + [P.get("dscr")], tag=f"dcp{m}")
                o2 = s.op("vector", lambda v, m=m, col=col, h0=h0, hw=hw:
                          v.tensor_add(
                              xsb[:, m * T + h0:m * T + h0 + hw],
                              xsb[:, m * T + h0:m * T + h0 + hw],
                              dscr[:, col:col + hw]),
                          waits=[o1, ld_s], tag=f"dadd{m}")
                ops.append(o2)
            P["dscr"] = ops
            st_ops.append(s.dma("sync", o_out[:, m * T:(m + 1) * T],
                                xsb[:, m * T:(m + 1) * T], waits=ops,
                                tag=f"stf{m}"))
            return ops

        stream_matmul(cx, "dnf", wl_t, 16, I1,
                      lambda kc, h0, hw: qa[:, kc * T + h0:kc * T + h0 + hw],
                      lambda kc: qa_ops[kc], evac_dn, PIECES_T, pst,
                      sb_dtype=FP8, first_waits=[qa_ops[1]],
                      psum_prev=[ev_hist[-2] + [qa_ops[-1]], ev_hist[-1]])
        s.op("sync", lambda e: e.nop(), waits=st_ops, tag="fin")
        s.emit(cx.stack)
    return cx.nc




def _phase_barrier(cx, tag):
    s = cx.s
    lasts = [s.ops[e][-1] for e in ENGINES if s.ops[e]]
    t = cx._fence_sb
    ops = []
    ops.append(s.op("vector", lambda e: e.memset(t[0:1, 8:12], 0.0),
                    waits=lasts, tag=f"barV{tag}"))
    ops.append(s.op("scalar", lambda e: e.memzero(t[0:1, 0:4]),
                    waits=lasts, tag=f"barS{tag}"))
    ops.append(s.op("gpsimd", lambda e: e.memset(t[0:1, 4:8], 0.0),
                    waits=lasts, tag=f"barG{tag}"))
    ops.append(s.op("tensor", lambda e: e.nop(), waits=lasts, tag=f"barT{tag}"))
    ops.append(s.op("sync", lambda e: e.nop(), waits=lasts, tag=f"bar0{tag}"))
    cx._last_fence = {}
    return ops


def ln_stream_stats(cx, c, stream, x_ready, scope, tagp):
    s = cx.s
    ntok = T
    brow = cx.sb(f"brow{tagp}", [1, 4 * ntok], F32, stack=scope)
    bc = cx.sb(f"bc{tagp}", [128, 2 * ntok], F32, stack=scope)
    sq = cx.sb(f"sq{tagp}", [128, ntok], BF16, stack=scope)
    scope2 = ExitStack()
    stat_a = cx.psum(f"statA{tagp}", [1, ntok], F32, stack=scope2)
    stat_b = cx.psum(f"statB{tagp}", [1, ntok], F32, stack=scope2)
    mm_sum = []
    mm_sq_last = None
    prev_sq = None
    for ci in range(16):
        xc = stream[:, ci * ntok:(ci + 1) * ntok]
        sop = s.op("scalar", lambda sc, b=xc: sc.activation(
            sq[:, :], b, AF.Square), waits=[x_ready, prev_sq],
            tag=f"lnsq{tagp}{ci}")
        for h0, hw in PIECES_T:
            mm_sum.append(s.op("tensor", lambda t, h0=h0, hw=hw, xc=xc, ci=ci:
                               t.matmul(stat_a[0:1, h0:h0 + hw],
                                        c["ones_col"][:, :], xc[:, h0:h0 + hw],
                                        start=(ci == 0), stop=(ci == 15)),
                               waits=[x_ready, c["ld"]], tag=f"lnA{tagp}"))
        for h0, hw in PIECES_T:
            prev_sq = s.op("tensor", lambda t, h0=h0, hw=hw, ci=ci: t.matmul(
                stat_b[0:1, h0:h0 + hw], c["ones_col"][:, :], sq[:, h0:h0 + hw],
                start=(ci == 0), stop=(ci == 15)), waits=[sop], tag=f"lnB{tagp}")
        mm_sq_last = prev_sq
    inv_d = 1.0 / D
    MEAN = brow[0:1, 0:ntok]
    RSTD = brow[0:1, ntok:2 * ntok]
    MB = brow[0:1, 2 * ntok:3 * ntok]
    SCR = brow[0:1, 3 * ntok:4 * ntok]
    o_mean = s.op("vector", lambda v: v.tensor_scalar_mul(
        MEAN, stat_a[0:1, :], inv_d), waits=[mm_sum[-1]], tag=f"mean{tagp}")
    o_m2 = s.op("scalar", lambda sc: sc.activation(SCR, MEAN, AF.Square),
                waits=[o_mean], tag=f"m2{tagp}")
    o_var = s.op("vector", lambda v: v.scalar_tensor_tensor(
        RSTD, stat_b[0:1, :], inv_d, SCR, ALU.mult, ALU.subtract),
        waits=[o_m2, mm_sq_last], tag=f"var{tagp}")
    scope2.close()
    bc_ps = cx.psum(f"bcps{tagp}", [128, ntok], F32, stack=scope)
    o_sqrt = s.op("scalar", lambda sc: sc.activation(
        RSTD, RSTD, AF.Sqrt, bias=c["eps_col"][0:1, :]), waits=[o_var],
        tag=f"sqrt{tagp}")
    o_rstd = s.op("vector", lambda v: v.reciprocal(RSTD, RSTD), waits=[o_sqrt],
                  tag=f"rstd{tagp}")
    o_mb = s.op("vector", lambda v: v.scalar_tensor_tensor(
        MB, MEAN, -1.0, RSTD, ALU.mult, ALU.mult), waits=[o_rstd], tag=f"mb{tagp}")
    bc1 = [s.op("tensor", lambda t, h0=h0, hw=hw: t.matmul(
        bc_ps[:, h0:h0 + hw], c["ones_row"][0:1, :],
        brow[0:1, ntok + h0:ntok + h0 + hw], start=True, stop=True),
        waits=[o_rstd, mm_sq_last], tag=f"bc1{tagp}") for h0, hw in PIECES_T]
    cp1 = s.op("vector", lambda v: v.tensor_copy(bc[:, 0:ntok], bc_ps[:, :]),
               waits=bc1, tag=f"bcc1{tagp}")
    bc2 = [s.op("tensor", lambda t, h0=h0, hw=hw: t.matmul(
        bc_ps[:, h0:h0 + hw], c["ones_row"][0:1, :],
        brow[0:1, 2 * ntok + h0:2 * ntok + h0 + hw], start=True, stop=True),
        waits=[o_mb, cp1], tag=f"bc2{tagp}") for h0, hw in PIECES_T]
    cp2 = s.op("vector", lambda v: v.tensor_copy(bc[:, ntok:], bc_ps[:, :]),
               waits=bc2, tag=f"bcc2{tagp}")
    return bc, [cp1, cp2]


def ln_apply_sb(cx, stream, bc, scr, out_ap, ci, waits, tagp):
    s = cx.s
    xc = stream[:, ci * T:(ci + 1) * T]
    o1 = s.op("vector", lambda v: v.tensor_mul(scr[:, :], xc, bc[:, 0:T]),
              waits=waits, tag=f"ap1{tagp}{ci}")
    o2 = s.op("vector", lambda v: v.tensor_add(out_ap, scr[:, :], bc[:, T:]),
              waits=[o1], tag=f"ap2{tagp}{ci}")
    return o2


def quant_weight_local(cx, c, wa_dram, wb_dram, rows_a, cols_a, qsb, ws_tm,
                       entry, tagp, r0=0, rw=None, shared=None):
    s = cx.s
    K = cols_a
    if rw is None:
        rw = rows_a
    nrt = rw // 128
    PW = min(1408, rw)
    sc = ExitStack()
    if shared is not None:
        rot, rotb = shared["rot"], shared["rotb"]
        wsrow, bcw = shared["wsrow"], shared["bcw"]
        pr, pb = shared["pr"], shared["pb"]
        carry = shared["carry"]
    else:
        rot = [cx.sb(f"qwra{tagp}{i}", [128, cols_a], F32, stack=sc)
               for i in range(2)]
        rotb = [cx.sb(f"qwrb{tagp}{i}", [128, PW], F32, stack=sc)
                for i in range(2)]
        wsrow = cx.sb(f"wsrow{tagp}", [1, rw], F32, stack=sc)
        bcw = cx.sb(f"bcw{tagp}", [128, PW], F32, stack=sc)
        pr = cx.psum(f"qwp{tagp}", [1, 512], F32, stack=sc)
        pb = cx.psum(f"qwb{tagp}", [128, 512], F32, stack=sc)
        carry = entry
    g_off = r0 // 128
    red_ops = []
    prev = {}
    for g in range(nrt):
        i = g % 2
        ld = s.dma("sync", rot[i][:, :],
                   wa_dram[r0 + g * 128:r0 + (g + 1) * 128, :],
                   waits=[prev.get(i), carry], tag=f"qwld{tagp}{g}")
        red = s.op("vector", lambda v, g=g, i=i: v.reduce_max(
            ws_tm[:, g_off + g:g_off + g + 1], rot[i][:, :], AX.X,
            apply_absolute_value=True), waits=[ld], tag=f"qwrd{tagp}{g}")
        prev[i] = red
        red_ops.append(red)
    prev_cp = None
    tposs = []
    for blk in range(0, nrt, 4):
        tp = []
        for j in range(min(4, nrt - blk)):
            tp.append(s.op("tensor", lambda t, blk=blk, j=j: t.transpose(
                pr[0:1, j * 128:(j + 1) * 128],
                ws_tm[:, g_off + blk + j:g_off + blk + j + 1],
                c["identf"][:, :]),
                waits=[red_ops[blk + j], prev_cp, c["ld"], carry],
                tag=f"qwt{tagp}"))
        prev_cp = s.op("vector", lambda v, blk=blk, n=min(4, nrt - blk):
                       v.tensor_copy(wsrow[0:1, blk * 128:(blk + n) * 128],
                                     pr[0:1, 0:n * 128]), waits=tp,
                       tag=f"qwtc{tagp}")
        tposs.append(prev_cp)
    rrec = s.op("vector", lambda v: v.reciprocal(wsrow[0:1, :], wsrow[0:1, :]),
                waits=tposs, tag=f"qwrec{tagp}")
    rmul = s.op("vector", lambda v: v.tensor_scalar_mul(
        wsrow[0:1, :], wsrow[0:1, :], 224.0), waits=[rrec], tag=f"qwm{tagp}")
    wss = s.op("vector", lambda v: v.tensor_scalar_mul(
        ws_tm[:, g_off:g_off + nrt], ws_tm[:, g_off:g_off + nrt], 1.0 / 224.0),
        waits=[rmul], tag=f"qws{tagp}")
    q_ops = [[] for _ in range(K // 128)]
    prevq = {}
    for p0 in range(0, rw, PW):
        pw = min(PW, rw - p0)
        bc_ops = []
        prevb = prevq.get("bc")
        for h0 in range(0, pw, 512):
            hw2 = min(512, pw - h0)
            mm = s.op("tensor", lambda t, p0=p0, h0=h0, hw2=hw2: t.matmul(
                pb[:, 0:hw2], c["ones_row"][0:1, :],
                wsrow[0:1, p0 + h0:p0 + h0 + hw2],
                start=True, stop=True), waits=[rmul, prevb], tag=f"qwbc{tagp}")
            prevb = s.op("vector", lambda v, h0=h0, hw2=hw2: v.tensor_copy(
                bcw[:, h0:h0 + hw2], pb[:, 0:hw2]),
                waits=[mm, prevq.get("q0"), prevq.get("q1")], tag=f"qwbcc{tagp}")
            bc_ops.append(prevb)
        for kc in range(K // 128):
            i = kc % 2
            ld = s.dma("sync", rotb[i][:, 0:pw],
                       wb_dram[kc * 128:(kc + 1) * 128, r0 + p0:r0 + p0 + pw],
                       waits=[prevq.get(i), carry], tag=f"qwbl{tagp}{kc}")
            qo = s.op("vector", lambda v, kc=kc, i=i, p0=p0, pw=pw: v.tensor_mul(
                qsb[:, kc * rw + p0:kc * rw + p0 + pw],
                rotb[i][:, 0:pw], bcw[:, 0:pw]),
                waits=[ld, bc_ops] + ([entry] if shared is not None else []),
                tag=f"qwq{tagp}{kc}")
            prevq[i] = qo
            prevq[f"q{i}"] = qo
            prevq["bc"] = qo
            q_ops[kc].append(qo)
    if shared is not None:
        shared["carry"] = prevq["bc"]
    sc.close()
    return q_ops, wss




def _fin_store(cx, stream, o_out, waits):
    s = cx.s
    stg = [cx.sb(f"fin{i}", [128, T], F32) for i in range(2)]
    sts = []
    prev = {}
    for m in range(16):
        i = m % 2
        cp = s.op("vector", lambda v, m=m, i=i: v.tensor_copy(
            stg[i][:, :], stream[:, m * T:(m + 1) * T]),
            waits=[waits, prev.get(i)], tag=f"fincp{m}")
        st = s.dma("sync", o_out[:, m * T:(m + 1) * T], stg[i][:, :],
                   waits=[cp], tag=f"finst{m}")
        prev[i] = st
        sts.append(st)
    s.op("sync", lambda e: e.nop(), waits=sts, tag="finfin")


def build_fused(upto=4):
    cx = Ctx()
    s = cx.s
    RG = [[0, 1, 2, 3], [4, 5, 6, 7]]
    x_in = cx.inp("x", [128, 16 * TW])
    wg1t = cx.inp("wg1t", [4, 128, D])
    wg2t = cx.inp("wg2t", [16, 128, 512])
    wut = cx.inp("wut", [16, 128, D])
    wvt = cx.inp("wvt", [16, 128, D])
    bg1 = cx.inp("bg1", [128, 4])
    bg2 = cx.inp("bg2", [128, 16])
    bu = cx.inp("bu", [128, 16])
    bv = cx.inp("bv", [128, 16])
    cw = cx.inp("cw", [128, 48])
    cb = cx.inp("cb", [128, 16])
    pmask = cx.inp("pmask", [128, 4])
    wgba = cx.inp("wgba", [I0, D])
    wgbt = cx.inp("wgbt", [D, I0])
    wuba = cx.inp("wuba", [I0, D])
    wubt = cx.inp("wubt", [D, I0])
    wdna = cx.inp("wdna", [D, I0])
    wdnt = cx.inp("wdnt", [I0, D])
    wq_t = cx.inp("wq_t", [16, 128, D])
    wk_t = cx.inp("wk_t", [16, 128, D])
    wv_nat = cx.inp("wv_nat", [128, 16 * 1024])
    wgk1p = cx.inp("wgk1p", [128, 256])
    wgk2 = cx.inp("wgk2", [16, D])
    bgk2 = cx.inp("bgk2", [128, 16])
    selm = cx.inp("selm", [128, 4])
    wg_nat = cx.inp("wg_nat", [128, 16 * 1024])
    wnorm = cx.inp("wnorm", [1, T])
    wo_t = cx.inp("wo_t", [16, 128, 1024])
    wg0_t = cx.inp("wg0_t", [64, 128, D])
    wg1f_t = cx.inp("wg1f_t", [64, 128, D])
    wl_t = cx.inp("wl_t", [16, 128, I1])
    o_out = cx.out("out", [128, 16 * T])
    tot_in = cx.nc.dram_tensor("tot_in", [128, 16], F32, kind="Internal").ap()
    tot_out = cx.nc.dram_tensor("tot_out", [512, 16], F32, kind="Internal").ap()
    sp_in = cx.nc.dram_tensor("sp_in", [128, 4112], F32, kind="Internal").ap()
    sp_out = cx.nc.dram_tensor("sp_out", [512, 4112], F32, kind="Internal").ap()
    tdram = cx.nc.dram_tensor("tdram", [128, 44 * T], BF16, kind="Internal").ap()
    qedram = cx.nc.dram_tensor("qedram", [128, 16 * T], BF16, kind="Internal").ap()

    with cx.stack:
        c = std_consts(cx, ident=True, mask128=True)
        tril, trl = cx.const_sb("trilc", np.triu(np.ones((128, 128), bf)))
        m8 = np.ones((128, 32), np.float32)
        m8[:, ::8] = 0.0
        mask8, m8l = cx.const_sb("mask8c", m8)
        onec, onl = cx.const_sb("onecol", np.ones((128, 1), np.float32))
        c["ld"] += [trl, m8l, onl]
        stream = cx.sb("stream", [128, 16 * T], BF16)

        p1 = ExitStack()
        cx.cur = p1
        xb1 = cx.sb("xb1", [128, 16 * TW], BF16)
        hb = cx.sb("hb", [128, 4 * T], BF16)
        gates = cx.sb("gat", [128, 16 * T], BF16)
        ug = cx.sb("ug1", [128, 16 * T], BF16)
        tot = cx.sb("tot_sb", [128, 16], F32)
        tots = cx.sb("tots", [128, 64], F32)
        pfx = cx.sb("pfx", [128, 16], F32)
        pm = cx.sb("pm", [128, 4], F32)
        vg = cx.sb("vg", [128, T], BF16)
        kc_t = cx.sb("kc_t", [128, T], BF16)
        zero_b = cx.sb("zerob", [128, T], BF16)
        t32 = cx.sb("t32", [128, T], F32)
        sm, small_ld = cx.smalls([("bg1", bg1), ("bg2", bg2), ("bu", bu),
                                  ("bv", bv), ("cw", cw), ("cb", cb)])
        slab_big = {"tiles": [cx.sb(f"slab_big{i}", [128, D], BF16)
                              for i in range(2)], "lastmm": [None, None]}
        slab_gt = {"tiles": [cx.sb(f"slab_gt{i}", [128, 512], BF16)
                             for i in range(2)], "lastmm": [None, None]}
        ld_pm = s.dma("sync", pm[:, :], pmask, tag="ldpm")
        ld_stream = s.dma("gpsimd",
                          stream[:, :].rearrange("p (ci t) -> p ci t", ci=16),
                          x_in[:, :].rearrange("p (ci t) -> p ci t", ci=16)[:, :, 1:1 + T],
                          tag="ldstream")
        z0 = s.op("vector", lambda v: v.memset(zero_b[:, :], 0.0), tag="z0")
        apps = ln_hbm(cx, c, x_in, xb1, TW, ((0, 512), (512, 512), (1024, 2)),
                      tagp="p1")
        conv_done = []
        prev = None
        for ci in range(16):
            xc = xb1[:, ci * TW:(ci + 1) * TW]
            stc = stream[:, ci * T:(ci + 1) * T]
            o1 = s.op("vector", lambda v, xc=xc, ci=ci: v.tensor_scalar_mul(
                t32[:, :], xc[:, 0:T], sm["cw"][:, 3 * ci:3 * ci + 1]),
                waits=[apps[ci], prev, small_ld], tag=f"cv1_{ci}")
            o2 = s.op("vector", lambda v, xc=xc, ci=ci: v.scalar_tensor_tensor(
                t32[:, :], xc[:, 1:1 + T], sm["cw"][:, 3 * ci + 1:3 * ci + 2],
                t32[:, :], ALU.mult, ALU.add), waits=[o1], tag=f"cv2_{ci}")
            o3 = s.op("vector", lambda v, xc=xc, ci=ci: v.scalar_tensor_tensor(
                t32[:, :], xc[:, 2:2 + T], sm["cw"][:, 3 * ci + 2:3 * ci + 3],
                t32[:, :], ALU.mult, ALU.add), waits=[o2], tag=f"cv3_{ci}")
            o4 = s.op("vector", lambda v, stc=stc, ci=ci: v.scalar_tensor_tensor(
                stc, t32[:, :], sm["cb"][:, ci:ci + 1], stc, ALU.add, ALU.add),
                waits=[o3, ld_stream], tag=f"cv4_{ci}")
            prev = o4
            conv_done.append(o4)

        pst = [cx.psum(f"pp{i}", [128, 1024], F32) for i in range(2)]

        def rhs_x1(kcc, h0, hw):
            return xb1[:, kcc * TW + 1 + h0: kcc * TW + 1 + h0 + hw]

        def evac_h(m, ps, pieces, waits):
            ops = []
            for col, h0, hw in pieces_cols(pieces):
                ops.append(s.op("scalar", lambda sc2, m=m, ps=ps, col=col,
                                h0=h0, hw=hw: sc2.activation(
                                    hb[:, m * T + h0:m * T + h0 + hw],
                                    ps[:, col:col + hw], AF.Relu,
                                    bias=sm["bg1"][:, m:m + 1]),
                                waits=waits + [small_ld], tag=f"evh{m}"))
            return ops

        ev_h = stream_matmul(cx, "h", wg1t, 4, D, rhs_x1,
                             lambda kcc: apps[kcc], evac_h, PIECES_T, pst,
                             slab_pair=slab_big)

        def evac_g(m, ps, pieces, waits):
            ops = []
            for col, h0, hw in pieces_cols(pieces):
                ops.append(s.op("scalar", lambda sc2, m=m, ps=ps, col=col,
                                h0=h0, hw=hw: sc2.activation(
                                    gates[:, m * T + h0:m * T + h0 + hw],
                                    ps[:, col:col + hw], AF.Sigmoid,
                                    bias=sm["bg2"][:, m:m + 1]),
                                waits=waits, tag=f"evg{m}"))
            return ops

        ev_g = stream_matmul(cx, "gt", wg2t, 16, 512,
                             lambda kcc, h0, hw: hb[:, kcc * T + h0:kcc * T + h0 + hw],
                             lambda kcc: ev_h[kcc], evac_g, PIECES_T, pst,
                             psum_prev=[ev_h[2], ev_h[3]],
                             slab_pair=slab_gt, rhs_cover=ev_h[3])

        ug_done = []

        def evac_u(m, ps, pieces, waits):
            ops = []
            for col, h0, hw in pieces_cols(pieces):
                ops.append(s.op("vector", lambda v, m=m, ps=ps, col=col, h0=h0,
                                hw=hw: v.scalar_tensor_tensor(
                                    ug[:, m * T + h0:m * T + h0 + hw],
                                    ps[:, col:col + hw], sm["bu"][:, m:m + 1],
                                    gates[:, m * T + h0:m * T + h0 + hw],
                                    ALU.add, ALU.mult),
                                waits=waits + [ev_g[m], small_ld],
                                tag=f"evu{m}"))
            red = s.op("vector", lambda v, m=m: v.reduce_sum(
                tot[:, m:m + 1], ug[:, m * T:(m + 1) * T], AX.X), waits=ops,
                tag=f"totr{m}")
            ug_done.append(red)
            return ops + [red]

        stream_matmul(cx, "u", wut, 16, D, rhs_x1, lambda kcc: apps[kcc],
                      evac_u, PIECES_T, pst, psum_prev=[ev_g[14], ev_g[15]],
                      slab_pair=slab_big, rhs_cover=apps[-1])

        st_tot = s.dma("sync", tot_in, tot[:, :], waits=ug_done, tag="sttot")
        if SIM1:
            cc1 = [s.dma("gpsimd", tot_out[j * 128:(j + 1) * 128, :], tot_in,
                         waits=[st_tot], tag=f"cc1sim{j}") for j in range(4)]
        else:
            cc1 = s.op("gpsimd", lambda g: g.collective_compute(
                "AllGather", ALU.bypass, replica_groups=RG, ins=[tot_in.opt()],
                outs=[tot_out.opt()]), waits=[st_tot], async_inc=1, tag="cc1")
        ld_t = s.dma("sync", tots[:, :].rearrange("p (g m) -> p g m", g=4),
                     tot_out.rearrange("(g p) m -> p g m", p=128),
                     waits=[cc1], tag="ldtot")
        o = s.op("vector", lambda v: v.memset(pfx[:, :], 0.0), tag="pfx0")
        for j in range(4):
            o = s.op("vector", lambda v, j=j, o=o: v.scalar_tensor_tensor(
                pfx[:, :], tots[:, j * 16:(j + 1) * 16], pm[:, j:j + 1],
                pfx[:, :], ALU.mult, ALU.add), waits=[ld_t, ld_pm, o],
                tag=f"pfx{j}")
        pfx_done = o

        go = gates
        go_ops = [None] * 16

        def evac_v(m, ps, pieces, waits):
            ops = []
            for col, h0, hw in pieces_cols(pieces):
                ops.append(s.op("vector", lambda v, m=m, ps=ps, col=col, h0=h0,
                                hw=hw: v.scalar_tensor_tensor(
                                    vg[:, h0:h0 + hw], ps[:, col:col + hw],
                                    sm["bv"][:, m:m + 1],
                                    gates[:, m * T + h0:m * T + h0 + hw],
                                    ALU.add, ALU.mult),
                                waits=waits + [small_ld,
                                               go_ops[m - 1] if m >= 1 else None],
                                tag=f"evv{m}"))
            scn = s.op("vector", lambda v, m=m: v.tensor_tensor_scan(
                kc_t[:, :], ug[:, m * T:(m + 1) * T], zero_b[:, :], 0.0,
                ALU.add, ALU.add),
                waits=[z0, pfx_done, ops, go_ops[m - 1] if m >= 1 else None],
                tag=f"scan{m}")
            scn2 = s.op("vector", lambda v, m=m: v.tensor_scalar_add(
                kc_t[:, :], kc_t[:, :], pfx[:, m:m + 1]), waits=[scn],
                tag=f"scanp{m}")
            gop = s.op("vector", lambda v, m=m: v.tensor_mul(
                go[:, m * T:(m + 1) * T], kc_t[:, :], vg[:, :]), waits=[scn2],
                tag=f"go{m}")
            go_ops[m] = gop
            return ops + [scn2, gop]

        ev_v = stream_matmul(cx, "v", wvt, 16, D, rhs_x1,
                             lambda kcc: apps[kcc], evac_v, PIECES_T, pst,
                             slab_pair=slab_big, rhs_cover=apps[-1])

        p1_stream = []

        def evac_uv(m, ps, pieces, waits):
            ops = []
            for col, h0, hw in pieces_cols(pieces):
                ops.append(s.op("vector", lambda v, m=m, ps=ps, col=col, h0=h0,
                                hw=hw: v.tensor_add(
                                    stream[:, m * T + h0:m * T + h0 + hw],
                                    stream[:, m * T + h0:m * T + h0 + hw],
                                    ps[:, col:col + hw]),
                                waits=waits + [conv_done], tag=f"evuv{m}"))
            p1_stream.extend(ops)
            return ops

        stream_matmul(cx, "uv", wut, 16, D,
                      lambda kcc, h0, hw: go[:, kcc * T + h0:kcc * T + h0 + hw],
                      lambda kcc: go_ops[kcc], evac_uv, PIECES_T, pst,
                      psum_prev=[ev_v[14], ev_v[15]], slab_pair=slab_big,
                      rhs_cover=go_ops[15])
        cx.cur = None
        p1.close()
        bar1 = _phase_barrier(cx, "b1")
        if upto == 1:
            _fin_store(cx, stream, o_out, bar1)
            s.emit(cx.stack)
            return cx.nc

        p2 = ExitStack()
        cx.cur = p2
        Mt = cx.sb("Mt2", [128, T], F32)
        wsd_tm = cx.sb("wsdtm", [128, 16], F32)
        p2a = ExitStack()
        cx.cur = p2a
        qx2 = cx.sb("qx2", [128, 16 * T], FP8)
        Mx = cx.sb("Mx2", [128, T], F32)
        absx = cx.sb("absx2", [128, T], F32)
        xfr = cx.sb("xfr2", [128, T], F32)
        wsg_tm = cx.sb("wsgtm", [128, 44], F32)
        wsu_tm = cx.sb("wsutm", [128, 44], F32)
        rmx_tm = cx.sb("rmxtm", [128, 32], F32)
        s448x = cx.sb("s448x2", [128, T], F32)
        r448x = cx.sb("r448x2", [128, T], F32)
        ln2 = ExitStack()
        bc2, bc2_ops = ln_stream_stats(cx, c, stream, bar1, ln2, "l2")
        scr2 = cx.sb("scr2l", [128, T], F32, stack=ln2)
        mops = []
        prevm = None
        for ci in range(16):
            a2 = ln_apply_sb(cx, stream, bc2, scr2, xfr[:, :], ci,
                             [bc2_ops, prevm], "x1")
            ab = s.op("scalar", lambda sc2: sc2.activation(
                absx[:, :], xfr[:, :], AF.Abs), waits=[a2, prevm], tag=f"abx{ci}")
            if ci == 0:
                prevm = s.op("vector", lambda v: v.tensor_copy(
                    Mx[:, :], absx[:, :]), waits=[ab], tag="mx0")
            else:
                prevm = s.op("vector", lambda v: v.tensor_max(
                    Mx[:, :], Mx[:, :], absx[:, :]), waits=[ab, prevm],
                    tag=f"mx{ci}")
            mops.append(prevm)
        _, _, bcx_rdy = rowmax_to_bcs(cx, c, Mx, mops[-1], c["identf"],
                                      tagp="x2", pre=(rmx_tm, s448x, r448x))
        q_ops = []
        prevq = None
        for ci in range(16):
            a2 = ln_apply_sb(cx, stream, bc2, scr2, xfr[:, :], ci,
                             [bcx_rdy, prevq], "x2")
            qo = s.op("vector", lambda v, ci=ci: v.tensor_mul(
                qx2[:, ci * T:(ci + 1) * T], xfr[:, :], s448x[:, :]),
                waits=[a2], tag=f"qx{ci}")
            prevq = qo
            q_ops.append(qo)
        ln2.close()

        QW = 1408
        pgu = ExitStack()
        qgsb = cx.sb("qgsb", [128, 16 * QW], FP8, stack=pgu)
        qusb = cx.sb("qusb", [128, 16 * QW], FP8, stack=pgu)
        psG = cx.psum("psG", [128, 1024], F32, stack=pgu)
        psU = cx.psum("psU", [128, 1024], F32, stack=pgu)
        g32 = cx.sb("g32", [128, T], F32, stack=pgu)
        sg32 = cx.sb("sg32", [128, T], F32, stack=pgu)
        u32 = cx.sb("u32", [128, T], F32, stack=pgu)
        tb = [cx.sb(f"tb{i}", [128, T], BF16, stack=pgu) for i in range(2)]
        abst = cx.sb("abst", [128, T], BF16, stack=pgu)
        ev_hist = {}
        mm_hist = {}
        st_hist = {}
        mt_ops = []
        sil_hist = {}
        qprev = [q_ops[-1]]
        qsh = {"rot": [cx.sb(f"qshra{i}", [128, D], F32, stack=pgu)
                       for i in range(2)],
               "rotb": [cx.sb(f"qshrb{i}", [128, QW], F32, stack=pgu)
                        for i in range(2)],
               "wsrow": cx.sb("qshws", [1, QW], F32, stack=pgu),
               "bcw": cx.sb("qshbc", [128, QW], F32, stack=pgu),
               "pr": cx.psum("qshpr", [1, 512], F32, stack=pgu),
               "pb": cx.psum("qshpb", [128, 512], F32, stack=pgu),
               "carry": [q_ops[-1]]}
        for qt4 in range(4):
            r0 = qt4 * QW
            qg_ops, wsg_rdy = quant_weight_local(
                cx, c, wgba, wgbt, I0, D, qgsb, wsg_tm, qprev, f"qg{qt4}",
                r0=r0, rw=QW, shared=qsh)
            qu_ops, wsu_rdy = quant_weight_local(
                cx, c, wuba, wubt, I0, D, qusb, wsu_tm, qprev, f"qu{qt4}",
                r0=r0, rw=QW, shared=qsh)
            for ml in range(11):
                m = qt4 * 11 + ml
                i = m % 2
                evp = ev_hist.get(m - 1, [])
                lastG = lastU = None
                for ps, qsb, qrdy in ((psG, qgsb, qg_ops),
                                      (psU, qusb, qu_ops)):
                    for col, h0, hw in pieces_cols(PIECES_T):
                        for kc in range(16):
                            wl = ([qrdy[15], evp]
                                  if (kc == 0 and col == 0) else [])
                            last = s.op("tensor", lambda t, ps=ps, qsb=qsb,
                                        kc=kc, ml=ml, col=col, h0=h0, hw=hw:
                                        t.matmul(
                                            ps[:, col:col + hw],
                                            qsb[:, kc * QW + ml * 128:kc * QW + (ml + 1) * 128],
                                            qx2[:, kc * T + h0:kc * T + h0 + hw],
                                            start=(kc == 0), stop=(kc == 15)),
                                        waits=wl, tag=f"c1mm{m}")
                    if ps is psG:
                        lastG = last
                    else:
                        lastU = last
                mm_hist[m] = last
                eprev = ev_hist.get(m - 1, [None])[-1]
                sprev = sil_hist.get(m - 1)
                e1 = s.op("vector", lambda v, m=m: v.scalar_tensor_tensor(
                    g32[:, :], psG[:, :], wsg_tm[:, m:m + 1], r448x[:, :],
                    ALU.mult, ALU.mult),
                    waits=[lastG, wsg_rdy, bcx_rdy, eprev, sprev],
                    tag=f"evg2{m}")
                e2 = s.op("scalar", lambda sc2: sc2.activation(
                    sg32[:, :], g32[:, :], AF.Silu), waits=[e1, eprev],
                    tag=f"sil{m}")
                sil_hist[m] = e2
                e3 = s.op("vector", lambda v, m=m: v.scalar_tensor_tensor(
                    u32[:, :], psU[:, :], wsu_tm[:, m:m + 1], r448x[:, :],
                    ALU.mult, ALU.mult), waits=[lastU, wsu_rdy, eprev],
                    tag=f"evu2{m}")
                e4 = s.op("vector", lambda v, i=i: v.tensor_mul(
                    tb[i][:, :], sg32[:, :], u32[:, :]),
                    waits=[e2, e3, st_hist.get(m - 2)], tag=f"tmul{m}")
                ab = s.op("scalar", lambda sc2, i=i: sc2.activation(
                    abst[:, :], tb[i][:, :], AF.Abs),
                    waits=[e4, mt_ops[-1] if mt_ops else None], tag=f"abt{m}")
                if m == 0:
                    e5 = s.op("vector", lambda v: v.tensor_copy(
                        Mt[:, :], abst[:, :]), waits=[ab], tag=f"mt{m}")
                else:
                    e5 = s.op("vector", lambda v: v.tensor_max(
                        Mt[:, :], Mt[:, :], abst[:, :]), waits=[ab, mt_ops[-1]],
                        tag=f"mt{m}")
                mt_ops.append(e5)
                st = s.dma("sync", tdram[:, m * T:(m + 1) * T], tb[i][:, :],
                           waits=[e4], tag=f"stt{m}")
                st_hist[m] = st
                ev_hist[m] = [e1, e3]
            qprev = ev_hist[qt4 * 11 + 10] + [mm_hist[qt4 * 11 + 10]]
        t_stores = list(st_hist.values())
        cx.cur = None
        pgu.close()
        p2a.close()
        cx.cur = p2
        bar2a = _phase_barrier(cx, "b2a")
        s448t = cx.sb("s448t2", [128, T], F32)
        r448t = cx.sb("r448t2", [128, T], F32)
        rmt_tm = cx.sb("rmttm", [128, 32], F32)
        _, _, bct_rdy = rowmax_to_bcs(cx, c, Mt, [mt_ops[-1], bar2a],
                                      c["identf"], tagp="t2",
                                      pre=(rmt_tm, s448t, r448t))
        qt = cx.sb("qt2", [128, 44 * T], FP8)
        trot = [cx.sb(f"trot{i}", [128, T], BF16) for i in range(2)]
        qt_ops = []
        prevt = {}
        for mi in range(44):
            i = mi % 2
            ldt = s.dma("sync", trot[i][:, :], tdram[:, mi * T:(mi + 1) * T],
                        waits=[prevt.get(i), st_hist[mi], bar2a], tag=f"ldt{mi}")
            qo = s.op("vector", lambda v, mi=mi, i=i: v.tensor_mul(
                qt[:, mi * T:(mi + 1) * T], trot[i][:, :], s448t[:, :]),
                waits=[ldt, bct_rdy], tag=f"qt{mi}")
            prevt[i] = qo
            qt_ops.append(qo)
        dscr = cx.sb("dscr2", [128, T], F32)
        p2_stream = []
        DHW = 1024
        pdn = ExitStack()
        qdsb = cx.sb("qdsb", [128, 44 * DHW], FP8, stack=pdn)
        pst2 = [cx.psum(f"pp2{i}", [128, 1024], F32, stack=pdn) for i in range(2)]
        dn_hist = {}
        dprev = [bar2a, qt_ops[-1]]
        for hh in range(2):
            qd_ops, wsd_rdy = quant_weight_local(
                cx, c, wdna, wdnt, D, I0, qdsb, wsd_tm, dprev, f"qd{hh}",
                r0=hh * DHW, rw=DHW)
            for ml in range(8):
                m = hh * 8 + ml
                i = m % 2
                evp = dn_hist.get(m - 2, [bar2a])
                last = None
                for col, h0, hw in pieces_cols(PIECES_T):
                    for kc in range(44):
                        wl = ([qd_ops[43], evp]
                              if (kc == 0 and col == 0) else [])
                        last = s.op("tensor", lambda t, i=i, kc=kc, ml=ml,
                                    col=col, h0=h0, hw=hw: t.matmul(
                                        pst2[i][:, col:col + hw],
                                        qdsb[:, kc * DHW + ml * 128:kc * DHW + (ml + 1) * 128],
                                        qt[:, kc * T + h0:kc * T + h0 + hw],
                                        start=(kc == 0), stop=(kc == 43)),
                                    waits=wl, tag=f"dnmm{m}")
                ops = []
                for col, h0, hw in pieces_cols(PIECES_T):
                    o1 = s.op("vector", lambda v, i=i, m=m, col=col, hw=hw:
                              v.scalar_tensor_tensor(
                                  dscr[:, col:col + hw],
                                  pst2[i][:, col:col + hw],
                                  wsd_tm[:, m:m + 1], r448t[:, col:col + hw],
                                  ALU.mult, ALU.mult),
                              waits=[last, wsd_rdy, bct_rdy,
                                     dn_hist.get(m - 1, [None])[-1]],
                              tag=f"evd{m}")
                    o2 = s.op("vector", lambda v, m=m, col=col, h0=h0, hw=hw:
                              v.tensor_add(
                                  stream[:, m * T + h0:m * T + h0 + hw],
                                  stream[:, m * T + h0:m * T + h0 + hw],
                                  dscr[:, col:col + hw]),
                              waits=[o1, p1_stream], tag=f"evd2{m}")
                    ops.append(o2)
                dn_hist[m] = ops
                p2_stream.extend(ops)
            dprev = dn_hist[hh * 8 + 7]
        cx.cur = None
        pdn.close()
        p2.close()
        bar2 = _phase_barrier(cx, "b2")
        if upto == 2:
            _fin_store(cx, stream, o_out, bar2)
            s.emit(cx.stack)
            return cx.nc

        p3 = ExitStack()
        cx.cur = p3
        xb3 = cx.sb("xb3", [128, 16 * T], BF16)
        o_sb = cx.sb("o_sb3", [128, 8 * T], BF16)
        spack = cx.sb("spack3", [128, 4096], F32)
        a_sb = cx.sb("a_sb3", [128, 16], F32)
        cc_sb = cx.sb("cc_sb3", [128, 128], F32)
        sel = cx.sb("sel3", [128, 4], F32)
        ld_sel = s.dma("sync", sel[:, :], selm, waits=[bar2], tag="ldsel")
        ln3 = ExitStack()
        bc3, bc3_ops = ln_stream_stats(cx, c, stream, bar2, ln3, "l3")
        scr3 = cx.sb("scr3l", [128, T], F32, stack=ln3)
        apps = []
        prevq = None
        for ci in range(16):
            a2 = ln_apply_sb(cx, stream, bc3, scr3,
                             xb3[:, ci * T:(ci + 1) * T], ci,
                             [bc3_ops, prevq], "x3")
            prevq = a2
            apps.append(a2)
        ln3.close()

        d_sub = ExitStack()
        cx.cur = d_sub
        v_tm = cx.sb("v_tm3", [128, 8 * T], BF16)
        sb_S = cx.sb("sbS3", [128, 1024], BF16)
        fm1 = cx.sb("fm13", [16, T], BF16)
        qe_h = cx.sb("qe_h3", [128, 4 * T], BF16)
        kd_h = cx.sb("kd_h3", [128, 4 * T], BF16)
        kdtm = cx.sb("kdtm3", [128, 8 * 512], BF16)
        ebt = cx.sb("ebt3", [128, T], BF16)
        embt = cx.sb("embt3", [128, T], BF16)
        gkp = cx.sb("gkp3", [128, T], F32)
        sA = cx.sb("sA3", [128, T], F32)
        sB = cx.sb("sB3", [128, T], F32)
        m0t = cx.sb("m0t3", [128, T], F32)
        b_t = cx.sb("b_t3", [128, T], F32)
        zs = cx.sb("zs3", [128, 32], F32)
        bl = cx.sb("bl3", [128, 32], F32)
        zsh = cx.sb("zsh3", [128, 32], F32)
        wexp = cx.sb("wexp3", [128, 32], F32)
        ebias = cx.sb("ebias3", [128, 8], F32)
        asb = cx.sb("asb3", [128, 128], BF16)
        misc = cx.sb("misc3", [128, 64], F32)
        slq = [cx.sb("slq3", [128, D], BF16)] * 2
        wgk1_sb = cx.sb("wgk1sb3", [128, 256], BF16)
        wgk2_sb = cx.sb("wgk2sb3", [16, 512], BF16)
        smv, small_ld3 = cx.smalls([("bgk2", bgk2)])
        ld1 = s.dma("gpsimd", wgk1_sb[:, :], wgk1p, waits=[bar2], tag="ldgk1")
        scope_v = ExitStack()
        wv_sb = cx.sb("wv_sb3", [128, 16 * 1024], BF16, stack=scope_v)
        psv = [cx.psum(f"psv3{i}", [128, 1024], F32, stack=scope_v) for i in range(2)]
        psf = cx.psum("psf3", [16, T], F32, stack=scope_v)
        ldv = s.dma("gpsimd", wv_sb[:, :], wv_nat, waits=[apps], tag="ldwv")
        v_ev = []
        for tc in range(8):
            i = tc % 2
            last = None
            for col, h0, hw in pieces_cols(PIECES_T):
                for kc in range(16):
                    wl = ([ldv, apps[-1], v_ev[tc - 2] if tc >= 2 else None]
                          if (kc == 0 and col == 0) else [])
                    last = s.op("tensor", lambda t, i=i, kc=kc, tc=tc, col=col,
                                h0=h0, hw=hw: t.matmul(
                                    psv[i][:, col:col + hw],
                                    xb3[:, kc * T + tc * 128:kc * T + tc * 128 + 128],
                                    wv_sb[:, kc * 1024 + h0:kc * 1024 + h0 + hw],
                                    start=(kc == 0), stop=(kc == 15)),
                                waits=wl, tag=f"vmm3{tc}")
            ev = s.op("vector", lambda v, i=i, tc=tc: v.tensor_copy(
                v_tm[:, tc * T:(tc + 1) * T], psv[i][:, :]), waits=[last],
                tag=f"vev3{tc}")
            v_ev.append(ev)
        lastf = None
        for col, h0, hw in pieces_cols(PIECES_T):
            for kc in range(16):
                lastf = s.op("tensor", lambda t, kc=kc, col=col, h0=h0, hw=hw:
                             t.matmul(psf[0:16, col:col + hw],
                                      wgk1_sb[:, kc * 16:(kc + 1) * 16],
                                      xb3[:, kc * T + h0:kc * T + h0 + hw],
                                      start=(kc == 0), stop=(kc == 15)),
                             waits=([ld1, apps[-1], v_ev[-1]]
                                    if (kc == 0 and col == 0) else []),
                             tag="fmm3")
        evf = s.op("vector", lambda v: v.tensor_copy(fm1[:, :], psf[0:16, :]),
                   waits=[lastf], tag="fev3")
        scope_v.close()

        psqk = cx.psum("psqk3", [128, 1024], F32)
        psA = cx.psum("psA3", [128, 128], F32)
        psO = cx.psum("psO3", [128, 256], F32)
        psOb = cx.psum("psOb3", [128, 128], BF16)
        psT = cx.psum("psT3", [128, 1024], F32)
        qk_last_ev = [evf]
        P = {}
        st_qe = [None] * 4
        o_cp_last = None
        final_ops = []
        ld2p = None
        for h in range(4):
            ld2 = s.dma("gpsimd", wgk2_sb[:, :], wgk2[:, h * 512:(h + 1) * 512],
                        waits=[bar2, ld2p], tag=f"ldgk2{h}")
            mq_done = []
            for mq in range(4):
                gq = 4 * h + mq
                lastg = None
                for col, h0, hw in pieces_cols(PIECES_T):
                    lastg = s.op("tensor", lambda t, mq=mq, col=col, h0=h0,
                                 hw=hw: t.matmul(
                                     psqk[:, col:col + hw],
                                     wgk2_sb[0:16, mq * 128:(mq + 1) * 128],
                                     fm1[:, h0:h0 + hw], start=True, stop=True),
                                 waits=[ld2, evf, qk_last_ev], tag=f"gkmm{gq}")
                e0 = s.op("vector", lambda v, gq=gq: v.tensor_scalar_add(
                    gkp[:, :], psqk[:, :], smv["bgk2"][:, gq:gq + 1]),
                    waits=[lastg, small_ld3, P.get("gkp")], tag=f"gk0{gq}")
                qk_last_ev = [e0]
                e1 = s.op("vector", lambda v: v.tensor_scalar_min(
                    m0t[:, :], gkp[:, :], 0.0), waits=[e0, P.get("m0")],
                    tag=f"gk1{gq}")
                e2 = s.op("scalar", lambda sc2: sc2.activation(
                    sA[:, :], gkp[:, :], AF.Abs), waits=[e0, P.get("sA")],
                    tag=f"gk2{gq}")
                e3 = s.op("scalar", lambda sc2: sc2.activation(
                    sB[:, :], sA[:, :], AF.Exp, scale=-1.0),
                    waits=[e2, P.get("sB")], tag=f"gk3{gq}")
                e4 = s.op("scalar", lambda sc2: sc2.activation(
                    sA[:, :], sB[:, :], AF.Ln, bias=onec[:, 0:1]),
                    waits=[e3], tag=f"gk4{gq}")
                e5 = s.op("vector", lambda v: v.scalar_tensor_tensor(
                    m0t[:, :], sA[:, :], -1.0, m0t[:, :], ALU.mult, ALU.add),
                    waits=[e4, e1], tag=f"gk5{gq}")
                P["sA"] = e5
                P["sB"] = e4
                P["m0"] = e5
                scn = s.op("vector", lambda v: v.tensor_tensor_scan(
                    b_t[:, :], c["mask128"][:, :], m0t[:, :], 0.0,
                    ALU.mult, ALU.add), waits=[e5, P.get("b")], tag=f"gk6{gq}")
                P["gkp"] = [e1, e2]
                x1 = s.op("vector", lambda v, mq=mq: v.tensor_copy(
                    zs[:, mq * 8:(mq + 1) * 8].rearrange("p (cc o) -> p cc o", o=1),
                    b_t[:, :].rearrange("p (cc t) -> p cc t", cc=8)[:, :, 64:65]),
                    waits=[scn, P.get("zsg")], tag=f"zs{gq}")
                x2 = s.op("vector", lambda v, mq=mq: v.tensor_copy(
                    bl[:, mq * 8:(mq + 1) * 8].rearrange("p (cc o) -> p cc o", o=1),
                    b_t[:, :].rearrange("p (cc t) -> p cc t", cc=8)[:, :, 127:128]),
                    waits=[scn, P.get("blg")], tag=f"bl{gq}")
                x3 = s.op("vector", lambda v, mq=mq: v.tensor_scalar(
                    ebias[:, :], zs[:, mq * 8:(mq + 1) * 8], -1.0 / 16.0,
                    LNSCALE_DK, ALU.mult, ALU.add), waits=[x1, P.get("ebias")],
                    tag=f"ebias{gq}")
                ebo = None
                for ch in range(8):
                    ebo = s.op("scalar", lambda sc2, ch=ch: sc2.activation(
                        ebt[:, ch * 128:(ch + 1) * 128],
                        b_t[:, ch * 128:(ch + 1) * 128], AF.Exp,
                        scale=1.0 / 16.0, bias=ebias[:, ch:ch + 1]),
                        waits=[x3, scn, P.get("ebt")], tag=f"eb{gq}")
                x4 = s.op("vector", lambda v, mq=mq: v.tensor_scalar_mul(
                    ebias[:, :], zs[:, mq * 8:(mq + 1) * 8], 1.0 / 16.0),
                    waits=[ebo], tag=f"embias{gq}")
                emo = None
                for ch in range(8):
                    emo = s.op("scalar", lambda sc2, ch=ch: sc2.activation(
                        embt[:, ch * 128:(ch + 1) * 128],
                        b_t[:, ch * 128:(ch + 1) * 128], AF.Exp,
                        scale=-1.0 / 16.0, bias=ebias[:, ch:ch + 1]),
                        waits=[x4, P.get("embt")], tag=f"em{gq}")
                P["b"] = [emo, x1, x2]
                P["ebias"] = emo
                i2 = mq % 2
                ldq = s.dma("gpsimd", slq[i2][:, :], wq_t[gq],
                            waits=[P.get(("sl", i2)), bar2], tag=f"ldq3{gq}")
                lastq = None
                for col, h0, hw in pieces_cols(PIECES_T):
                    for kc in range(16):
                        wl = ([ldq, apps[-1], qk_last_ev]
                              if (kc == 0 and col == 0) else [])
                        lastq = s.op("tensor", lambda t, i2=i2, kc=kc, col=col,
                                     h0=h0, hw=hw: t.matmul(
                                         psqk[:, col:col + hw],
                                         slq[i2][:, kc * 128:(kc + 1) * 128],
                                         xb3[:, kc * T + h0:kc * T + h0 + hw],
                                         start=(kc == 0), stop=(kc == 15)),
                                     waits=wl, tag=f"qmm{gq}")
                qev = s.op("vector", lambda v, mq=mq: v.tensor_mul(
                    qe_h[:, mq * T:(mq + 1) * T], psqk[:, :], ebt[:, :]),
                    waits=[lastq, ebo, st_qe[h - 1] if h > 0 else None,
                           P.get(("qe", mq))], tag=f"qev{gq}")
                qk_last_ev = [qev]
                ldk = s.dma("gpsimd", slq[i2][:, :], wk_t[gq],
                            waits=[lastq], tag=f"ldk3{gq}")
                lastk = None
                for col, h0, hw in pieces_cols(PIECES_T):
                    for kc in range(16):
                        wl = ([ldk, qk_last_ev]
                              if (kc == 0 and col == 0) else [])
                        lastk = s.op("tensor", lambda t, i2=i2, kc=kc, col=col,
                                     h0=h0, hw=hw: t.matmul(
                                         psqk[:, col:col + hw],
                                         slq[i2][:, kc * 128:(kc + 1) * 128],
                                         xb3[:, kc * T + h0:kc * T + h0 + hw],
                                         start=(kc == 0), stop=(kc == 15)),
                                     waits=wl, tag=f"kmm{gq}")
                kev = s.op("vector", lambda v, mq=mq: v.tensor_mul(
                    kd_h[:, mq * T:(mq + 1) * T], psqk[:, :], embt[:, :]),
                    waits=[lastk, emo, P.get(("kd", mq))], tag=f"kev{gq}")
                qk_last_ev = [kev]
                P[("sl", i2)] = lastk
                P["ebt"] = qev
                P["embt"] = kev
                mq_done.append([qev, kev])
            w2 = s.op("vector", lambda v: v.tensor_sub(
                wexp[:, :], bl[:, :], zs[:, :]),
                waits=[mq_done, P.get("wexpg")], tag=f"wx1{h}")
            w0 = s.op("vector", lambda v: v.tensor_copy(
                zsh[:, 0:31], zs[:, 1:32]), waits=[mq_done, P.get("zshg")],
                tag=f"zsh{h}")
            w1 = s.op("vector", lambda v: v.memset(
                zsh[:, :].rearrange("p (m cc) -> p m cc", m=4)[:, :, 7:8], 0.0),
                waits=[w0], tag=f"zshz{h}")
            w3 = s.op("vector", lambda v: v.tensor_add(
                wexp[:, :], wexp[:, :], zsh[:, :]), waits=[w2, w1], tag=f"wx2{h}")
            w4 = s.op("scalar", lambda sc2: sc2.activation(
                wexp[:, :], wexp[:, :], AF.Exp, scale=1.0 / 16.0),
                waits=[w3], tag=f"wx3{h}")
            P["zshg"] = w3
            tr_cp = []
            prev_cp = P.get("kdtm")
            for ccc in range(8):
                for mq in range(4):
                    tr = s.op("tensor", lambda t, ccc=ccc, mq=mq: t.transpose(
                        psOb[:, :],
                        kd_h[:, mq * T + ccc * 128:mq * T + ccc * 128 + 128],
                        c["ident"][:, :]),
                        waits=[mq_done[mq], prev_cp], tag=f"ktr{h}")
                    cp = s.op("vector", lambda v, ccc=ccc, mq=mq: v.tensor_copy(
                        kdtm[:, ccc * 512 + mq * 128:ccc * 512 + mq * 128 + 128],
                        psOb[:, :]), waits=[tr], tag=f"ktc{h}")
                    prev_cp = cp
                    tr_cp.append(cp)
            z1 = s.op("vector", lambda v, h=h: v.memset(
                spack[:, h * 1024:(h + 1) * 1024], 0.0), waits=[bar2],
                tag=f"sz{h}")
            z2 = s.op("vector", lambda v: v.memset(sb_S[:, :], 0.0),
                      waits=[P.get("sbS")], tag=f"sbz{h}")
            prev_asb = P.get("asbg")
            prev_ocp = None
            prev_upd = [z1] * 4
            prev_cast = [z2] * 4
            for ccc in range(8):
                lastA = None
                for mq in range(4):
                    lastA = s.op("tensor", lambda t, ccc=ccc, mq=mq: t.matmul(
                        psA[:, :],
                        kd_h[:, mq * T + ccc * 128:mq * T + ccc * 128 + 128],
                        qe_h[:, mq * T + ccc * 128:mq * T + ccc * 128 + 128],
                        start=(mq == 0), stop=(mq == 3)),
                        waits=[mq_done, prev_asb], tag=f"Amm{h}{ccc}")
                aev = s.op("vector", lambda v: v.tensor_mul(
                    asb[:, :], psA[:, :], tril[:, :]), waits=[lastA, trl],
                    tag=f"Aev{h}{ccc}")
                prev_asb = aev
                o_mms = [s.op("tensor", lambda t, ccc=ccc, h=h: t.matmul(
                    psO[:, :], asb[:, :],
                    v_tm[:, ccc * T + h * 256:ccc * T + h * 256 + 256],
                    start=True, stop=(ccc == 0)),
                    waits=[aev, v_ev[ccc], prev_ocp], tag=f"Omm{h}{ccc}")]
                if ccc > 0:
                    for mq in range(4):
                        o_mms.append(s.op("tensor", lambda t, ccc=ccc, mq=mq:
                                          t.matmul(
                                              psO[:, :],
                                              qe_h[:, mq * T + ccc * 128:mq * T + ccc * 128 + 128],
                                              sb_S[:, mq * 256:(mq + 1) * 256],
                                              start=False, stop=(mq == 3)),
                                          waits=[prev_cast[mq]],
                                          tag=f"O2mm{h}{ccc}"))
                ocp = s.op("vector", lambda v, ccc=ccc, h=h: v.tensor_copy(
                    o_sb[:, ccc * T + h * 256:ccc * T + h * 256 + 256],
                    psO[:, :]), waits=o_mms, tag=f"Ocp{h}{ccc}")
                prev_ocp = ocp
                o_cp_last = ocp
                t_mms = []
                for mq in range(4):
                    t_mms.append(s.op("tensor", lambda t, ccc=ccc, mq=mq, h=h:
                                      t.matmul(
                                          psT[:, mq * 256:(mq + 1) * 256],
                                          kdtm[:, ccc * 512 + mq * 128:ccc * 512 + mq * 128 + 128],
                                          v_tm[:, ccc * T + h * 256:ccc * T + h * 256 + 256],
                                          start=True, stop=True),
                                      waits=[tr_cp[ccc * 4 + mq], v_ev[ccc],
                                             prev_upd[mq]], tag=f"Tmm{h}{ccc}"))
                for mq in range(4):
                    sl_ = spack[:, h * 1024 + mq * 256:h * 1024 + (mq + 1) * 256]
                    u1 = s.op("vector", lambda v, sl_=sl_, mq=mq: v.tensor_add(
                        sl_, sl_, psT[:, mq * 256:(mq + 1) * 256]),
                        waits=[t_mms[mq], prev_upd[mq]], tag=f"su{h}{ccc}")
                    u2 = s.op("vector", lambda v, sl_=sl_, mq=mq, ccc=ccc:
                              v.tensor_scalar_mul(
                                  sl_, sl_,
                                  wexp[:, mq * 8 + ccc:mq * 8 + ccc + 1]),
                              waits=[u1, w4], tag=f"sm{h}{ccc}")
                    prev_upd[mq] = u2
                    if ccc < 7:
                        u3 = s.op("vector", lambda v, sl_=sl_, mq=mq:
                                  v.tensor_copy(
                                      sb_S[:, mq * 256:(mq + 1) * 256], sl_),
                                  waits=[u2, o_mms], tag=f"scast{h}{ccc}")
                        prev_cast[mq] = u3
            st_qe[h] = s.dma("sync", qedram[:, h * 4096:(h + 1) * 4096],
                             qe_h[:, :], waits=[prev_asb, prev_ocp],
                             tag=f"stqe{h}")
            P["kdtm"] = prev_upd
            for mq in range(4):
                P[("kd", mq)] = [prev_asb, tr_cp[-1]]
                P[("qe", mq)] = [prev_asb, prev_ocp]
            P["sbS"] = prev_cast
            P["asbg"] = prev_asb
            P["wexpg"] = prev_upd
            fo = []
            for mq in range(4):
                fo.append(s.op("vector", lambda v, mq=mq: v.reduce_sum(
                    misc[:, mq:mq + 1], bl[:, mq * 8:(mq + 1) * 8], AX.X),
                    waits=[mq_done, P.get("misc")], tag=f"bls{h}{mq}"))
            a1 = s.op("scalar", lambda sc2, h=h: sc2.activation(
                a_sb[:, h * 4:(h + 1) * 4], misc[:, 0:4], AF.Exp,
                scale=1.0 / 16.0), waits=fo, tag=f"aex{h}")
            P["misc"] = a1
            c1_ = s.op("vector", lambda v: v.tensor_tensor_scan(
                misc[:, 32:64], mask8[:, :], bl[:, 0:32], 0.0,
                ALU.mult, ALU.add), waits=[mq_done, P.get("misc2"), m8l],
                tag=f"ccs{h}")
            c2_ = s.op("vector", lambda v: v.tensor_sub(
                misc[:, 32:64], misc[:, 32:64], bl[:, 0:32]), waits=[c1_],
                tag=f"ccsub{h}")
            c3_ = s.op("vector", lambda v: v.tensor_add(
                misc[:, 32:64], misc[:, 32:64], zs[:, 0:32]), waits=[c2_],
                tag=f"ccadd{h}")
            c4_ = s.op("scalar", lambda sc2, h=h: sc2.activation(
                cc_sb[:, h * 32:(h + 1) * 32], misc[:, 32:64], AF.Exp,
                scale=1.0 / 16.0), waits=[c3_], tag=f"ccex{h}")
            P["misc2"] = c4_
            P["zsg"] = [w4, c4_]
            P["blg"] = [w4, a1, c4_]
            final_ops += [a1, c4_]
            ld2p = lastg
        cx.cur = p3
        d_sub.close()
        sbin = cx.sb("sbin3", [128, 4096], BF16)
        og_fm = cx.sb("og_fm3", [128, 8 * T], BF16)
        st_sp = s.dma("sync", sp_in[:, 0:4096], spack[:, :],
                      waits=[P["kdtm"]], tag="stsp")
        st_a = s.dma("sync", sp_in[:, 4096:4112], a_sb[:, :], waits=final_ops,
                     tag="sta")
        if SIM1:
            cc2 = [s.dma("gpsimd", sp_out[j * 128:(j + 1) * 128, :], sp_in,
                         waits=[st_sp, st_a], tag=f"cc2sim{j}") for j in range(4)]
        else:
            cc2 = s.op("gpsimd", lambda g: g.collective_compute(
                "AllGather", ALU.bypass, replica_groups=RG, ins=[sp_in.opt()],
                outs=[sp_out.opt()]), waits=[st_sp, st_a], async_inc=1,
                tag="cc2")
        bar3a = _phase_barrier(cx, "b3a")
        sc1 = ExitStack()
        T_acc = cx.sb("T_acc3", [128, 4096], F32, stack=sc1)
        S_in = cx.sb("S_in3", [128, 4096], F32, stack=sc1)
        rotS = [cx.sb(f"rotS3{i}", [128, 4096], F32, stack=sc1) for i in range(2)]
        a_gt = cx.sb("a_gt3", [128, 64], F32, stack=sc1)
        ld_a = s.dma("sync", a_gt[:, :].rearrange("p (g m) -> p g m", g=4),
                     sp_out.rearrange("(g p) m -> p g m", p=128)[:, :, 4096:4112],
                     waits=[cc2, bar3a], tag="lda3")
        ld_S = []
        for j in range(4):
            ld_S.append(s.dma("sync", rotS[j % 2][:, :],
                              sp_out[j * 128:(j + 1) * 128, 0:4096],
                              waits=[cc2, bar3a], tag=f"ldS3{j}"))
        cp0 = s.op("vector", lambda v: v.tensor_copy(T_acc[:, :], rotS[0][:, :]),
                   waits=[ld_S[0]], tag="cp03")
        si0 = s.op("vector", lambda v: v.tensor_scalar_mul(
            S_in[:, :], T_acc[:, :], sel[:, 0:1]), waits=[cp0, ld_sel],
            tag="si03")
        ld_S[2].waits.append(cp0)
        prevsi = si0
        prevT = [cp0]
        for j in range(1, 4):
            tops = []
            for hm in range(16):
                tops.append(s.op("vector", lambda v, j=j, hm=hm:
                                 v.scalar_tensor_tensor(
                                     T_acc[:, hm * 256:(hm + 1) * 256],
                                     T_acc[:, hm * 256:(hm + 1) * 256],
                                     a_gt[:, j * 16 + hm:j * 16 + hm + 1],
                                     rotS[j % 2][:, hm * 256:(hm + 1) * 256],
                                     ALU.mult, ALU.add),
                                 waits=[ld_S[j], ld_a, prevT, prevsi],
                                 tag=f"tj3{j}"))
            if j + 2 < 4:
                ld_S[j + 2].waits.extend(tops)
            prevT = tops
            prevsi = s.op("vector", lambda v, j=j: v.scalar_tensor_tensor(
                S_in[:, :], T_acc[:, :], sel[:, j:j + 1], S_in[:, :],
                ALU.mult, ALU.add), waits=[tops, prevsi], tag=f"si3{j}")
        cbin = s.op("vector", lambda v: v.tensor_copy(sbin[:, :], S_in[:, :]),
                    waits=[prevsi], tag="cbin3")
        cx.cur = p3
        sc1.close()
        sc2c = ExitStack()
        qrot = [cx.sb(f"qrot3{i}", [128, 4096], BF16, stack=sc2c)
                for i in range(2)]
        qs = cx.sb("qs3", [128, 128], BF16, stack=sc2c)
        psC = cx.psum("psC3", [128, 256], F32, stack=sc2c)
        prev_oadd = None
        prev_qs = None
        corr_ops = []
        h_last = {}
        for h in range(4):
            ldq_h = s.dma("sync", qrot[h % 2][:, :],
                          qedram[:, h * 4096:(h + 1) * 4096],
                          waits=[h_last.get(h % 2), st_qe[h],
                                 cbin if h < 2 else None], tag=f"ldqc{h}")
            for ccc in range(8):
                cmms = []
                for mq in range(4):
                    qso = s.op("vector", lambda v, h=h, ccc=ccc, mq=mq:
                               v.tensor_scalar_mul(
                                   qs[:, :],
                                   qrot[h % 2][:, mq * T + ccc * 128:mq * T + ccc * 128 + 128],
                                   cc_sb[:, h * 32 + mq * 8 + ccc:h * 32 + mq * 8 + ccc + 1]),
                               waits=[ldq_h, prev_qs], tag=f"qsc{h}{ccc}")
                    cmms.append(s.op("tensor", lambda t, h=h, mq=mq: t.matmul(
                        psC[:, :], qs[:, :],
                        sbin[:, (h * 4 + mq) * 256:(h * 4 + mq + 1) * 256],
                        start=(mq == 0), stop=(mq == 3)),
                        waits=[qso, cbin, prev_oadd], tag=f"cmm3{h}{ccc}"))
                    prev_qs = cmms[-1]
                oadd = s.op("vector", lambda v, h=h, ccc=ccc: v.tensor_add(
                    o_sb[:, ccc * T + h * 256:ccc * T + h * 256 + 256],
                    o_sb[:, ccc * T + h * 256:ccc * T + h * 256 + 256],
                    psC[:, :]), waits=[cmms], tag=f"oadd3{h}{ccc}")
                prev_oadd = oadd
                corr_ops.append(oadd)
            h_last[h % 2] = prev_qs
        cx.cur = p3
        sc2c.close()
        sc4 = ExitStack()
        cx.cur = sc4
        dump = cx.sb("dump3", [128, 256], F32)
        racc = cx.sb("racc3", [128, 32], F32)
        rstd = cx.sb("rstd3", [128, 32], F32)
        wnb = cx.sb("wnb3", [128, T], BF16)
        wrow = cx.sb("wrow3", [1, T], BF16)
        ld_wr = s.dma("gpsimd", wrow[:, :], wnorm, waits=[corr_ops[-1]],
                      tag="ldwr3")
        racc_ops = []
        prev_d = None
        for tc in range(8):
            for h in range(4):
                prev_d = s.op("scalar", lambda sc2, tc=tc, h=h: sc2.activation(
                    dump[:, :],
                    o_sb[:, tc * T + h * 256:tc * T + h * 256 + 256],
                    AF.Square, accum_out=racc[:, tc * 4 + h:tc * 4 + h + 1]),
                    waits=[corr_ops, prev_d], tag=f"sq3{tc}{h}")
                racc_ops.append(prev_d)
        r1 = s.op("scalar", lambda sc2: sc2.activation(
            rstd[:, :], racc[:, :], AF.Sqrt, scale=1.0 / 256.0,
            bias=c["eps_col"][:, 0:1]), waits=racc_ops, tag="rsq3")
        r2 = s.op("vector", lambda v: v.reciprocal(rstd[:, :], rstd[:, :]),
                  waits=[r1], tag="rrec3")
        psW = cx.psum("psW3", [128, 512], F32)
        wb_ops = []
        prevw = None
        for h0 in range(0, T, 512):
            mm = s.op("tensor", lambda t, h0=h0: t.matmul(
                psW[:, :], c["ones_row_b"][0:1, :], wrow[0:1, h0:h0 + 512],
                start=True, stop=True),
                waits=[ld_wr, prevw, c["ld"], corr_ops[-1]], tag="wbc3")
            prevw = s.op("vector", lambda v, h0=h0: v.tensor_copy(
                wnb[:, h0:h0 + 512], psW[:, :]), waits=[mm], tag="wbcc3")
            wb_ops.append(prevw)
        wg_sb = cx.sb("wg_sb3", [128, 16 * 1024], BF16)
        sg_t = cx.sb("sg_t3", [128, T], BF16)
        psg = [cx.psum(f"psg3{i}", [128, 1024], F32) for i in range(2)]
        cpx = None
        ld_wg = s.dma("gpsimd", wg_sb[:, :], wg_nat,
                      waits=[corr_ops[-1]], tag="ldwg3")
        og_ops = []
        prev_sg = None
        for tc in range(8):
            i = tc % 2
            last = None
            for col, h0, hw in pieces_cols(PIECES_T):
                for kc in range(16):
                    wl = ([ld_wg, apps[-1], og_ops[-8:] if tc >= 2 else None]
                          if (kc == 0 and col == 0) else [])
                    last = s.op("tensor", lambda t, i=i, kc=kc, tc=tc, col=col,
                                h0=h0, hw=hw: t.matmul(
                                    psg[i][:, col:col + hw],
                                    xb3[:, kc * T + tc * 128:kc * T + tc * 128 + 128],
                                    wg_sb[:, kc * 1024 + h0:kc * 1024 + h0 + hw],
                                    start=(kc == 0), stop=(kc == 15)),
                                waits=wl, tag=f"gmm3{tc}")
            sg = s.op("scalar", lambda sc2, i=i: sc2.activation(
                sg_t[:, :], psg[i][:, :], AF.Silu), waits=[last, prev_sg],
                tag=f"gsil3{tc}")
            gt = s.op("vector", lambda v: v.tensor_mul(
                sg_t[:, :], sg_t[:, :], wnb[:, :]), waits=[sg, wb_ops],
                tag=f"gwn3{tc}")
            tc_ops = []
            for h in range(4):
                tc_ops.append(s.op("vector", lambda v, tc=tc, h=h:
                                   v.scalar_tensor_tensor(
                                       o_sb[:, tc * T + h * 256:tc * T + h * 256 + 256],
                                       o_sb[:, tc * T + h * 256:tc * T + h * 256 + 256],
                                       rstd[:, tc * 4 + h:tc * 4 + h + 1],
                                       sg_t[:, h * 256:(h + 1) * 256],
                                       ALU.mult, ALU.mult),
                                   waits=[gt, r2, corr_ops, racc_ops],
                                   tag=f"og3{tc}{h}"))
            og_ops.extend(tc_ops)
            prev_sg = tc_ops
        psTr = cx.psum("psTr3", [128, 128], BF16)
        prev_cp = None
        tr_cps = {}
        for tc in range(8):
            for vc in range(8):
                tr = s.op("tensor", lambda t, tc=tc, vc=vc: t.transpose(
                    psTr[:, :],
                    o_sb[:, tc * T + vc * 128:tc * T + vc * 128 + 128],
                    c["ident"][:, :]), waits=[og_ops, prev_cp], tag=f"ogtr3{tc}")
                cp = s.op("vector", lambda v, tc=tc, vc=vc: v.tensor_copy(
                    og_fm[:, vc * T + tc * 128:vc * T + tc * 128 + 128],
                    psTr[:, :]), waits=[tr], tag=f"ogtc3{tc}")
                prev_cp = cp
                tr_cps[vc] = cp
        cx.cur = p3
        sc4.close()
        bar3b = _phase_barrier(cx, "b3b")
        cx.cur = p3
        pstwo = [cx.psum(f"ppwo{i}", [128, 1024], F32) for i in range(2)]
        p3_stream = []

        def evac_wo(m, ps, pieces, waits):
            ops = []
            for col, h0, hw in pieces_cols(pieces):
                ops.append(s.op("vector", lambda v, m=m, ps=ps, col=col, h0=h0,
                                hw=hw: v.tensor_add(
                                    stream[:, m * T + h0:m * T + h0 + hw],
                                    stream[:, m * T + h0:m * T + h0 + hw],
                                    ps[:, col:col + hw]),
                                waits=waits + [p2_stream], tag=f"evwo{m}"))
            p3_stream.extend(ops)
            return ops

        stream_matmul(cx, "wo", wo_t, 16, 1024,
                      lambda kc, h0, hw: og_fm[:, kc * T + h0:kc * T + h0 + hw],
                      lambda kc: [tr_cps[kc], bar3b], evac_wo, PIECES_T, pstwo,
                      first_waits=[bar3b], rhs_cover=[tr_cps[7], bar3b])
        cx.cur = None
        p3.close()
        bar3 = _phase_barrier(cx, "b3")
        if upto == 3:
            _fin_store(cx, stream, o_out, bar3)
            s.emit(cx.stack)
            return cx.nc

        p4 = ExitStack()
        cx.cur = p4
        qx4 = cx.sb("qx4", [128, 16 * T], FP8)
        qa = cx.sb("qa4", [128, 64 * T], FP8)
        gb = cx.sb("gb4", [128, T], BF16)
        gelb = cx.sb("gelb4", [128, T], BF16)
        ub = cx.sb("ub4", [128, T], BF16)
        actb = cx.sb("actb4", [128, T], BF16)
        dscr4 = cx.sb("dscr4", [128, T], BF16)
        outst = [cx.sb(f"outst{i}", [128, T], F32) for i in range(2)]
        ln4 = ExitStack()
        bc4, bc4_ops = ln_stream_stats(cx, c, stream, bar3, ln4, "l4")
        scr4 = cx.sb("scr4l", [128, T], F32, stack=ln4)
        q_ops4 = []
        prevq = None
        for ci in range(16):
            a2 = ln_apply_sb(cx, stream, bc4, scr4,
                             qx4[:, ci * T:(ci + 1) * T], ci,
                             [bc4_ops, prevq], "x4")
            prevq = a2
            q_ops4.append(a2)
        ln4.close()
        slg0 = [cx.sb(f"slg04{i}", [128, D], FP8) for i in range(2)]
        slg1 = [cx.sb(f"slg14{i}", [128, D], FP8) for i in range(2)]
        scg = ExitStack()
        psG4 = [cx.psum(f"psG4{i}", [128, 1024], F32, stack=scg) for i in range(2)]
        psU4 = [cx.psum(f"psU4{i}", [128, 1024], F32, stack=scg) for i in range(2)]
        ev_hist4 = []
        mm_hist4 = []
        qa_ops = []
        P4d = {}
        for m in range(64):
            i = m % 2
            wprev = [mm_hist4[m - 2]] if m >= 2 else [q_ops4[-1], bar3]
            ldg = s.dma("gpsimd", slg0[i][:, :], wg0_t[m],
                        waits=wprev, tag=f"ldg04{m}")
            ldu = s.dma("gpsimd", slg1[i][:, :], wg1f_t[m],
                        waits=wprev, tag=f"ldg14{m}")
            evp = ev_hist4[m - 2] if m >= 2 else []
            lastG = lastU = None
            for ps, sl, ldx_ in ((psG4[i], slg0[i], ldg), (psU4[i], slg1[i], ldu)):
                for col, h0, hw in pieces_cols(PIECES_T):
                    for kc in range(16):
                        wl = ([ldx_, q_ops4[-1], evp]
                              if (kc == 0 and col == 0) else [])
                        last = s.op("tensor", lambda t, ps=ps, sl=sl, kc=kc,
                                    col=col, h0=h0, hw=hw: t.matmul(
                                        ps[:, col:col + hw],
                                        sl[:, kc * 128:(kc + 1) * 128],
                                        qx4[:, kc * T + h0:kc * T + h0 + hw],
                                        start=(kc == 0), stop=(kc == 15)),
                                    waits=wl, tag=f"fmm4{m}")
                if ps is psG4[i]:
                    lastG = last
                else:
                    lastU = last
            mm_hist4.append(lastU)
            e1 = s.op("vector", lambda v, i=i: v.tensor_copy(
                gb[:, :], psG4[i][:, :]), waits=[lastG, P4d.get("gb")],
                tag=f"gcp4{m}")
            e2 = s.op("scalar", lambda sc2: sc2.activation(
                gelb[:, :], gb[:, :], AF.Gelu_apprx_tanh),
                waits=[e1, P4d.get("gelb")], tag=f"gel4{m}")
            e3 = s.op("scalar", lambda sc2, i=i: sc2.copy(
                ub[:, :], psU4[i][:, :]), waits=[lastU, P4d.get("ub"), e2],
                tag=f"ucp4{m}")
            e4 = s.op("vector", lambda v: v.tensor_mul(
                actb[:, :], gelb[:, :], ub[:, :]),
                waits=[e2, e3, P4d.get("actb")], tag=f"amul4{m}")
            e5 = s.op("gpsimd", lambda g, m=m: g.tensor_copy(
                qa[:, m * T:(m + 1) * T], actb[:, :]), waits=[e4],
                tag=f"qa4{m}")
            P4d["gb"] = e2
            P4d["gelb"] = e4
            P4d["ub"] = e4
            P4d["actb"] = e5
            ev_hist4.append([e1, e3])
            qa_ops.append(e5)
        scg.close()
        pst4 = [cx.psum(f"ppf4{i}", [128, 1024], F32) for i in range(2)]
        st_ops4 = []
        P4e = {}

        def evac_dn4(m, ps, pieces, waits):
            ops = []
            for col, h0, hw in pieces_cols(pieces):
                o1 = s.op("vector", lambda v, ps=ps, col=col, hw=hw:
                          v.tensor_copy(dscr4[:, col:col + hw],
                                        ps[:, col:col + hw]),
                          waits=waits + [P4d.get("dscr")], tag=f"dcp4{m}")
                o2 = s.op("vector", lambda v, m=m, col=col, h0=h0, hw=hw:
                          v.tensor_add(
                              outst[m % 2][:, h0:h0 + hw],
                              stream[:, m * T + h0:m * T + h0 + hw],
                              dscr4[:, col:col + hw]),
                          waits=[o1, p3_stream, P4e.get(("st", m % 2))],
                          tag=f"dadd4{m}")
                ops.append(o2)
            P4d["dscr"] = ops
            st = s.dma("sync", o_out[:, m * T:(m + 1) * T], outst[m % 2][:, :],
                       waits=ops, tag=f"stf4{m}")
            P4e[("st", m % 2)] = st
            st_ops4.append(st)
            return ops

        stream_matmul(cx, "dnf", wl_t, 16, I1,
                      lambda kc, h0, hw: qa[:, kc * T + h0:kc * T + h0 + hw],
                      lambda kc: qa_ops[kc], evac_dn4, PIECES_T, pst4,
                      sb_dtype=FP8, first_waits=[qa_ops[1]],
                      psum_prev=[ev_hist4[-2] + [qa_ops[-1]], ev_hist4[-1]],
                      rhs_cover=qa_ops[-1])
        s.op("sync", lambda e: e.nop(), waits=st_ops4, tag="fin4")
        cx.cur = None
        p4.close()
        s.emit(cx.stack)
        return cx.nc


_BUILT = {}


def _get(name, fn):
    if name not in _BUILT:
        _BUILT[name] = fn()
    return _BUILT[name]


def kernel(**inputs):
    inp = {k: np.asarray(v) for k, v in inputs.items()}
    cores = list(range(NCORE))
    ranges = shard_ranges()
    x = inp["L0_attn_x"]
    xsh = prep_x_shards(x)
    Pidx = np.asarray(inp["L0_ffn_down_reorder_index"]).astype(np.int64)
    _up_idx_unused = np.asarray(inp["L0_ffn_up_reorder_index"])
    wg_p = np.ascontiguousarray(np.asarray(inp["L0_ffn_w_gate"])[Pidx, :])
    wu_p = np.ascontiguousarray(np.asarray(inp["L0_ffn_w_up"])[Pidx, :])
    wd = np.ascontiguousarray(np.asarray(inp["L0_ffn_w_down"]))
    com = dict(
        wg1t=tile_lhsT_p(np.ascontiguousarray(inp["L0_attn_Wg1"].T)),
        wg2t=tile_lhsT_p(np.ascontiguousarray(inp["L0_attn_Wg2"].T)),
        wut=tile_lhsT_p(np.ascontiguousarray(inp["L0_attn_Wu"].T)),
        wvt=tile_lhsT_p(np.ascontiguousarray(inp["L0_attn_Wv"].T)),
        bg1=col16(inp["L0_attn_bg1"]), bg2=col16(inp["L0_attn_bg2"]),
        bu=col16(inp["L0_attn_bu"]), bv=col16(inp["L0_attn_bv"]),
        cw=np.ascontiguousarray(
            inp["L0_attn_conv_w"][:, 0, :].reshape(16, 128, 3)
            .transpose(1, 0, 2).reshape(128, 48)),
        cb=col16(inp["L0_attn_conv_b"]),
        wgba=wg_p, wgbt=np.ascontiguousarray(wg_p.T),
        wuba=wu_p, wubt=np.ascontiguousarray(wu_p.T),
        wdna=wd, wdnt=np.ascontiguousarray(wd.T),
        wq_t=tile_lhsT_p(np.ascontiguousarray(inp["L1_attn_Wq"])),
        wk_t=tile_lhsT_p(np.ascontiguousarray(inp["L1_attn_Wk"])),
        wv_nat=fm_layout(np.ascontiguousarray(inp["L1_attn_Wv"])),
        wgk1p=np.ascontiguousarray(
            inp["L1_attn_Wgk1"].reshape(16, 128, 16).transpose(1, 0, 2)
            .reshape(128, 256)),
        wgk2=np.ascontiguousarray(inp["L1_attn_Wgk2"]),
        bgk2=col16(inp["L1_attn_bgk2"]),
        wg_nat=fm_layout(np.ascontiguousarray(inp["L1_attn_Wg"])),
        wnorm=np.ascontiguousarray(
            np.tile(np.asarray(inp["L1_attn_Wnorm"]), 4)[None, :]
            .astype(np.float32)),
        wo_t=tile_lhsT_p(np.ascontiguousarray(inp["L1_attn_Wo"])),
        wg0_t=tile_lhsT_p(np.ascontiguousarray(inp["L1_ffn_w_gating"][0])),
        wg1f_t=tile_lhsT_p(np.ascontiguousarray(inp["L1_ffn_w_gating"][1])),
        wl_t=tile_lhsT_p(np.ascontiguousarray(inp["L1_ffn_w_linear"])),
    )
    in_maps = []
    for c in cores:
        pmask = np.zeros((128, 4), np.float32)
        pmask[:, :c % 4] = 1.0
        sel = np.zeros((128, 4), np.float32)
        if c % 4 > 0:
            sel[:, c % 4 - 1] = 1.0
        in_maps.append(dict(com, x=xsh[c], pmask=pmask, selm=sel))
    nc = _get("fused", build_fused)
    res = run_bass_kernel_spmd(nc, in_maps, core_ids=cores)
    out = np.zeros((2, 4096, 2048), np.float32)
    for c, (b, t0, t1) in enumerate(ranges):
        out[b, t0:t1] = fm_unlayout(np.asarray(res.results[c]["out"]), T).T
    return out



# version 10 (final)
# speedup vs baseline: 1.1462x; 1.1462x over previous
import numpy as np
import ml_dtypes
import concourse.bass as bass
import concourse.mybir as mybir
from concourse.bass_utils import run_bass_kernel_spmd
from contextlib import ExitStack

F32 = mybir.dt.float32
BF16 = mybir.dt.bfloat16
FP8 = mybir.dt.float8e4
DR = mybir.MatmulPerfMode.DoubleRow
AF = mybir.ActivationFunctionType
ALU = mybir.AluOpType
AX = mybir.AxisListType

NCORE = 8
T = 1024
TW = T + 2
D = 2048
EPS = 1e-5
I0 = 5632
I1 = 8192

bf = ml_dtypes.bfloat16
f8 = ml_dtypes.float8_e4m3fn

PIECES_T = ((0, 512), (512, 512))
UV_COPY_ONLY = False
SIM1 = False


class Op:
    __slots__ = ("engine", "fn", "waits", "async_inc", "seq", "sig_idx", "tag")

    def __init__(self, engine, fn, waits, async_inc, seq, tag):
        self.engine = engine
        self.fn = fn
        self.waits = waits
        self.async_inc = async_inc
        self.seq = seq
        self.sig_idx = None
        self.tag = tag


ENGINES = ("sync", "scalar", "vector", "tensor", "gpsimd")


def _flat(waits):
    out = []
    for x in waits:
        if x is None:
            continue
        if isinstance(x, (list, tuple)):
            out.extend(_flat(x))
        else:
            out.append(x)
    return out


class Sched:
    def __init__(self, nc):
        self.nc = nc
        self.ops = {e: [] for e in ENGINES}
        self.n = 0
        self._fence_cb = None

    def op(self, engine, fn, waits=(), async_inc=None, tag=""):
        o = Op(engine, fn, _flat(waits), async_inc, self.n, tag)
        self.n += 1
        self.ops[engine].append(o)
        return o

    def dma(self, engine, out, in_, waits=(), tag="", fence=True, **kw):
        w = _flat(waits)
        if fence and self._fence_cb is not None:
            eng_ops = {}
            other = []
            for x in w:
                if x.async_inc is None and x.engine in ("vector", "scalar", "gpsimd"):
                    eng_ops.setdefault(x.engine, []).append(x)
                else:
                    other.append(x)
            for e2, grp in eng_ops.items():
                other.append(self._fence_cb(self, e2, grp))
            w = other
        return self.op(engine, lambda eng: eng.dma_start(out, in_, **kw),
                       waits=w, async_inc=16, tag=tag)

    @staticmethod
    def _needs_sig(consumer_engine, producer):
        return True

    def emit(self, stack):
        K_SLOTS = {"sync": 6, "gpsimd": 6, "scalar": 2, "vector": 2,
                   "tensor": 1}
        need = set()
        for e in ENGINES:
            for o in self.ops[e]:
                if o.async_inc is not None:
                    need.add(o)
                for w in o.waits:
                    if self._needs_sig(e, w):
                        need.add(w)
        sig_total = {}
        self._slot = {}
        for e in ENGINES:
            cnt = {"c": 0}
            for k in range(K_SLOTS[e]):
                cnt[("a", k)] = 0
            aidx = 0
            cc_i = 0
            for o in self.ops[e]:
                if o in need:
                    if o.async_inc is not None:
                        if o.async_inc == 16:
                            sl = aidx % K_SLOTS[e]
                            aidx += 1
                        else:
                            sl = f"cc{cc_i}"
                            cc_i += 1
                            cnt.setdefault(("a", sl), 0)
                        self._slot[o] = sl
                        cnt[("a", sl)] += o.async_inc
                        o.sig_idx = cnt[("a", sl)]
                    else:
                        cnt["c"] += 1
                        o.sig_idx = cnt["c"]
            sig_total[(e, "c")] = cnt["c"]
            for k, v in cnt.items():
                if isinstance(k, tuple) and k[0] == "a":
                    sig_total[(e, "a", k[1])] = v
        idx = {e: 0 for e in ENGINES}
        done = set()
        total = sum(len(v) for v in self.ops.values())
        progressed = True
        while progressed:
            progressed = False
            for e in ENGINES:
                while idx[e] < len(self.ops[e]):
                    o = self.ops[e][idx[e]]
                    ok = all(w in done for w in o.waits)
                    if ok:
                        done.add(o)
                        idx[e] += 1
                        progressed = True
                    else:
                        break
        if len(done) != total:
            stuck = {e: (idx[e], len(self.ops[e]),
                         self.ops[e][idx[e]].tag if idx[e] < len(self.ops[e]) else "")
                     for e in ENGINES}
            raise RuntimeError(f"Sched deadlock: {stuck}")
        sems = {k: stack.enter_context(
                    self.nc.semaphore("sch_" + "_".join(str(y) for y in k)))
                for k in sig_total if sig_total[k] > 0}
        block = stack.enter_context(self.nc.Block())

        def make_body(e):
            def body(eng):
                cum = {}
                for o in self.ops[e]:
                    byeng = {}
                    for w in o.waits:
                        if self._needs_sig(e, w):
                            if w.async_inc is not None:
                                k = (w.engine, "a", self._slot[w])
                            else:
                                k = (w.engine, "c")
                            byeng[k] = max(byeng.get(k, 0), w.sig_idx)
                    if o.sig_idx is not None and o.async_inc is not None:
                        sl = self._slot[o]
                        k = (e, "a", sl)
                        byeng[k] = max(byeng.get(k, 0), cum.get(sl, 0))
                    for pe, v in byeng.items():
                        if v > 0:
                            eng.wait_ge(sems[pe], v)
                    inst = o.fn(eng)
                    if o.sig_idx is not None:
                        if o.async_inc is not None:
                            sl = self._slot[o]
                            inst.then_inc(sems[(e, "a", sl)], o.async_inc)
                            cum[sl] = o.sig_idx
                        else:
                            inst.then_inc(sems[(e, "c")], 1)
            return body

        for e in ENGINES:
            if self.ops[e]:
                getattr(block, e)(make_body(e))


class Ctx:
    def __init__(self):
        self.nc = bass.Bass("TRN2", target_bir_lowering=False)
        self.s = Sched(self.nc)
        self.stack = ExitStack()
        self._fence_sb = None

        self._last_fence = {}

        def fence_cb(s, engine, grp):
            if self._fence_sb is None:
                self._fence_sb = self.sb("fence_sb", [1, 16], F32)
            t = self._fence_sb
            grp = list(grp) + [self._last_fence.get(engine)]
            if engine == "scalar":
                o = s.op("scalar", lambda e: e.memzero(t[0:1, 0:4]),
                         waits=grp, tag="fenceS")
            elif engine == "gpsimd":
                o = s.op("gpsimd", lambda e: e.memset(t[0:1, 4:8], 0.0),
                         waits=grp, tag="fenceG")
            else:
                o = s.op("vector", lambda e: e.memset(t[0:1, 8:12], 0.0),
                         waits=grp, tag="fenceV")
            self._last_fence[engine] = o
            return o

        self.s._fence_cb = fence_cb

    def inp(self, name, shape, dtype=F32):
        return self.nc.dram_tensor(name, list(shape), dtype, kind="ExternalInput").ap()

    def out(self, name, shape, dtype=F32):
        return self.nc.dram_tensor(name, list(shape), dtype, kind="ExternalOutput").ap()

    cur = None

    def sb(self, name, shape, dtype=F32, stack=None):
        return (stack or self.cur or self.stack).enter_context(
            self.nc.sbuf_tensor(name, list(shape), dtype))

    def psum(self, name, shape, dtype=F32, stack=None):
        return (stack or self.cur or self.stack).enter_context(
            self.nc.psum_tensor(name, list(shape), dtype))

    def const_sb(self, name, arr):
        dt_map = {np.dtype(np.float32): F32, np.dtype(bf): BF16, np.dtype(f8): FP8}
        t = self.sb(name, arr.shape, dt_map[arr.dtype])
        dram = self.nc.inline_tensor(np.ascontiguousarray(arr), name=name + "_c").ap()
        op = self.s.dma("sync", t[:, :], dram, tag=f"ld_{name}")
        return t, op

    def smalls(self, specs):
        tiles, lds = {}, []
        for nm, ap_ in specs:
            t = self.sb(nm + "_sb", list(ap_.tensor.shape), F32)
            tiles[nm] = t
            lds.append(self.s.dma("sync", t[:, :], ap_, tag=f"ld{nm}"))
        return tiles, lds


def std_consts(cx, ident=False, mask128=False):
    c = {}
    ld = []
    if cx._fence_sb is None:
        cx._fence_sb = cx.sb("fence_sb", [1, 16], F32)
    t, o = cx.const_sb("ones_col", np.ones((128, 1), dtype=bf))
    c["ones_col"] = t; ld.append(o)
    t, o = cx.const_sb("ones_row", np.ones((1, 128), dtype=np.float32))
    c["ones_row"] = t; ld.append(o)
    t, o = cx.const_sb("onesrb", np.ones((1, 128), dtype=bf))
    c["ones_row_b"] = t; ld.append(o)
    t, o = cx.const_sb("eps_col", np.full((128, 1), EPS, dtype=np.float32))
    c["eps_col"] = t; ld.append(o)
    if ident:
        t, o = cx.const_sb("identb", np.eye(128, dtype=bf))
        c["ident"] = t; ld.append(o)
        t, o = cx.const_sb("identf2", np.eye(128, dtype=np.float32))
        c["identf"] = t; ld.append(o)
    if mask128:
        m = np.ones((128, T), dtype=bf)
        m[:, ::128] = 0
        t, o = cx.const_sb("mask128", m)
        c["mask128"] = t; ld.append(o)
    c["ld"] = ld
    return c


def build_ln(cx, c, xsb, xbsb, ntok, x_ready, pieces, scope, tagp=""):
    s = cx.s
    brow = cx.sb(f"brow{tagp}", [1, 4 * ntok], F32, stack=scope)
    bc = cx.sb(f"bc{tagp}", [128, 2 * ntok], F32, stack=scope)
    sq = cx.sb(f"sq{tagp}", [128, ntok], BF16, stack=scope)
    scope2 = ExitStack()
    stat_a = cx.psum(f"statA{tagp}", [1, ntok], F32, stack=scope2)
    stat_b = cx.psum(f"statB{tagp}", [1, ntok], F32, stack=scope2)

    cast_ops = []
    mm_sum = []
    mm_sq_last = None
    prev_sq_mm = None
    for ci in range(16):
        xc = xsb[:, ci * ntok:(ci + 1) * ntok]
        xbc = xbsb[:, ci * ntok:(ci + 1) * ntok]
        cop = s.op("vector", lambda v, a=xbc, b=xc: v.tensor_copy(a, b),
                   waits=[x_ready], tag=f"lncast{tagp}{ci}")
        cast_ops.append(cop)
        sop = s.op("scalar", lambda sc, b=xbc: sc.activation(sq[:, :], b, AF.Square),
                   waits=[cop, prev_sq_mm], tag=f"lnsq{tagp}{ci}")
        for h0, hw in pieces:
            mm_sum.append(s.op("tensor", lambda t, ci=ci, h0=h0, hw=hw, xbc=xbc:
                               t.matmul(stat_a[0:1, h0:h0 + hw],
                                        c["ones_col"][:, :], xbc[:, h0:h0 + hw],
                                        start=(ci == 0), stop=(ci == 15)),
                               waits=[cop, c["ld"]], tag=f"lnmmA{tagp}"))
        for h0, hw in pieces:
            prev_sq_mm = s.op("tensor", lambda t, ci=ci, h0=h0, hw=hw: t.matmul(
                stat_b[0:1, h0:h0 + hw], c["ones_col"][:, :], sq[:, h0:h0 + hw],
                start=(ci == 0), stop=(ci == 15)), waits=[sop], tag=f"lnmmB{tagp}")
        mm_sq_last = prev_sq_mm
    inv_d = 1.0 / D
    MEAN = brow[0:1, 0:ntok]
    RSTD = brow[0:1, ntok:2 * ntok]
    MB = brow[0:1, 2 * ntok:3 * ntok]
    SCR = brow[0:1, 3 * ntok:4 * ntok]
    o_mean = s.op("vector", lambda v: v.tensor_scalar_mul(
        MEAN, stat_a[0:1, :], inv_d), waits=[mm_sum[-1]], tag=f"mean{tagp}")
    o_m2 = s.op("scalar", lambda sc: sc.activation(
        SCR, MEAN, AF.Square), waits=[o_mean], tag=f"m2{tagp}")
    o_var = s.op("vector", lambda v: v.scalar_tensor_tensor(
        RSTD, stat_b[0:1, :], inv_d, SCR,
        ALU.mult, ALU.subtract), waits=[o_m2, mm_sq_last], tag=f"var{tagp}")
    scope2.close()
    bc_ps = cx.psum(f"bcps{tagp}", [128, ntok], F32, stack=scope)
    o_sqrt = s.op("scalar", lambda sc: sc.activation(
        RSTD, RSTD, AF.Sqrt, bias=c["eps_col"][0:1, :]),
        waits=[o_var], tag=f"sqrt{tagp}")
    o_rstd = s.op("vector", lambda v: v.reciprocal(
        RSTD, RSTD), waits=[o_sqrt], tag=f"rstd{tagp}")
    o_mb = s.op("vector", lambda v: v.scalar_tensor_tensor(
        MB, MEAN, -1.0, RSTD,
        ALU.mult, ALU.mult), waits=[o_rstd], tag=f"mb{tagp}")
    bc1mm = [s.op("tensor", lambda t, h0=h0, hw=hw: t.matmul(
        bc_ps[:, h0:h0 + hw], c["ones_row"][0:1, :],
        brow[0:1, ntok + h0:ntok + h0 + hw],
        start=True, stop=True), waits=[o_rstd, mm_sq_last], tag=f"bc1mm{tagp}")
        for h0, hw in pieces]
    cp1 = s.op("vector", lambda v: v.tensor_copy(bc[:, 0:ntok], bc_ps[:, :]),
               waits=bc1mm, tag=f"bccp1{tagp}")
    bc2mm = [s.op("tensor", lambda t, h0=h0, hw=hw: t.matmul(
        bc_ps[:, h0:h0 + hw], c["ones_row"][0:1, :],
        brow[0:1, 2 * ntok + h0:2 * ntok + h0 + hw],
        start=True, stop=True), waits=[o_mb, cp1], tag=f"bc2mm{tagp}")
        for h0, hw in pieces]
    cp2 = s.op("vector", lambda v: v.tensor_copy(bc[:, ntok:], bc_ps[:, :]),
               waits=bc2mm, tag=f"bccp2{tagp}")
    return bc, [cp1, cp2]


def ln_full(cx, c, xsb, xbsb, ntok, x_ready, pieces, tagp=""):
    scr = cx.sb(f"lnscr{tagp}", [128, ntok], F32)
    scope = ExitStack()
    bc, bc_ops = build_ln(cx, c, xsb, xbsb, ntok, x_ready, pieces, scope, tagp)
    apps = []
    prev = None
    for ci in range(16):
        xc = xsb[:, ci * ntok:(ci + 1) * ntok]
        oc = xbsb[:, ci * ntok:(ci + 1) * ntok]
        o1 = cx.s.op("vector", lambda v, xc=xc: v.tensor_mul(
            scr[:, :], xc, bc[:, 0:ntok]), waits=[bc_ops, prev],
            tag=f"app1{tagp}{ci}")
        o2 = cx.s.op("vector", lambda v, oc=oc: v.tensor_add(
            oc, scr[:, :], bc[:, ntok:]), waits=[o1], tag=f"app2{tagp}{ci}")
        apps.append(o2)
        prev = o2
    scope.close()
    return apps


def stream_matmul(cx, name, dram_tiled, nslab, K, rhs_fn, rhs_ready, evac_fn,
                  pieces, pst, sb_dtype=BF16, psum_prev=None, slab_pair=None,
                  first_waits=None, rhs_cover=None, pair_rhs3=None):
    s = cx.s
    nkc = K // 128
    if slab_pair is None:
        slab_pair = [cx.sb(f"{name}_sl{i}", [128, K], sb_dtype) for i in range(2)]
    if isinstance(slab_pair, dict):
        sp_tiles, sp_last = slab_pair["tiles"], slab_pair["lastmm"]
    else:
        sp_tiles, sp_last = slab_pair, [None, None]
    cast = dram_tiled.tensor.dtype != sb_dtype
    evacs = []
    for m in range(nslab):
        i = m % 2
        sl = sp_tiles[i]
        w = [sp_last[i]] if sp_last[i] is not None else (
            [first_waits] if first_waits else [])
        if dram_tiled.tensor.shape[1] == 128 and K != 128:
            ld = s.dma("gpsimd" if cast else "sync", sl[:, :], dram_tiled[m],
                       waits=w, tag=f"{name}_ld{m}")
        else:
            ld = s.dma("gpsimd" if cast else "sync",
                       sl[:, :].rearrange("p (kc m) -> p kc m", kc=nkc),
                       dram_tiled[m].rearrange("(kc p) m -> p kc m", p=128),
                       waits=w, tag=f"{name}_ld{m}")
        ps = pst[i]
        ev_prev = evacs[m - 2] if m >= 2 else (psum_prev[i] if psum_prev else [])
        col = 0
        mlast = None
        if pair_rhs3 is not None:
            slr = sl[:, :].rearrange("p (c n) -> p c n", c=nkc)
            for h0, hw in pieces:
                for kp in range(nkc // 2):
                    if rhs_cover is not None:
                        wl = ([ld, rhs_cover, ev_prev]
                              if (kp == 0 and col == 0) else [])
                    else:
                        wl = [ld, rhs_ready(2 * kp + 1)]
                        if kp == 0 and col == 0:
                            wl.append(ev_prev)
                    mlast = s.op("tensor", lambda t, kp=kp, h0=h0, hw=hw,
                                 col=col, slr=slr, ps=ps: t.matmul(
                                     ps[:, col:col + hw],
                                     slr[:, 2 * kp:2 * kp + 2, :],
                                     pair_rhs3(kp, h0, hw), start=(kp == 0),
                                     stop=(kp == nkc // 2 - 1),
                                     perf_mode=DR),
                                 waits=wl, tag=f"{name}_mm{m}")
                col += hw
            sp_last[i] = mlast
            evacs.append(_flat([evac_fn(m, ps, pieces, [mlast])]))
            continue
        for h0, hw in pieces:
            for kc in range(nkc):
                if rhs_cover is not None:
                    wl = ([ld, rhs_cover, ev_prev]
                          if (kc == 0 and col == 0) else [])
                else:
                    wl = [ld, rhs_ready(kc)]
                    if kc == 0 and col == 0:
                        wl.append(ev_prev)
                mlast = s.op("tensor", lambda t, kc=kc, h0=h0, hw=hw, col=col,
                             sl=sl, ps=ps: t.matmul(
                                 ps[:, col:col + hw],
                                 sl[:, kc * 128:(kc + 1) * 128],
                                 rhs_fn(kc, h0, hw), start=(kc == 0),
                                 stop=(kc == nkc - 1)),
                             waits=wl, tag=f"{name}_mm{m}")
            col += hw
        sp_last[i] = mlast
        evacs.append(_flat([evac_fn(m, ps, pieces, [mlast])]))
    return evacs


def pieces_cols(pieces):
    cols = []
    col = 0
    for h0, hw in pieces:
        cols.append((col, h0, hw))
        col += hw
    return cols


def tile_lhsT(w_eff):
    K, M = w_eff.shape
    assert M % 128 == 0 and K % 128 == 0
    return np.ascontiguousarray(w_eff.reshape(K, M // 128, 128).transpose(1, 0, 2))


def tile_lhsT_p(w_eff):
    K, M = w_eff.shape
    assert M % 128 == 0 and K % 128 == 0
    a = w_eff.reshape(K // 128, 128, M // 128, 128)
    return np.ascontiguousarray(
        a.transpose(2, 1, 0, 3).reshape(M // 128, 128, K))


def col16(v):
    return np.ascontiguousarray(v.reshape(-1, 128).T)


def fm_layout(xt):
    n = xt.shape[1]
    return np.ascontiguousarray(
        xt.reshape(16, 128, n).transpose(1, 0, 2).reshape(128, 16 * n))


def fm_unlayout(a, n):
    return np.ascontiguousarray(
        a.reshape(128, 16, n).transpose(1, 0, 2).reshape(2048, n))


def shard_ranges():
    return [(c // 4, (c % 4) * T, (c % 4) * T + T) for c in range(NCORE)]


def prep_x_shards(x):
    shards = []
    for b, t0, t1 in shard_ranges():
        ext = np.zeros((TW, D), dtype=np.float32)
        lo = max(t0 - 1, 0)
        hi = min(t1 + 1, 4096)
        ext[1 - (t0 - lo):1 + (hi - t0)] = x[b, lo:hi]
        shards.append(fm_layout(np.ascontiguousarray(ext.T)))
    return shards


def build_launch_q(gate_rows, down_rows):
    cx = Ctx()
    s = cx.s
    wg = cx.inp("wg", [gate_rows, D])
    wu = cx.inp("wu", [gate_rows, D])
    wd = cx.inp("wd", [down_rows, I0])
    qg = cx.out("qg", [gate_rows, D], FP8)
    qu = cx.out("qu", [gate_rows, D], FP8)
    qd = cx.out("qd", [down_rows, I0], FP8)
    wsg = cx.out("wsg", [gate_rows, 1])
    wsu = cx.out("wsu", [gate_rows, 1])
    wsd = cx.out("wsd", [down_rows, 1])

    with cx.stack:
        def quant_one(tag, src, qdst, wsdst, rows, cols):
            wt = [cx.sb(f"wt_{tag}{i}", [128, cols], F32) for i in range(2)]
            qt = [cx.sb(f"qt_{tag}{i}", [128, cols], FP8) for i in range(2)]
            mx = cx.sb(f"mx_{tag}", [128, 8], F32)
            prev = {}
            for g in range((rows + 127) // 128):
                r0 = g * 128
                nr = min(128, rows - r0)
                i = g % 2
                ld = s.dma("sync", wt[i][0:nr, :], src[r0:r0 + nr, :],
                           waits=[prev.get(("q", i))], tag=f"q{tag}ld{g}")
                red = s.op("vector", lambda v, nr=nr, i=i: v.reduce_max(
                    mx[0:nr, 4 * i:4 * i + 1], wt[i][0:nr, :], AX.X,
                    apply_absolute_value=True),
                    waits=[ld, prev.get(("s2", i))], tag=f"q{tag}red{g}")
                rec = s.op("vector", lambda v, nr=nr, i=i: v.reciprocal(
                    mx[0:nr, 4 * i + 1:4 * i + 2], mx[0:nr, 4 * i:4 * i + 1]),
                    waits=[red], tag=f"q{tag}rec{g}")
                scl = s.op("vector", lambda v, nr=nr, i=i: v.tensor_scalar_mul(
                    mx[0:nr, 4 * i + 2:4 * i + 3],
                    mx[0:nr, 4 * i + 1:4 * i + 2], 224.0),
                    waits=[rec], tag=f"q{tag}scl{g}")
                wsv = s.op("vector", lambda v, nr=nr, i=i: v.tensor_scalar_mul(
                    mx[0:nr, 4 * i + 3:4 * i + 4], mx[0:nr, 4 * i:4 * i + 1],
                    1.0 / 224.0), waits=[red], tag=f"q{tag}wsv{g}")
                qop = s.op("scalar", lambda sc, nr=nr, i=i: sc.activation(
                    qt[i][0:nr, :], wt[i][0:nr, :], AF.Copy,
                    scale=mx[0:nr, 4 * i + 2:4 * i + 3]),
                    waits=[scl, prev.get(("s1", i))], tag=f"q{tag}q{g}")
                st1 = s.dma("sync", qdst[r0:r0 + nr, :], qt[i][0:nr, :],
                            waits=[qop], tag=f"q{tag}st1{g}")
                st2 = s.dma("sync", wsdst[r0:r0 + nr, :],
                            mx[0:nr, 4 * i + 3:4 * i + 4], waits=[wsv],
                            tag=f"q{tag}st2{g}")
                prev[("q", i)] = qop
                prev[("s1", i)] = st1
                prev[("s2", i)] = st2

        quant_one("g", wg, qg, wsg, gate_rows, D)
        quant_one("u", wu, qu, wsu, gate_rows, D)
        quant_one("d", wd, qd, wsd, down_rows, I0)
        s.emit(cx.stack)
    return cx.nc


def build_launch_a():
    cx = Ctx()
    s = cx.s
    x_in = cx.inp("x", [128, 16 * TW])
    wg1t = cx.inp("wg1t", [4, D, 128])
    wg2t = cx.inp("wg2t", [16, 512, 128])
    wut = cx.inp("wut", [16, D, 128])
    bg1 = cx.inp("bg1", [128, 4])
    bg2 = cx.inp("bg2", [128, 16])
    bu = cx.inp("bu", [128, 16])
    cw = cx.inp("cw", [128, 48])
    cb = cx.inp("cb", [128, 16])
    o_stream = cx.out("stream1p", [128, 16 * T])
    o_xb = cx.out("xbo", [128, 16 * T], BF16)
    o_gates = cx.out("gateso", [128, 16 * T], BF16)
    o_ug = cx.out("ugo", [128, 16 * T], BF16)
    o_tot = cx.out("toto", [128, 16])

    with cx.stack:
        c = std_consts(cx)
        xsb = cx.sb("xsb", [128, 16 * TW], F32)
        xb = cx.sb("xbl", [128, 16 * TW], BF16)
        hb = cx.sb("hb", [128, 4 * T], BF16)
        gates = cx.sb("gat", [128, 16 * T], BF16)
        ugs = [cx.sb(f"ug{i}", [128, T], BF16) for i in range(2)]
        tot = cx.sb("tot_sb", [128, 16], F32)
        sm, small_ld = cx.smalls([("bg1", bg1), ("bg2", bg2), ("bu", bu),
                                  ("cw", cw), ("cb", cb)])
        ldx = s.dma("sync", xsb[:, :], x_in, tag="ldx")

        pieces_tw = ((0, 512), (512, 512), (1024, 2))
        apps = ln_full(cx, c, xsb, xb, TW, [ldx, c["ld"]], pieces_tw, tagp="a")

        t32 = cx.sb("t32", [128, T], F32)
        conv_done = []
        prev = None
        for ci in range(16):
            xc = xb[:, ci * TW:(ci + 1) * TW]
            sc_ = xsb[:, ci * TW:(ci + 1) * TW]
            o1 = s.op("vector", lambda v, xc=xc, ci=ci: v.tensor_scalar_mul(
                t32[:, :], xc[:, 0:T], sm["cw"][:, 3 * ci:3 * ci + 1]),
                waits=[apps[ci], prev, small_ld], tag=f"cv1_{ci}")
            o2 = s.op("vector", lambda v, xc=xc, ci=ci: v.scalar_tensor_tensor(
                t32[:, :], xc[:, 1:1 + T], sm["cw"][:, 3 * ci + 1:3 * ci + 2],
                t32[:, :], ALU.mult, ALU.add), waits=[o1], tag=f"cv2_{ci}")
            o3 = s.op("vector", lambda v, xc=xc, ci=ci: v.scalar_tensor_tensor(
                t32[:, :], xc[:, 2:2 + T], sm["cw"][:, 3 * ci + 2:3 * ci + 3],
                t32[:, :], ALU.mult, ALU.add), waits=[o2], tag=f"cv3_{ci}")
            o4 = s.op("vector", lambda v, sc_=sc_, ci=ci: v.scalar_tensor_tensor(
                sc_[:, 1:1 + T], t32[:, :], sm["cb"][:, ci:ci + 1],
                sc_[:, 1:1 + T], ALU.add, ALU.add), waits=[o3], tag=f"cv4_{ci}")
            prev = o4
            conv_done.append(o4)

        pst = [cx.psum(f"pp{i}", [128, 1024], F32) for i in range(2)]

        def rhs_x(kc, h0, hw):
            return xb[:, kc * TW + 1 + h0: kc * TW + 1 + h0 + hw]

        def rhs_x_ready(kc):
            return apps[kc]

        def evac_h(m, ps, pieces, waits):
            ops = []
            for col, h0, hw in pieces_cols(pieces):
                ops.append(s.op("scalar", lambda sc, m=m, ps=ps, col=col, h0=h0,
                                hw=hw: sc.activation(
                                    hb[:, m * T + h0:m * T + h0 + hw],
                                    ps[:, col:col + hw], AF.Relu,
                                    bias=sm["bg1"][:, m:m + 1]),
                                waits=waits + [small_ld], tag=f"evh{m}"))
            return ops

        ev_h = stream_matmul(cx, "h", wg1t, 4, D, rhs_x, rhs_x_ready, evac_h,
                             PIECES_T, pst, first_waits=apps)

        def rhs_h(kc, h0, hw):
            return hb[:, kc * T + h0:kc * T + h0 + hw]

        def evac_g(m, ps, pieces, waits):
            ops = []
            for col, h0, hw in pieces_cols(pieces):
                ops.append(s.op("scalar", lambda sc, m=m, ps=ps, col=col, h0=h0,
                                hw=hw: sc.activation(
                                    gates[:, m * T + h0:m * T + h0 + hw],
                                    ps[:, col:col + hw], AF.Sigmoid,
                                    bias=sm["bg2"][:, m:m + 1]),
                                waits=waits, tag=f"evg{m}"))
            return ops

        ev_g = stream_matmul(cx, "gt", wg2t, 16, 512, rhs_h,
                             lambda kc: ev_h[kc], evac_g, PIECES_T, pst,
                             psum_prev=[ev_h[2], ev_h[3]])

        ug_stores = []

        def evac_u(m, ps, pieces, waits):
            i = m % 2
            prev_store = ug_stores[m - 2] if m >= 2 else None
            ops = []
            for col, h0, hw in pieces_cols(pieces):
                ops.append(s.op("vector", lambda v, m=m, ps=ps, col=col, h0=h0,
                                hw=hw, i=i: v.scalar_tensor_tensor(
                                    ugs[i][:, h0:h0 + hw], ps[:, col:col + hw],
                                    sm["bu"][:, m:m + 1],
                                    gates[:, m * T + h0:m * T + h0 + hw],
                                    ALU.add, ALU.mult),
                                waits=waits + [ev_g[m], prev_store, small_ld],
                                tag=f"evu{m}"))
            red = s.op("vector", lambda v, m=m, i=i: v.reduce_sum(
                tot[:, m:m + 1], ugs[i][:, :], AX.X), waits=ops, tag=f"tot{m}")
            st = s.dma("sync", o_ug[:, m * T:(m + 1) * T], ugs[i][:, :],
                       waits=[red], tag=f"stug{m}")
            ug_stores.append(st)
            return ops + [red]

        stream_matmul(cx, "u", wut, 16, D, rhs_x, rhs_x_ready, evac_u,
                      PIECES_T, pst, psum_prev=[ev_g[14], ev_g[15]])

        st1 = s.dma("sync",
                    o_stream[:, :].rearrange("p (c t) -> p c t", c=16),
                    xsb[:, :].rearrange("p (c t) -> p c t", c=16)[:, :, 1:1 + T],
                    waits=conv_done, tag="st_stream")
        st2 = s.dma("sync",
                    o_xb[:, :].rearrange("p (c t) -> p c t", c=16),
                    xb[:, :].rearrange("p (c t) -> p c t", c=16)[:, :, 1:1 + T],
                    waits=apps, tag="st_xb")
        st3 = s.dma("sync", o_gates, gates[:, :], waits=ev_g, tag="st_gates")
        st4 = s.dma("sync", o_tot, tot[:, :], waits=ug_stores, tag="st_tot")
        s.op("sync", lambda e: e.nop(), waits=[st1, st2, st3, st4] + ug_stores,
             tag="fin")
        s.emit(cx.stack)
    return cx.nc


def build_launch_b():
    cx = Ctx()
    s = cx.s
    streamp = cx.inp("streamp", [128, 16 * T])
    xb_in = cx.inp("xbi", [128, 16 * T], BF16)
    gates_in = cx.inp("gatesi", [128, 16 * T], BF16)
    ug_in = cx.inp("ugi", [128, 16 * T], BF16)
    totall = cx.inp("totall", [512, 16])
    pmask = cx.inp("pmask", [128, 4])
    wvt = cx.inp("wvt", [16, D, 128])
    wut = cx.inp("wut", [16, D, 128])
    bv = cx.inp("bv", [128, 16])
    o_stream = cx.out("stream1", [128, 16 * T])
    o_go = cx.out("godbg", [128, 16 * T], BF16)
    o_kc = cx.out("kcdbg", [128, T], BF16)
    o_vg = cx.out("vgdbg", [128, T], BF16)
    o_pfx = cx.out("pfxdbg", [128, 16])

    with cx.stack:
        c = std_consts(cx)
        xsb = cx.sb("xsb", [128, 16 * T], F32)
        xb = cx.sb("xbl", [128, 16 * T], BF16)
        gates = cx.sb("gat", [128, 16 * T], BF16)
        ug = cx.sb("ug", [128, 16 * T], BF16)
        go = gates
        vg = cx.sb("vg", [128, T], BF16)
        kc_t = cx.sb("kc_t", [128, T], BF16)
        zero_b = cx.sb("zerob", [128, T], BF16)
        tots = cx.sb("tots", [128, 64], F32)
        pfx = cx.sb("pfx", [128, 16], F32)
        pm = cx.sb("pm", [128, 4], F32)
        sm, small_ld = cx.smalls([("bv", bv)])

        ld_s = s.dma("sync", xsb[:, :], streamp, tag="ldS")
        ld_x = s.dma("sync", xb[:, :], xb_in, tag="ldxb")
        ld_g = s.dma("sync", gates[:, :], gates_in, tag="ldg")
        ld_u = s.dma("sync", ug[:, :], ug_in, tag="ldug")
        ld_t = s.dma("sync", tots[:, :].rearrange("p (g m) -> p g m", g=4),
                     totall.rearrange("(g p) m -> p g m", p=128), tag="ldtot")
        ld_p = s.dma("sync", pm[:, :], pmask, tag="ldpm")
        z0 = s.op("vector", lambda v: v.memset(zero_b[:, :], 0.0), tag="z0")

        pfx_ops = []
        o = s.op("vector", lambda v: v.memset(pfx[:, :], 0.0), tag="pfx0")
        for j in range(4):
            o = s.op("vector", lambda v, j=j, o=o: v.scalar_tensor_tensor(
                pfx[:, :], tots[:, j * 16:(j + 1) * 16], pm[:, j:j + 1],
                pfx[:, :], ALU.mult, ALU.add),
                waits=[ld_t, ld_p, o], tag=f"pfx{j}")
        pfx_ops.append(o)

        pst = [cx.psum(f"pp{i}", [128, 1024], F32) for i in range(2)]

        go_ops = [None] * 16

        def rhs_x(kc, h0, hw):
            return xb[:, kc * T + h0: kc * T + h0 + hw]

        def evac_v(m, ps, pieces, waits):
            ops = []
            for col, h0, hw in pieces_cols(pieces):
                ops.append(s.op("vector", lambda v, m=m, ps=ps, col=col, h0=h0,
                                hw=hw: v.scalar_tensor_tensor(
                                    vg[:, h0:h0 + hw], ps[:, col:col + hw],
                                    sm["bv"][:, m:m + 1],
                                    gates[:, m * T + h0:m * T + h0 + hw],
                                    ALU.add, ALU.mult),
                                waits=waits + [small_ld, ld_g, go_ops[m - 1] if m >= 1 else None],
                                tag=f"evv{m}"))
            scn = s.op("vector", lambda v, m=m: v.tensor_tensor_scan(
                kc_t[:, :], ug[:, m * T:(m + 1) * T], zero_b[:, :],
                0.0, ALU.add, ALU.add),
                waits=[ld_u, z0, pfx_ops, ops, go_ops[m - 1] if m >= 1 else None],
                tag=f"scan{m}")
            scn2 = s.op("vector", lambda v, m=m: v.tensor_scalar_add(
                kc_t[:, :], kc_t[:, :], pfx[:, m:m + 1]), waits=[scn],
                tag=f"scanp{m}")
            gop = s.op("vector", lambda v, m=m: v.tensor_mul(
                go[:, m * T:(m + 1) * T], kc_t[:, :], vg[:, :]),
                waits=[scn2], tag=f"go{m}")
            go_ops[m] = gop
            return ops + [scn, gop]

        ev_v = stream_matmul(cx, "v", wvt, 16, D, rhs_x, lambda kc: ld_x,
                             evac_v, PIECES_T, pst)
        sdb = [s.dma("sync", o_go, go[:, :], waits=go_ops, tag="dbgo"),
               s.dma("sync", o_kc, kc_t[:, :], waits=go_ops, tag="dbkc"),
               s.dma("sync", o_vg, vg[:, :], waits=go_ops, tag="dbvg"),
               s.dma("sync", o_pfx, pfx[:, :], waits=go_ops, tag="dbpf")]

        st_ops = []

        def evac_uv(m, ps, pieces, waits):
            ops = []
            for col, h0, hw in pieces_cols(pieces):
                if UV_COPY_ONLY:
                    ops.append(s.op("vector", lambda v, m=m, ps=ps, col=col,
                                    h0=h0, hw=hw: v.tensor_copy(
                                        xsb[:, m * T + h0:m * T + h0 + hw],
                                        ps[:, col:col + hw]),
                                    waits=waits + [ld_s], tag=f"evuv{m}"))
                else:
                    ops.append(s.op("vector", lambda v, m=m, ps=ps, col=col,
                                    h0=h0, hw=hw: v.tensor_add(
                                        xsb[:, m * T + h0:m * T + h0 + hw],
                                        xsb[:, m * T + h0:m * T + h0 + hw],
                                        ps[:, col:col + hw]),
                                    waits=waits + [ld_s], tag=f"evuv{m}"))

            st_ops.append(s.dma("sync", o_stream[:, m * T:(m + 1) * T],
                                xsb[:, m * T:(m + 1) * T], waits=ops,
                                tag=f"stuv{m}"))
            return ops

        stream_matmul(cx, "uv", wut, 16, D,
                      lambda kc, h0, hw: go[:, kc * T + h0:kc * T + h0 + hw],
                      lambda kc: go_ops[kc], evac_uv, PIECES_T, pst,
                      psum_prev=[ev_v[14], ev_v[15]])

        s.op("sync", lambda e: e.nop(), waits=st_ops + sdb, tag="fin")
        s.emit(cx.stack)
    return cx.nc




def rowmax_to_bcs(cx, c, M, m_ready, identf, tagp="", ntok=T, pre=None):
    s = cx.s
    nt = ntok // 128
    scope = ExitStack()
    if pre is None:
        tm = cx.sb(f"rm_tm{tagp}", [128, 4 * nt], F32)
        s448 = cx.sb(f"s448bc{tagp}", [128, ntok], F32)
        r448 = cx.sb(f"r448bc{tagp}", [128, ntok], F32)
    else:
        tm, s448, r448 = pre
    rowsb = cx.sb(f"rmrow{tagp}", [1, 2 * ntok], F32, stack=scope)
    pst = cx.psum(f"rm_ps{tagp}", [128, 128], F32, stack=scope)
    psr = cx.psum(f"rm_psr{tagp}", [1, 1024], F32, stack=scope)
    psb = cx.psum(f"rm_psb{tagp}", [128, 512], F32, stack=scope)
    prev_red = None
    tm_ops = []
    for tc in range(nt):
        tr = s.op("tensor", lambda t, tc=tc: t.transpose(
            pst[:, :], M[:, tc * 128:(tc + 1) * 128], identf[:, :]),
            waits=[m_ready, prev_red], tag=f"rmT{tagp}{tc}")
        red = s.op("vector", lambda v, tc=tc: v.reduce_max(
            tm[:, tc:tc + 1], pst[:, :], AX.X), waits=[tr], tag=f"rmR{tagp}{tc}")
        prev_red = red
        tm_ops.append(red)
    o_rec = s.op("vector", lambda v: v.reciprocal(tm[:, nt:2 * nt], tm[:, 0:nt]),
                 waits=tm_ops, tag=f"rmrec{tagp}")
    o_s448 = s.op("vector", lambda v: v.tensor_scalar_mul(
        tm[:, 2 * nt:3 * nt], tm[:, nt:2 * nt], 224.0), waits=[o_rec],
        tag=f"rms{tagp}")
    o_r448 = s.op("vector", lambda v: v.tensor_scalar_mul(
        tm[:, 3 * nt:4 * nt], tm[:, 0:nt], 1.0 / 224.0), waits=tm_ops,
        tag=f"rmr{tagp}")
    prev_cp = None
    row_ops = []
    for which, (src_c, dst0) in enumerate(((2 * nt, 0), (3 * nt, ntok))):
        for tc in range(nt):
            tr = s.op("tensor", lambda t, tc=tc, src_c=src_c: t.transpose(
                psr[0:1, tc * 128:(tc + 1) * 128],
                tm[:, src_c + tc:src_c + tc + 1], identf[:, :]),
                waits=[o_s448, o_r448, prev_cp], tag=f"rmTT{tagp}")
        cp = s.op("vector", lambda v, which=which, dst0=dst0: v.tensor_copy(
            rowsb[0:1, dst0:dst0 + ntok], psr[0:1, 0:ntok]),
            waits=[], tag=f"rmcp{tagp}{which}")
        cp.waits.extend(cx.s.ops["tensor"][-nt:])
        prev_cp = cp
        row_ops.append(cp)
    ready = []
    prev = None
    for which, (dst, r0) in enumerate(((s448, 0), (r448, ntok))):
        for h0 in range(0, ntok, 512):
            mm = s.op("tensor", lambda t, h0=h0, r0=r0: t.matmul(
                psb[:, 0:512], c["ones_row"][0:1, :],
                rowsb[0:1, r0 + h0:r0 + h0 + 512], start=True, stop=True),
                waits=[row_ops[which], prev], tag=f"rmbc{tagp}")
            cp = s.op("vector", lambda v, dst=dst, h0=h0: v.tensor_copy(
                dst[:, h0:h0 + 512], psb[:, 0:512]), waits=[mm],
                tag=f"rmbcc{tagp}")
            prev = cp
            ready.append(cp)
    scope.close()
    return s448, r448, ready


def ln_full32(cx, c, xsb, xout32, ntok, x_ready, pieces, tagp=""):
    scope = ExitStack()
    rot = [cx.sb(f"lncr{tagp}{i}", [128, ntok], BF16, stack=scope) for i in range(2)]
    scr = cx.sb(f"lnscr{tagp}", [128, ntok], F32, stack=scope)
    bc, bc_ops = build_ln_rot(cx, c, xsb, rot, ntok, x_ready, pieces, scope, tagp)
    apps = []
    prev = None
    for ci in range(16):
        xc = xsb[:, ci * ntok:(ci + 1) * ntok]
        oc = xout32[:, ci * ntok:(ci + 1) * ntok]
        o1 = cx.s.op("vector", lambda v, xc=xc: v.tensor_mul(
            scr[:, :], xc, bc[:, 0:ntok]), waits=[bc_ops, prev],
            tag=f"a321{tagp}{ci}")
        o2 = cx.s.op("vector", lambda v, oc=oc: v.tensor_add(
            oc, scr[:, :], bc[:, ntok:]), waits=[o1], tag=f"a322{tagp}{ci}")
        apps.append(o2)
        prev = o2
    scope.close()
    return apps


def build_ln_rot(cx, c, xsb, rot, ntok, x_ready, pieces, scope, tagp=""):
    s = cx.s
    brow = cx.sb(f"brow{tagp}", [1, 4 * ntok], F32, stack=scope)
    bc = cx.sb(f"bc{tagp}", [128, 2 * ntok], F32, stack=scope)
    sq = cx.sb(f"sq{tagp}", [128, ntok], BF16, stack=scope)
    scope2 = ExitStack()
    stat_a = cx.psum(f"statA{tagp}", [1, ntok], F32, stack=scope2)
    stat_b = cx.psum(f"statB{tagp}", [1, ntok], F32, stack=scope2)
    mm_sum = []
    mm_sq_last = None
    prev_sq_mm = None
    prev_sum_mm = {}
    for ci in range(16):
        xc = xsb[:, ci * ntok:(ci + 1) * ntok]
        xbc = rot[ci % 2]
        cop = s.op("vector", lambda v, a=xbc, b=xc: v.tensor_copy(a[:, :], b),
                   waits=[x_ready, prev_sum_mm.get(ci % 2)],
                   tag=f"lncast{tagp}{ci}")
        sop = s.op("scalar", lambda sc, b=xbc: sc.activation(
            sq[:, :], b[:, :], AF.Square), waits=[cop, prev_sq_mm],
            tag=f"lnsq{tagp}{ci}")
        last_sum = None
        for h0, hw in pieces:
            last_sum = s.op("tensor", lambda t, ci=ci, h0=h0, hw=hw, xbc=xbc:
                            t.matmul(stat_a[0:1, h0:h0 + hw],
                                     c["ones_col"][:, :], xbc[:, h0:h0 + hw],
                                     start=(ci == 0), stop=(ci == 15)),
                            waits=[cop], tag=f"lnmmA{tagp}")
            mm_sum.append(last_sum)
        prev_sum_mm[ci % 2] = last_sum
        for h0, hw in pieces:
            prev_sq_mm = s.op("tensor", lambda t, ci=ci, h0=h0, hw=hw: t.matmul(
                stat_b[0:1, h0:h0 + hw], c["ones_col"][:, :], sq[:, h0:h0 + hw],
                start=(ci == 0), stop=(ci == 15)), waits=[sop],
                tag=f"lnmmB{tagp}")
        mm_sq_last = prev_sq_mm
    inv_d = 1.0 / D
    MEAN = brow[0:1, 0:ntok]
    RSTD = brow[0:1, ntok:2 * ntok]
    MB = brow[0:1, 2 * ntok:3 * ntok]
    SCR = brow[0:1, 3 * ntok:4 * ntok]
    o_mean = s.op("vector", lambda v: v.tensor_scalar_mul(
        MEAN, stat_a[0:1, :], inv_d), waits=[mm_sum[-1]], tag=f"mean{tagp}")
    o_m2 = s.op("scalar", lambda sc: sc.activation(
        SCR, MEAN, AF.Square), waits=[o_mean], tag=f"m2{tagp}")
    o_var = s.op("vector", lambda v: v.scalar_tensor_tensor(
        RSTD, stat_b[0:1, :], inv_d, SCR,
        ALU.mult, ALU.subtract), waits=[o_m2, mm_sq_last], tag=f"var{tagp}")
    scope2.close()
    bc_ps = cx.psum(f"bcps{tagp}", [128, ntok], F32, stack=scope)
    o_sqrt = s.op("scalar", lambda sc: sc.activation(
        RSTD, RSTD, AF.Sqrt, bias=c["eps_col"][0:1, :]),
        waits=[o_var], tag=f"sqrt{tagp}")
    o_rstd = s.op("vector", lambda v: v.reciprocal(
        RSTD, RSTD), waits=[o_sqrt], tag=f"rstd{tagp}")
    o_mb = s.op("vector", lambda v: v.scalar_tensor_tensor(
        MB, MEAN, -1.0, RSTD, ALU.mult, ALU.mult), waits=[o_rstd],
        tag=f"mb{tagp}")
    bc1mm = [s.op("tensor", lambda t, h0=h0, hw=hw: t.matmul(
        bc_ps[:, h0:h0 + hw], c["ones_row"][0:1, :],
        brow[0:1, ntok + h0:ntok + h0 + hw],
        start=True, stop=True), waits=[o_rstd, mm_sq_last], tag=f"bc1mm{tagp}")
        for h0, hw in pieces]
    cp1 = s.op("vector", lambda v: v.tensor_copy(bc[:, 0:ntok], bc_ps[:, :]),
               waits=bc1mm, tag=f"bccp1{tagp}")
    bc2mm = [s.op("tensor", lambda t, h0=h0, hw=hw: t.matmul(
        bc_ps[:, h0:h0 + hw], c["ones_row"][0:1, :],
        brow[0:1, 2 * ntok + h0:2 * ntok + h0 + hw],
        start=True, stop=True), waits=[o_mb, cp1], tag=f"bc2mm{tagp}")
        for h0, hw in pieces]
    cp2 = s.op("vector", lambda v: v.tensor_copy(bc[:, ntok:], bc_ps[:, :]),
               waits=bc2mm, tag=f"bccp2{tagp}")
    return bc, [cp1, cp2]


def build_launch_c1():
    cx = Ctx()
    s = cx.s
    streami = cx.inp("streami", [128, 16 * T])
    qgt = cx.inp("qgt", [44, D, 128], FP8)
    qut = cx.inp("qut", [44, D, 128], FP8)
    wsg = cx.inp("wsg", [128, 44])
    wsu = cx.inp("wsu", [128, 44])
    o_t = cx.out("t_out", [128, 44 * T], BF16)
    o_mt = cx.out("mt", [128, T])

    with cx.stack:
        c = std_consts(cx)
        identf, idl = cx.const_sb("identf", np.eye(128, dtype=np.float32))
        c["ld"].append(idl)
        xsb = cx.sb("xsb", [128, 16 * T], F32)
        xf = cx.sb("xf", [128, 16 * T], F32)
        qx = cx.sb("qx", [128, 16 * T], FP8)
        Mx = cx.sb("Mx", [128, T], F32)
        Mt = cx.sb("Mt", [128, T], F32)
        smv, small_ld = cx.smalls([("wsg", wsg), ("wsu", wsu)])

        ld_s = s.dma("sync", xsb[:, :], streami, tag="ldS")
        apps = ln_full32(cx, c, xsb, xf, T, [ld_s, c["ld"]], PIECES_T, tagp="c")

        absx = cx.sb("absx", [128, T], F32)
        mops = []
        prev = None
        prev_abs = None
        for ci in range(16):
            ab = s.op("scalar", lambda sc, ci=ci: sc.activation(
                absx[:, :], xf[:, ci * T:(ci + 1) * T], AF.Abs),
                waits=[apps[ci], prev], tag=f"abx{ci}")
            if ci == 0:
                prev = s.op("vector", lambda v: v.tensor_copy(
                    Mx[:, :], absx[:, :]), waits=[ab], tag="mx0")
            else:
                prev = s.op("vector", lambda v: v.tensor_max(
                    Mx[:, :], Mx[:, :], absx[:, :]), waits=[ab, prev],
                    tag=f"mx{ci}")
            mops.append(prev)
        s448x, r448x, bc_rdy = rowmax_to_bcs(cx, c, Mx, mops[-1], identf,
                                             tagp="x")
        q_ops = []
        for ci in range(16):
            q_ops.append(s.op("vector", lambda v, ci=ci: v.tensor_mul(
                qx[:, ci * T:(ci + 1) * T], xf[:, ci * T:(ci + 1) * T],
                s448x[:, :]), waits=[apps[ci], bc_rdy], tag=f"qx{ci}"))

        slg = [cx.sb(f"slg{i}", [128, D], FP8) for i in range(2)]
        slu = [cx.sb(f"slu{i}", [128, D], FP8) for i in range(2)]
        psG = [cx.psum(f"psG{i}", [128, 1024], F32) for i in range(2)]
        psU = [cx.psum(f"psU{i}", [128, 1024], F32) for i in range(2)]
        g32 = cx.sb("g32", [128, T], F32)
        abst = cx.sb("abst", [128, T], BF16)
        sg32 = cx.sb("sg32", [128, T], F32)
        u32 = cx.sb("u32", [128, T], F32)
        tb = [cx.sb(f"tb{i}", [128, T], BF16) for i in range(2)]

        ev_hist = []
        mm_hist = []
        st_hist = []
        mt_ops = []
        cx._c1_sil = []
        for m in range(44):
            i = m % 2
            wprev = [mm_hist[m - 2]] if m >= 2 else [q_ops[-1]]
            ldg = s.dma("sync", slg[i][:, :].rearrange("p (kc m) -> p kc m", kc=16),
                        qgt[m].rearrange("(kc p) m -> p kc m", p=128),
                        waits=wprev, tag=f"ldg{m}")
            ldu = s.dma("sync", slu[i][:, :].rearrange("p (kc m) -> p kc m", kc=16),
                        qut[m].rearrange("(kc p) m -> p kc m", p=128),
                        waits=wprev, tag=f"ldu{m}")
            evp = ev_hist[m - 2] if m >= 2 else []
            lastG = lastU = None
            for ps, sl, ldx_ in ((psG[i], slg[i], ldg), (psU[i], slu[i], ldu)):
                for col, h0, hw in pieces_cols(PIECES_T):
                    for kc in range(16):
                        wl = [ldx_, q_ops[kc]]
                        if kc == 0 and col == 0:
                            wl.append(evp)
                        last = s.op("tensor", lambda t, ps=ps, sl=sl, kc=kc,
                                    col=col, h0=h0, hw=hw: t.matmul(
                                        ps[:, col:col + hw],
                                        sl[:, kc * 128:(kc + 1) * 128],
                                        qx[:, kc * T + h0:kc * T + h0 + hw],
                                        start=(kc == 0), stop=(kc == 15)),
                                    waits=wl, tag=f"c1mm{m}")
                if ps is psG[i]:
                    lastG = last
                else:
                    lastU = last
            mm_hist.append(lastU)
            eprev = ev_hist[m - 1][-1] if m >= 1 else None
            sprev = cx._c1_sil[m - 1] if m >= 1 else None
            e1 = s.op("vector", lambda v, m=m, i=i: v.scalar_tensor_tensor(
                g32[:, :], psG[i][:, :], smv["wsg"][:, m:m + 1], r448x[:, :],
                ALU.mult, ALU.mult), waits=[lastG, small_ld, bc_rdy, eprev, sprev],
                tag=f"evg{m}")
            e2 = s.op("scalar", lambda sc: sc.activation(
                sg32[:, :], g32[:, :], AF.Silu), waits=[e1, eprev], tag=f"sil{m}")
            cx._c1_sil.append(e2)
            e3 = s.op("vector", lambda v, m=m, i=i: v.scalar_tensor_tensor(
                u32[:, :], psU[i][:, :], smv["wsu"][:, m:m + 1], r448x[:, :],
                ALU.mult, ALU.mult), waits=[lastU, eprev], tag=f"evu{m}")
            e4 = s.op("vector", lambda v, m=m, i=i: v.tensor_mul(
                tb[i][:, :], sg32[:, :], u32[:, :]),
                waits=[e2, e3, st_hist[m - 2] if m >= 2 else None],
                tag=f"tmul{m}")
            ab = s.op("scalar", lambda sc, i=i: sc.activation(
                abst[:, :], tb[i][:, :], AF.Abs),
                waits=[e4, mt_ops[-1] if mt_ops else None], tag=f"abt{m}")
            if m == 0:
                e5 = s.op("vector", lambda v: v.tensor_copy(
                    Mt[:, :], abst[:, :]), waits=[ab], tag=f"mt{m}")
            else:
                e5 = s.op("vector", lambda v: v.tensor_max(
                    Mt[:, :], Mt[:, :], abst[:, :]), waits=[ab, mt_ops[-1]],
                    tag=f"mt{m}")
            mt_ops.append(e5)
            st = s.dma("sync", o_t[:, m * T:(m + 1) * T], tb[i][:, :],
                       waits=[e4], tag=f"stt{m}")
            st_hist.append(st)
            ev_hist.append([e1, e3, e4])
        stm = s.dma("sync", o_mt, Mt[:, :], waits=[mt_ops[-1]], tag="stmt")
        s.op("sync", lambda e: e.nop(), waits=[stm] + st_hist, tag="fin")
        s.emit(cx.stack)
    return cx.nc


def build_launch_c2():
    cx = Ctx()
    s = cx.s
    streami = cx.inp("streami", [128, 16 * T])
    t_in = cx.inp("t_in", [128, 44 * T], BF16)
    mt_in = cx.inp("mt_in", [128, T])
    qdt = cx.inp("qdt", [16, I0, 128], FP8)
    wsd = cx.inp("wsd", [128, 16])
    o_stream = cx.out("stream2", [128, 16 * T])

    with cx.stack:
        c = std_consts(cx)
        identf, idl = cx.const_sb("identf", np.eye(128, dtype=np.float32))
        c["ld"].append(idl)
        xsb = cx.sb("xsb", [128, 16 * T], F32)
        qt = cx.sb("qt", [128, 44 * T], FP8)
        Mt = cx.sb("Mt", [128, T], F32)
        trot = [cx.sb(f"trot{i}", [128, T], BF16) for i in range(2)]
        smv, small_ld = cx.smalls([("wsd", wsd)])

        ld_s = s.dma("sync", xsb[:, :], streami, tag="ldS")
        ld_m = s.dma("sync", Mt[:, :], mt_in, tag="ldM")
        s448t, r448t, bc_rdy = rowmax_to_bcs(cx, c, Mt, ld_m, identf, tagp="t")

        q_ops = []
        prev_q = {}
        for mi in range(44):
            i = mi % 2
            ldt = s.dma("sync", trot[i][:, :], t_in[:, mi * T:(mi + 1) * T],
                        waits=[prev_q.get(i)], tag=f"ldt{mi}")
            qo = s.op("vector", lambda v, mi=mi, i=i: v.tensor_mul(
                qt[:, mi * T:(mi + 1) * T], trot[i][:, :], s448t[:, :]),
                waits=[ldt, bc_rdy], tag=f"qt{mi}")
            prev_q[i] = qo
            q_ops.append(qo)

        pst = [cx.psum(f"pp{i}", [128, 1024], F32) for i in range(2)]
        st_ops = []

        dscr = cx.sb("dscr", [128, 1024], F32)

        def evac_d(m, ps, pieces, waits):
            ops = []
            for col, h0, hw in pieces_cols(pieces):
                o1 = s.op("vector", lambda v, m=m, ps=ps, col=col, hw=hw:
                          v.scalar_tensor_tensor(
                              dscr[:, col:col + hw],
                              ps[:, col:col + hw], smv["wsd"][:, m:m + 1],
                              r448t[:, col:col + hw], ALU.mult, ALU.mult),
                          waits=waits + [small_ld, bc_rdy], tag=f"evd{m}")
                o2 = s.op("vector", lambda v, m=m, h0=h0, hw=hw, col=col:
                          v.tensor_add(
                              xsb[:, m * T + h0:m * T + h0 + hw],
                              xsb[:, m * T + h0:m * T + h0 + hw],
                              dscr[:, col:col + hw]),
                          waits=[o1, ld_s], tag=f"evd2{m}")
                ops.append(o2)
            st_ops.append(s.dma("sync", o_stream[:, m * T:(m + 1) * T],
                                xsb[:, m * T:(m + 1) * T], waits=ops,
                                tag=f"std{m}"))
            return ops

        stream_matmul(cx, "dn", qdt, 16, I0,
                      lambda kc, h0, hw: qt[:, kc * T + h0:kc * T + h0 + hw],
                      lambda kc: q_ops[kc], evac_d, PIECES_T, pst,
                      sb_dtype=FP8, first_waits=bc_rdy)
        s.op("sync", lambda e: e.nop(), waits=st_ops, tag="fin")
        s.emit(cx.stack)
    return cx.nc




LNSCALE_DK = -3.1192868061458647


def ln_hbm(cx, c, x_dram, xbsb, ntok, pieces, tagp=""):
    s = cx.s
    scope = ExitStack()
    brow = cx.sb(f"brow{tagp}", [1, 4 * ntok], F32, stack=scope)
    bc = cx.sb(f"bc{tagp}", [128, 2 * ntok], F32, stack=scope)
    rotb = [cx.sb(f"lhb{tagp}{i}", [128, ntok], BF16, stack=scope) for i in range(2)]
    rotf = [cx.sb(f"lhf{tagp}{i}", [128, ntok], F32, stack=scope) for i in range(2)]
    sq = cx.sb(f"sq{tagp}", [128, ntok], BF16, stack=scope)
    scr = cx.sb(f"lnscr{tagp}", [128, ntok], F32, stack=scope)
    scope2 = ExitStack()
    stat_a = cx.psum(f"statA{tagp}", [1, ntok], F32, stack=scope2)
    stat_b = cx.psum(f"statB{tagp}", [1, ntok], F32, stack=scope2)
    mm_sum = []
    mm_sq_last = None
    prev_sq = None
    prev_mm = {}
    for ci in range(16):
        xbc = rotb[ci % 2]
        ld = s.dma("gpsimd", xbc[:, :], x_dram[:, ci * ntok:(ci + 1) * ntok],
                   waits=[prev_mm.get(ci % 2), c["ld"]], tag=f"lhld{tagp}{ci}")
        sop = s.op("scalar", lambda sc, b=xbc: sc.activation(
            sq[:, :], b[:, :], AF.Square), waits=[ld, prev_sq],
            tag=f"lnsq{tagp}{ci}")
        last = None
        for h0, hw in pieces:
            last = s.op("tensor", lambda t, ci=ci, h0=h0, hw=hw, xbc=xbc:
                        t.matmul(stat_a[0:1, h0:h0 + hw], c["ones_col"][:, :],
                                 xbc[:, h0:h0 + hw], start=(ci == 0),
                                 stop=(ci == 15)), waits=[ld], tag=f"lnA{tagp}")
            mm_sum.append(last)
        for h0, hw in pieces:
            prev_sq = s.op("tensor", lambda t, ci=ci, h0=h0, hw=hw: t.matmul(
                stat_b[0:1, h0:h0 + hw], c["ones_col"][:, :], sq[:, h0:h0 + hw],
                start=(ci == 0), stop=(ci == 15)), waits=[sop], tag=f"lnB{tagp}")
        prev_mm[ci % 2] = [last, prev_sq]
        mm_sq_last = prev_sq
    inv_d = 1.0 / D
    MEAN = brow[0:1, 0:ntok]
    RSTD = brow[0:1, ntok:2 * ntok]
    MB = brow[0:1, 2 * ntok:3 * ntok]
    SCR = brow[0:1, 3 * ntok:4 * ntok]
    o_mean = s.op("vector", lambda v: v.tensor_scalar_mul(
        MEAN, stat_a[0:1, :], inv_d), waits=[mm_sum[-1]], tag=f"mean{tagp}")
    o_m2 = s.op("scalar", lambda sc: sc.activation(SCR, MEAN, AF.Square),
                waits=[o_mean], tag=f"m2{tagp}")
    o_var = s.op("vector", lambda v: v.scalar_tensor_tensor(
        RSTD, stat_b[0:1, :], inv_d, SCR, ALU.mult, ALU.subtract),
        waits=[o_m2, mm_sq_last], tag=f"var{tagp}")
    scope2.close()
    bc_ps = cx.psum(f"bcps{tagp}", [128, ntok], F32, stack=scope)
    o_sqrt = s.op("scalar", lambda sc: sc.activation(
        RSTD, RSTD, AF.Sqrt, bias=c["eps_col"][0:1, :]), waits=[o_var],
        tag=f"sqrt{tagp}")
    o_rstd = s.op("vector", lambda v: v.reciprocal(RSTD, RSTD), waits=[o_sqrt],
                  tag=f"rstd{tagp}")
    o_mb = s.op("vector", lambda v: v.scalar_tensor_tensor(
        MB, MEAN, -1.0, RSTD, ALU.mult, ALU.mult), waits=[o_rstd], tag=f"mb{tagp}")
    bc1 = [s.op("tensor", lambda t, h0=h0, hw=hw: t.matmul(
        bc_ps[:, h0:h0 + hw], c["ones_row"][0:1, :],
        brow[0:1, ntok + h0:ntok + h0 + hw], start=True, stop=True),
        waits=[o_rstd, mm_sq_last], tag=f"bc1{tagp}") for h0, hw in pieces]
    cp1 = s.op("vector", lambda v: v.tensor_copy(bc[:, 0:ntok], bc_ps[:, :]),
               waits=bc1, tag=f"bcc1{tagp}")
    bc2 = [s.op("tensor", lambda t, h0=h0, hw=hw: t.matmul(
        bc_ps[:, h0:h0 + hw], c["ones_row"][0:1, :],
        brow[0:1, 2 * ntok + h0:2 * ntok + h0 + hw], start=True, stop=True),
        waits=[o_mb, cp1], tag=f"bc2{tagp}") for h0, hw in pieces]
    cp2 = s.op("vector", lambda v: v.tensor_copy(bc[:, ntok:], bc_ps[:, :]),
               waits=bc2, tag=f"bcc2{tagp}")
    apps = []
    prev_o1 = {}
    prev = None
    for ci in range(16):
        xfc = rotf[ci % 2]
        ld = s.dma("sync", xfc[:, :], x_dram[:, ci * ntok:(ci + 1) * ntok],
                   waits=[prev_o1.get(ci % 2)], tag=f"lhf{tagp}{ci}")
        o1 = s.op("vector", lambda v, xfc=xfc: v.tensor_mul(
            scr[:, :], xfc[:, :], bc[:, 0:ntok]), waits=[ld, cp1, cp2, prev],
            tag=f"ap1{tagp}{ci}")
        o2 = s.op("vector", lambda v, ci=ci: v.tensor_add(
            xbsb[:, ci * ntok:(ci + 1) * ntok], scr[:, :], bc[:, ntok:]),
            waits=[o1], tag=f"ap2{tagp}{ci}")
        prev_o1[ci % 2] = o1
        prev = o2
        apps.append(o2)
    scope.close()
    return apps


def build_launch_d():
    cx = Ctx()
    s = cx.s
    streami = cx.inp("streami", [128, 16 * T])
    wq_t = cx.inp("wq_t", [16, D, 128])
    wk_t = cx.inp("wk_t", [16, D, 128])
    wv_nat = cx.inp("wv_nat", [D, 1024])
    wgk1p = cx.inp("wgk1p", [128, 256])
    wgk2 = cx.inp("wgk2", [16, D])
    bgk2 = cx.inp("bgk2", [128, 16])
    o_o = cx.out("o_tm", [128, 8 * T], BF16)
    o_qe = cx.out("qe_all", [128, 16 * T], BF16)
    o_sp = cx.out("spack", [128, 4096])
    o_a = cx.out("a_out", [128, 16])
    o_cc = cx.out("cc_out", [128, 128])
    o_xb = cx.out("xb_out", [128, 16 * T], BF16)

    with cx.stack:
        c = std_consts(cx, ident=True, mask128=True)
        tril, trl = cx.const_sb("trilc", np.triu(np.ones((128, 128), np.float32)))
        m8 = np.ones((128, 32), np.float32)
        m8[:, ::8] = 0.0
        mask8, m8l = cx.const_sb("mask8c", m8)
        onec, onl = cx.const_sb("onecol", np.ones((128, 1), np.float32))
        c["ld"] += [trl, m8l, onl]
        xb = cx.sb("xb", [128, 16 * T], BF16)
        v_tm = cx.sb("v_tm", [128, 8 * T], BF16)
        o_sb = cx.sb("o_sb", [128, 8 * T], BF16)
        spack = cx.sb("spack_sb", [128, 4096], F32)
        sb_S = cx.sb("sbS", [128, 1024], BF16)
        fm1 = cx.sb("fm1", [16, T], BF16)
        qe_h = cx.sb("qe_h", [128, 4 * T], BF16)
        kd_h = cx.sb("kd_h", [128, 4 * T], BF16)
        kdtm = cx.sb("kdtm", [128, 8 * 512], BF16)
        ebt = cx.sb("ebt", [128, T], BF16)
        embt = cx.sb("embt", [128, T], BF16)
        gkp = cx.sb("gkp", [128, T], F32)
        sA = cx.sb("sAd", [128, T], F32)
        sB = cx.sb("sBd", [128, T], F32)
        m0t = cx.sb("m0t", [128, T], F32)
        b_t = cx.sb("b_td", [128, T], F32)
        zs = cx.sb("zs", [128, 32], F32)
        bl = cx.sb("bl", [128, 32], F32)
        zsh = cx.sb("zsh", [128, 32], F32)
        wexp = cx.sb("wexp", [128, 32], F32)
        ebias = cx.sb("ebias", [128, 8], F32)
        asb = cx.sb("asbd", [128, 128], BF16)
        misc = cx.sb("miscd", [128, 64], F32)
        a_sb = cx.sb("a_sb", [128, 16], F32)
        cc_sb = cx.sb("cc_sb", [128, 128], F32)
        slq = [cx.sb(f"slq{i}", [128, D], BF16) for i in range(2)]
        wgk1_sb = cx.sb("wgk1sb", [128, 256], BF16)
        wgk2_sb = cx.sb("wgk2sb", [16, D], BF16)
        smv, small_ld = cx.smalls([("bgk2", bgk2)])

        apps = ln_hbm(cx, c, streami, xb, T, PIECES_T, tagp="d")
        st_xb = s.dma("sync", o_xb, xb[:, :], waits=apps, tag="stxb")
        ld1 = s.dma("gpsimd", wgk1_sb[:, :], wgk1p, tag="ldgk1")
        ld2 = s.dma("gpsimd", wgk2_sb[:, :], wgk2, tag="ldgk2")

        scope_v = ExitStack()
        wv_sb = cx.sb("wv_sb", [128, 16 * 1024], BF16, stack=scope_v)
        psv = [cx.psum(f"psv{i}", [128, 1024], F32, stack=scope_v) for i in range(2)]
        psf = cx.psum("psf", [16, T], F32, stack=scope_v)
        ldv = s.dma("gpsimd", wv_sb[:, :].rearrange("p (kc m) -> p kc m", kc=16),
                    wv_nat.rearrange("(kc p) m -> p kc m", p=128),
                    waits=[apps], tag="ldwv")
        v_ev = []
        for tc in range(8):
            i = tc % 2
            last = None
            for col, h0, hw in pieces_cols(PIECES_T):
                for kc in range(16):
                    wl = ([ldv, apps[-1], v_ev[tc - 2] if tc >= 2 else None]
                          if (kc == 0 and col == 0) else [])
                    last = s.op("tensor", lambda t, i=i, kc=kc, tc=tc, col=col,
                                h0=h0, hw=hw: t.matmul(
                                    psv[i][:, col:col + hw],
                                    xb[:, kc * T + tc * 128:kc * T + tc * 128 + 128],
                                    wv_sb[:, kc * 1024 + h0:kc * 1024 + h0 + hw],
                                    start=(kc == 0), stop=(kc == 15)),
                                waits=wl, tag=f"vmm{tc}")
            ev = s.op("vector", lambda v, i=i, tc=tc: v.tensor_copy(
                v_tm[:, tc * T:(tc + 1) * T], psv[i][:, :]), waits=[last],
                tag=f"vev{tc}")
            v_ev.append(ev)
        lastf = None
        for col, h0, hw in pieces_cols(PIECES_T):
            for kc in range(16):
                lastf = s.op("tensor", lambda t, kc=kc, col=col, h0=h0, hw=hw:
                             t.matmul(psf[0:16, col:col + hw],
                                      wgk1_sb[:, kc * 16:(kc + 1) * 16],
                                      xb[:, kc * T + h0:kc * T + h0 + hw],
                                      start=(kc == 0), stop=(kc == 15)),
                             waits=[ld1, apps[kc], v_ev[-1] if (kc == 0 and col == 0) else None],
                             tag="fmm")
        evf = s.op("vector", lambda v: v.tensor_copy(fm1[:, :], psf[0:16, :]),
                   waits=[lastf], tag="fev")
        scope_v.close()

        psqk = cx.psum("psqk", [128, 1024], F32)
        psA = cx.psum("psA", [128, 128], F32)
        psO = cx.psum("psO", [128, 256], F32)
        psOb = cx.psum("psOb", [128, 128], BF16)
        psT = cx.psum("psT", [128, 1024], F32)
        qk_last_ev = [evf]
        P = {}

        st_qe = [None] * 4
        o_cp_last = None
        final_ops = []
        for h in range(4):
            mq_done = []
            for mq in range(4):
                gq = 4 * h + mq
                lastg = None
                for col, h0, hw in pieces_cols(PIECES_T):
                    lastg = s.op("tensor", lambda t, gq=gq, col=col, h0=h0,
                                 hw=hw: t.matmul(
                                     psqk[:, col:col + hw],
                                     wgk2_sb[0:16, gq * 128:(gq + 1) * 128],
                                     fm1[:, h0:h0 + hw], start=True, stop=True),
                                 waits=[ld2, evf, qk_last_ev], tag=f"gkmm{gq}")
                e0 = s.op("vector", lambda v, gq=gq: v.tensor_scalar_add(
                    gkp[:, :], psqk[:, :], smv["bgk2"][:, gq:gq + 1]),
                    waits=[lastg, small_ld, P.get("gkp")], tag=f"gk0{gq}")
                qk_last_ev = [e0]
                e1 = s.op("vector", lambda v: v.tensor_scalar_min(
                    m0t[:, :], gkp[:, :], 0.0), waits=[e0, P.get("m0")],
                    tag=f"gk1{gq}")
                e2 = s.op("scalar", lambda sc: sc.activation(
                    sA[:, :], gkp[:, :], AF.Abs), waits=[e0, P.get("sA")],
                    tag=f"gk2{gq}")
                e3 = s.op("scalar", lambda sc: sc.activation(
                    sB[:, :], sA[:, :], AF.Exp, scale=-1.0),
                    waits=[e2, P.get("sB")], tag=f"gk3{gq}")
                e4 = s.op("scalar", lambda sc: sc.activation(
                    sA[:, :], sB[:, :], AF.Ln, bias=onec[:, 0:1]),
                    waits=[e3], tag=f"gk4{gq}")
                e5 = s.op("vector", lambda v: v.scalar_tensor_tensor(
                    m0t[:, :], sA[:, :], -1.0, m0t[:, :], ALU.mult, ALU.add),
                    waits=[e4, e1], tag=f"gk5{gq}")
                P["sA"] = e5
                P["sB"] = e4
                P["m0"] = e5
                scn = s.op("vector", lambda v: v.tensor_tensor_scan(
                    b_t[:, :], c["mask128"][:, :], m0t[:, :], 0.0,
                    ALU.mult, ALU.add), waits=[e5, P.get("b")], tag=f"gk6{gq}")
                P["gkp"] = [e1, e2]
                x1 = s.op("vector", lambda v, mq=mq: v.tensor_copy(
                    zs[:, mq * 8:(mq + 1) * 8].rearrange("p (c o) -> p c o", o=1),
                    b_t[:, :].rearrange("p (c t) -> p c t", c=8)[:, :, 64:65]),
                    waits=[scn, P.get("zsg")], tag=f"zs{gq}")
                x2 = s.op("vector", lambda v, mq=mq: v.tensor_copy(
                    bl[:, mq * 8:(mq + 1) * 8].rearrange("p (c o) -> p c o", o=1),
                    b_t[:, :].rearrange("p (c t) -> p c t", c=8)[:, :, 127:128]),
                    waits=[scn, P.get("blg")], tag=f"bl{gq}")
                x3 = s.op("vector", lambda v, mq=mq: v.tensor_scalar(
                    ebias[:, :], zs[:, mq * 8:(mq + 1) * 8], -1.0 / 16.0,
                    LNSCALE_DK, ALU.mult, ALU.add), waits=[x1, P.get("ebias")],
                    tag=f"ebias{gq}")
                ebo = None
                for ch in range(8):
                    ebo = s.op("scalar", lambda sc, ch=ch: sc.activation(
                        ebt[:, ch * 128:(ch + 1) * 128],
                        b_t[:, ch * 128:(ch + 1) * 128], AF.Exp,
                        scale=1.0 / 16.0, bias=ebias[:, ch:ch + 1]),
                        waits=[x3, scn, P.get("ebt")], tag=f"eb{gq}")
                x4 = s.op("vector", lambda v, mq=mq: v.tensor_scalar_mul(
                    ebias[:, :], zs[:, mq * 8:(mq + 1) * 8], 1.0 / 16.0),
                    waits=[ebo], tag=f"embias{gq}")
                emo = None
                for ch in range(8):
                    emo = s.op("scalar", lambda sc, ch=ch: sc.activation(
                        embt[:, ch * 128:(ch + 1) * 128],
                        b_t[:, ch * 128:(ch + 1) * 128], AF.Exp,
                        scale=-1.0 / 16.0, bias=ebias[:, ch:ch + 1]),
                        waits=[x4, P.get("embt")], tag=f"em{gq}")
                P["b"] = [emo, x1, x2]
                P["ebias"] = emo
                i2 = mq % 2
                ldq = s.dma("gpsimd",
                            slq[i2][:, :].rearrange("p (kc m) -> p kc m", kc=16),
                            wq_t[gq].rearrange("(kc p) m -> p kc m", p=128),
                            waits=[P.get(("sl", i2))], tag=f"ldq{gq}")
                lastq = None
                for col, h0, hw in pieces_cols(PIECES_T):
                    for kc in range(16):
                        wl = ([ldq, apps[-1], qk_last_ev]
                              if (kc == 0 and col == 0) else [])
                        lastq = s.op("tensor", lambda t, i2=i2, kc=kc, col=col,
                                     h0=h0, hw=hw: t.matmul(
                                         psqk[:, col:col + hw],
                                         slq[i2][:, kc * 128:(kc + 1) * 128],
                                         xb[:, kc * T + h0:kc * T + h0 + hw],
                                         start=(kc == 0), stop=(kc == 15)),
                                     waits=wl, tag=f"qmm{gq}")
                qev = s.op("vector", lambda v, mq=mq: v.tensor_mul(
                    qe_h[:, mq * T:(mq + 1) * T], psqk[:, :], ebt[:, :]),
                    waits=[lastq, ebo, st_qe[h - 1] if h > 0 else None,
                           P.get(("qe", mq))], tag=f"qev{gq}")
                qk_last_ev = [qev]
                ldk = s.dma("gpsimd",
                            slq[i2][:, :].rearrange("p (kc m) -> p kc m", kc=16),
                            wk_t[gq].rearrange("(kc p) m -> p kc m", p=128),
                            waits=[lastq], tag=f"ldk{gq}")
                lastk = None
                for col, h0, hw in pieces_cols(PIECES_T):
                    for kc in range(16):
                        wl = ([ldk, qk_last_ev]
                              if (kc == 0 and col == 0) else [])
                        lastk = s.op("tensor", lambda t, i2=i2, kc=kc, col=col,
                                     h0=h0, hw=hw: t.matmul(
                                         psqk[:, col:col + hw],
                                         slq[i2][:, kc * 128:(kc + 1) * 128],
                                         xb[:, kc * T + h0:kc * T + h0 + hw],
                                         start=(kc == 0), stop=(kc == 15)),
                                     waits=wl, tag=f"kmm{gq}")
                kev = s.op("vector", lambda v, mq=mq: v.tensor_mul(
                    kd_h[:, mq * T:(mq + 1) * T], psqk[:, :], embt[:, :]),
                    waits=[lastk, emo, P.get(("kd", mq))], tag=f"kev{gq}")
                qk_last_ev = [kev]
                P[("sl", i2)] = lastk
                P["ebt"] = qev
                P["embt"] = kev
                mq_done.append([qev, kev])
            w2 = s.op("vector", lambda v: v.tensor_sub(
                wexp[:, :], bl[:, :], zs[:, :]), waits=[mq_done, P.get("wexpg")],
                tag=f"wx1{h}")
            w0 = s.op("vector", lambda v: v.tensor_copy(
                zsh[:, 0:31], zs[:, 1:32]), waits=[mq_done, P.get("zshg")],
                tag=f"zsh{h}")
            w1 = s.op("vector", lambda v: v.memset(
                zsh[:, :].rearrange("p (m c) -> p m c", m=4)[:, :, 7:8], 0.0),
                waits=[w0], tag=f"zshz{h}")
            w3 = s.op("vector", lambda v: v.tensor_add(
                wexp[:, :], wexp[:, :], zsh[:, :]), waits=[w2, w1], tag=f"wx2{h}")
            w4 = s.op("scalar", lambda sc: sc.activation(
                wexp[:, :], wexp[:, :], AF.Exp, scale=1.0 / 16.0),
                waits=[w3], tag=f"wx3{h}")
            P["zshg"] = w3
            tr_cp = []
            prev_cp = P.get("kdtm")
            for ccc in range(8):
                for mq in range(4):
                    tr = s.op("tensor", lambda t, ccc=ccc, mq=mq: t.transpose(
                        psOb[:, :],
                        kd_h[:, mq * T + ccc * 128:mq * T + ccc * 128 + 128],
                        c["ident"][:, :]),
                        waits=[mq_done[mq], prev_cp], tag=f"ktr{h}")
                    cp = s.op("vector", lambda v, ccc=ccc, mq=mq: v.tensor_copy(
                        kdtm[:, ccc * 512 + mq * 128:ccc * 512 + mq * 128 + 128],
                        psOb[:, :]), waits=[tr], tag=f"ktc{h}")
                    prev_cp = cp
                    tr_cp.append(cp)
            z1 = s.op("vector", lambda v, h=h: v.memset(
                spack[:, h * 1024:(h + 1) * 1024], 0.0), waits=[], tag=f"sz{h}")
            z2 = s.op("vector", lambda v: v.memset(sb_S[:, :], 0.0),
                      waits=[P.get("sbS")], tag=f"sbz{h}")
            prev_asb = P.get("asbg")
            prev_ocp = None
            prev_upd = [z1] * 4
            prev_cast = [z2] * 4
            for ccc in range(8):
                lastA = None
                for mq in range(4):
                    lastA = s.op("tensor", lambda t, ccc=ccc, mq=mq: t.matmul(
                        psA[:, :],
                        kd_h[:, mq * T + ccc * 128:mq * T + ccc * 128 + 128],
                        qe_h[:, mq * T + ccc * 128:mq * T + ccc * 128 + 128],
                        start=(mq == 0), stop=(mq == 3)),
                        waits=[mq_done, prev_asb], tag=f"Amm{h}{ccc}")
                aev = s.op("vector", lambda v: v.tensor_mul(
                    asb[:, :], psA[:, :], tril[:, :]), waits=[lastA, trl],
                    tag=f"Aev{h}{ccc}")
                prev_asb = aev
                o_mms = [s.op("tensor", lambda t, ccc=ccc, h=h: t.matmul(
                    psO[:, :], asb[:, :],
                    v_tm[:, ccc * T + h * 256:ccc * T + h * 256 + 256],
                    start=True, stop=(ccc == 0)),
                    waits=[aev, v_ev[ccc], prev_ocp], tag=f"Omm{h}{ccc}")]
                if ccc > 0:
                    for mq in range(4):
                        o_mms.append(s.op("tensor", lambda t, ccc=ccc, mq=mq:
                                          t.matmul(
                                              psO[:, :],
                                              qe_h[:, mq * T + ccc * 128:mq * T + ccc * 128 + 128],
                                              sb_S[:, mq * 256:(mq + 1) * 256],
                                              start=False, stop=(mq == 3)),
                                          waits=[prev_cast[mq]],
                                          tag=f"O2mm{h}{ccc}"))
                ocp = s.op("vector", lambda v, ccc=ccc, h=h: v.tensor_copy(
                    o_sb[:, ccc * T + h * 256:ccc * T + h * 256 + 256],
                    psO[:, :]), waits=o_mms, tag=f"Ocp{h}{ccc}")
                prev_ocp = ocp
                o_cp_last = ocp
                t_mms = []
                for mq in range(4):
                    t_mms.append(s.op("tensor", lambda t, ccc=ccc, mq=mq, h=h:
                                      t.matmul(
                                          psT[:, mq * 256:(mq + 1) * 256],
                                          kdtm[:, ccc * 512 + mq * 128:ccc * 512 + mq * 128 + 128],
                                          v_tm[:, ccc * T + h * 256:ccc * T + h * 256 + 256],
                                          start=True, stop=True),
                                      waits=[tr_cp[ccc * 4 + mq], v_ev[ccc],
                                             prev_upd[mq]], tag=f"Tmm{h}{ccc}"))
                for mq in range(4):
                    sl_ = spack[:, h * 1024 + mq * 256:h * 1024 + (mq + 1) * 256]
                    u1 = s.op("vector", lambda v, sl_=sl_, mq=mq: v.tensor_add(
                        sl_, sl_, psT[:, mq * 256:(mq + 1) * 256]),
                        waits=[t_mms[mq], prev_upd[mq]], tag=f"su{h}{ccc}")
                    u2 = s.op("vector", lambda v, sl_=sl_, mq=mq, ccc=ccc:
                              v.tensor_scalar_mul(
                                  sl_, sl_, wexp[:, mq * 8 + ccc:mq * 8 + ccc + 1]),
                              waits=[u1, w4], tag=f"sm{h}{ccc}")
                    prev_upd[mq] = u2
                    if ccc < 7:
                        u3 = s.op("vector", lambda v, sl_=sl_, mq=mq:
                                  v.tensor_copy(sb_S[:, mq * 256:(mq + 1) * 256],
                                                sl_),
                                  waits=[u2, o_mms], tag=f"scast{h}{ccc}")
                        prev_cast[mq] = u3
            st_qe[h] = s.dma("sync", o_qe[:, h * 4096:(h + 1) * 4096],
                             qe_h[:, :], waits=[prev_asb, prev_ocp], tag=f"stqe{h}")
            P["kdtm"] = prev_upd
            for mq in range(4):
                P[("kd", mq)] = [prev_asb, tr_cp[-1]]
                P[("qe", mq)] = [prev_asb, prev_ocp]
            P["sbS"] = prev_cast
            P["asbg"] = prev_asb
            P["wexpg"] = prev_upd
            fo = []
            for mq in range(4):
                fo.append(s.op("vector", lambda v, mq=mq: v.reduce_sum(
                    misc[:, mq:mq + 1], bl[:, mq * 8:(mq + 1) * 8], AX.X),
                    waits=[mq_done, P.get("misc")], tag=f"bls{h}{mq}"))
            a1 = s.op("scalar", lambda sc, h=h: sc.activation(
                a_sb[:, h * 4:(h + 1) * 4], misc[:, 0:4], AF.Exp,
                scale=1.0 / 16.0), waits=fo, tag=f"aex{h}")
            P["misc"] = a1
            c1_ = s.op("vector", lambda v: v.tensor_tensor_scan(
                misc[:, 32:64], mask8[:, :], bl[:, 0:32], 0.0,
                ALU.mult, ALU.add), waits=[mq_done, P.get("misc2"), m8l],
                tag=f"ccs{h}")
            c2_ = s.op("vector", lambda v: v.tensor_sub(
                misc[:, 32:64], misc[:, 32:64], bl[:, 0:32]), waits=[c1_],
                tag=f"ccsub{h}")
            c3_ = s.op("vector", lambda v: v.tensor_add(
                misc[:, 32:64], misc[:, 32:64], zs[:, 0:32]), waits=[c2_],
                tag=f"ccadd{h}")
            c4_ = s.op("scalar", lambda sc, h=h: sc.activation(
                cc_sb[:, h * 32:(h + 1) * 32], misc[:, 32:64], AF.Exp,
                scale=1.0 / 16.0), waits=[c3_], tag=f"ccex{h}")
            P["misc2"] = c4_
            P["zsg"] = [w4, c4_]
            P["blg"] = [w4, a1, c4_]
            final_ops += [a1, c4_]
        st1 = s.dma("sync", o_o, o_sb[:, :], waits=[o_cp_last], tag="sto")
        st2 = s.dma("sync", o_sp, spack[:, :], waits=[P["kdtm"]], tag="stsp")
        st3 = s.dma("sync", o_a, a_sb[:, :], waits=final_ops, tag="sta")
        st4 = s.dma("sync", o_cc, cc_sb[:, :], waits=final_ops, tag="stcc")
        s.op("sync", lambda e: e.nop(),
             waits=[st1, st2, st3, st4, st_xb] + st_qe, tag="fin")
        s.emit(cx.stack)
    return cx.nc



def build_launch_e():
    cx = Ctx()
    s = cx.s
    streami = cx.inp("streami", [128, 16 * T])
    o_in = cx.inp("o_in", [128, 8 * T], BF16)
    qe_in = cx.inp("qe_in", [128, 16 * T], BF16)
    cc_in = cx.inp("cc_in", [128, 128])
    sp_all = cx.inp("sp_all", [512, 4096])
    a_all = cx.inp("a_all", [512, 16])
    selm = cx.inp("selm", [128, 4])
    xbi = cx.inp("xbi", [128, 16 * T], BF16)
    wg_nat = cx.inp("wg_nat", [D, 1024])
    wnorm = cx.inp("wnorm", [1, 1024])
    wo_t = cx.inp("wo_t", [16, 1024, 128])
    o_stream = cx.out("stream3", [128, 16 * T])

    with cx.stack:
        c = std_consts(cx, ident=True)
        o_sb = cx.sb("o_sbe", [128, 8 * T], BF16)
        sbin = cx.sb("sbin", [128, 4096], BF16)
        cc_sb = cx.sb("cc_sbe", [128, 128], F32)
        sel = cx.sb("sele", [128, 4], F32)
        rstd = cx.sb("rstde", [128, 32], F32)
        racc = cx.sb("racce", [128, 32], F32)
        wnb = cx.sb("wnbe", [128, T], BF16)
        wrow = cx.sb("wrowe", [1, T], BF16)

        ld_o = s.dma("sync", o_sb[:, :], o_in, tag="ldo")
        ld_cc = s.dma("sync", cc_sb[:, :], cc_in, tag="ldcc")
        ld_sel = s.dma("sync", sel[:, :], selm, tag="ldsel")
        ld_wr = s.dma("gpsimd", wrow[:, :], wnorm, tag="ldwr")

        sc1 = ExitStack()
        T_acc = cx.sb("T_acc", [128, 4096], F32, stack=sc1)
        S_in = cx.sb("S_in", [128, 4096], F32, stack=sc1)
        rotS = [cx.sb(f"rotS{i}", [128, 4096], F32, stack=sc1) for i in range(2)]
        a_sb = cx.sb("a_sbe", [128, 64], F32, stack=sc1)
        ld_a = s.dma("sync", a_sb[:, :].rearrange("p (g m) -> p g m", g=4),
                     a_all.rearrange("(g p) m -> p g m", p=128), tag="lda",
                     stack=None) if False else s.dma(
                         "sync", a_sb[:, :].rearrange("p (g m) -> p g m", g=4),
                         a_all.rearrange("(g p) m -> p g m", p=128), tag="lda")
        ld_S = []
        for j in range(4):
            ld_S.append(s.dma("sync", rotS[j % 2][:, :],
                              sp_all[j * 128:(j + 1) * 128, :],
                              waits=[None], tag=f"ldS{j}"))
        cp0 = s.op("vector", lambda v: v.tensor_copy(T_acc[:, :], rotS[0][:, :]),
                   waits=[ld_S[0]], tag="cp0")
        si0 = s.op("vector", lambda v: v.tensor_scalar_mul(
            S_in[:, :], T_acc[:, :], sel[:, 0:1]), waits=[cp0, ld_sel], tag="si0")
        ld_S[2].waits.append(cp0)
        prevsi = si0
        prevT = cp0
        for j in range(1, 4):
            tops = []
            for hm in range(16):
                tops.append(s.op("vector", lambda v, j=j, hm=hm:
                                 v.scalar_tensor_tensor(
                                     T_acc[:, hm * 256:(hm + 1) * 256],
                                     T_acc[:, hm * 256:(hm + 1) * 256],
                                     a_sb[:, j * 16 + hm:j * 16 + hm + 1],
                                     rotS[j % 2][:, hm * 256:(hm + 1) * 256],
                                     ALU.mult, ALU.add),
                                 waits=[ld_S[j], ld_a, prevT, prevsi],
                                 tag=f"tj{j}"))
            if j + 2 < 4:
                ld_S[j + 2].waits.extend(tops)
            prevT = tops
            prevsi = s.op("vector", lambda v, j=j: v.scalar_tensor_tensor(
                S_in[:, :], T_acc[:, :], sel[:, j:j + 1], S_in[:, :],
                ALU.mult, ALU.add), waits=[tops, prevsi], tag=f"si{j}")
        cbin = s.op("vector", lambda v: v.tensor_copy(sbin[:, :], S_in[:, :]),
                    waits=[prevsi], tag="cbin")
        sc1.close()

        sc2 = ExitStack()
        qrot = [cx.sb(f"qrot{i}", [128, 4096], BF16, stack=sc2) for i in range(2)]
        qs = cx.sb("qse", [128, 128], BF16, stack=sc2)
        psC = cx.psum("psC", [128, 256], F32, stack=sc2)
        prev_oadd = None
        prev_qs = None
        corr_ops = []
        h_last = {}
        for h in range(4):
            ldq_h = s.dma("sync", qrot[h % 2][:, :],
                          qe_in[:, h * 4096:(h + 1) * 4096],
                          waits=[h_last.get(h % 2), cbin if h < 2 else None],
                          tag=f"ldq{h}")
            for ccc in range(8):
                cmms = []
                for mq in range(4):
                    qso = s.op("vector", lambda v, h=h, ccc=ccc, mq=mq:
                               v.tensor_scalar_mul(
                                   qs[:, :],
                                   qrot[h % 2][:, mq * T + ccc * 128:mq * T + ccc * 128 + 128],
                                   cc_sb[:, h * 32 + mq * 8 + ccc:h * 32 + mq * 8 + ccc + 1]),
                               waits=[ldq_h, ld_cc, prev_qs], tag=f"qs{h}{ccc}")
                    cmms.append(s.op("tensor", lambda t, h=h, mq=mq: t.matmul(
                        psC[:, :], qs[:, :],
                        sbin[:, (h * 4 + mq) * 256:(h * 4 + mq + 1) * 256],
                        start=(mq == 0), stop=(mq == 3)),
                        waits=[qso, cbin, prev_oadd], tag=f"cmm{h}{ccc}"))
                    prev_qs = cmms[-1]
                oadd = s.op("vector", lambda v, h=h, ccc=ccc: v.tensor_add(
                    o_sb[:, ccc * T + h * 256:ccc * T + h * 256 + 256],
                    o_sb[:, ccc * T + h * 256:ccc * T + h * 256 + 256],
                    psC[:, :]), waits=[cmms, ld_o], tag=f"oadd{h}{ccc}")
                prev_oadd = oadd
                corr_ops.append(oadd)
            h_last[h % 2] = prev_qs
        sc2.close()

        dump = cx.sb("dumpe", [128, 256], F32)
        racc_ops = []
        prev_d = None
        for tc in range(8):
            for h in range(4):
                prev_d = s.op("scalar", lambda sc, tc=tc, h=h: sc.activation(
                    dump[:, :], o_sb[:, tc * T + h * 256:tc * T + h * 256 + 256],
                    AF.Square, accum_out=racc[:, tc * 4 + h:tc * 4 + h + 1]),
                    waits=[corr_ops, prev_d], tag=f"sq{tc}{h}")
                racc_ops.append(prev_d)
        r1 = s.op("scalar", lambda sc: sc.activation(
            rstd[:, :], racc[:, :], AF.Sqrt, scale=1.0 / 256.0,
            bias=c["eps_col"][:, 0:1]), waits=racc_ops, tag="rsq")
        r2 = s.op("vector", lambda v: v.reciprocal(rstd[:, :], rstd[:, :]),
                  waits=[r1], tag="rrec")
        psW = cx.psum("psW", [128, 512], F32)
        wb_ops = []
        prev = None
        for h0 in range(0, T, 512):
            mm = s.op("tensor", lambda t, h0=h0: t.matmul(
                psW[:, :], c["ones_row_b"][0:1, :], wrow[0:1, h0:h0 + 512],
                start=True, stop=True),
                waits=[ld_wr, prev, c["ld"], corr_ops[-1]], tag="wbc")
            prev = s.op("vector", lambda v, h0=h0: v.tensor_copy(
                wnb[:, h0:h0 + 512], psW[:, :]), waits=[mm], tag="wbcc")
            wb_ops.append(prev)

        sc4 = ExitStack()
        og_fm = cx.sb("og_fm", [128, 8 * T], BF16)
        xb = cx.sb("xbe", [128, 16 * T], BF16, stack=sc4)
        wg_sb = cx.sb("wg_sbe", [128, 16 * 1024], BF16, stack=sc4)
        sg_t = cx.sb("sg_te", [128, T], BF16, stack=sc4)
        psg = [cx.psum(f"psge{i}", [128, 1024], F32, stack=sc4) for i in range(2)]
        ld_xb = s.dma("sync", xb[:, :], xbi, waits=[corr_ops[-1]], tag="ldxb")
        ld_wg = s.dma("gpsimd", wg_sb[:, :].rearrange("p (kc m) -> p kc m", kc=16),
                      wg_nat.rearrange("(kc p) m -> p kc m", p=128),
                      waits=[corr_ops[-1]], tag="ldwg")
        og_ops = []
        prev_sg = None
        for tc in range(8):
            i = tc % 2
            last = None
            for col, h0, hw in pieces_cols(PIECES_T):
                for kc in range(16):
                    wl = [ld_xb, ld_wg]
                    if kc == 0 and col == 0:
                        wl.append(og_ops[-8:] if tc >= 2 else None)
                    last = s.op("tensor", lambda t, i=i, kc=kc, tc=tc, col=col,
                                h0=h0, hw=hw: t.matmul(
                                    psg[i][:, col:col + hw],
                                    xb[:, kc * T + tc * 128:kc * T + tc * 128 + 128],
                                    wg_sb[:, kc * 1024 + h0:kc * 1024 + h0 + hw],
                                    start=(kc == 0), stop=(kc == 15)),
                                waits=wl, tag=f"gmm{tc}")
            sg = s.op("scalar", lambda sc, i=i: sc.activation(
                sg_t[:, :], psg[i][:, :], AF.Silu), waits=[last, prev_sg],
                tag=f"gsil{tc}")
            gt = s.op("vector", lambda v: v.tensor_mul(
                sg_t[:, :], sg_t[:, :], wnb[:, :]), waits=[sg, wb_ops],
                tag=f"gwn{tc}")
            tc_ops = []
            for h in range(4):
                tc_ops.append(s.op("vector", lambda v, tc=tc, h=h:
                                   v.scalar_tensor_tensor(
                                       o_sb[:, tc * T + h * 256:tc * T + h * 256 + 256],
                                       o_sb[:, tc * T + h * 256:tc * T + h * 256 + 256],
                                       rstd[:, tc * 4 + h:tc * 4 + h + 1],
                                       sg_t[:, h * 256:(h + 1) * 256],
                                       ALU.mult, ALU.mult),
                                   waits=[gt, r2, corr_ops, racc_ops],
                                   tag=f"og{tc}{h}"))
            og_ops.extend(tc_ops)
            prev_sg = tc_ops
        psTr = cx.psum("psTre", [128, 128], BF16, stack=sc4)
        prev_cp = None
        tr_cps = {}
        for tc in range(8):
            for vc in range(8):
                tr = s.op("tensor", lambda t, tc=tc, vc=vc: t.transpose(
                    psTr[:, :], o_sb[:, tc * T + vc * 128:tc * T + vc * 128 + 128],
                    c["ident"][:, :]), waits=[og_ops, prev_cp], tag=f"ogtr{tc}")
                cp = s.op("vector", lambda v, tc=tc, vc=vc: v.tensor_copy(
                    og_fm[:, vc * T + tc * 128:vc * T + tc * 128 + 128],
                    psTr[:, :]), waits=[tr], tag=f"ogtc{tc}")
                prev_cp = cp
                tr_cps[vc] = cp
        sc4.close()

        xsb = cx.sb("xsbe", [128, 16 * T], F32)
        ld_s = s.dma("sync", xsb[:, :], streami, waits=[prev_cp, og_ops],
                     tag="ldS")
        pst = [cx.psum(f"ppe{i}", [128, 1024], F32) for i in range(2)]
        st_ops = []

        def evac_wo(m, ps, pieces, waits):
            ops = []
            for col, h0, hw in pieces_cols(pieces):
                ops.append(s.op("vector", lambda v, m=m, ps=ps, col=col, h0=h0,
                                hw=hw: v.tensor_add(
                                    xsb[:, m * T + h0:m * T + h0 + hw],
                                    xsb[:, m * T + h0:m * T + h0 + hw],
                                    ps[:, col:col + hw]),
                                waits=waits + [ld_s], tag=f"evwo{m}"))
            st_ops.append(s.dma("sync", o_stream[:, m * T:(m + 1) * T],
                                xsb[:, m * T:(m + 1) * T], waits=ops,
                                tag=f"stwo{m}"))
            return ops

        stream_matmul(cx, "wo", wo_t, 16, 1024,
                      lambda kc, h0, hw: og_fm[:, kc * T + h0:kc * T + h0 + hw],
                      lambda kc: tr_cps[kc], evac_wo, PIECES_T, pst,
                      first_waits=og_ops)
        s.op("sync", lambda e: e.nop(), waits=st_ops, tag="fin")
        s.emit(cx.stack)
    return cx.nc




def build_launch_f():
    cx = Ctx()
    s = cx.s
    streami = cx.inp("streami", [128, 16 * T])
    wg0_t = cx.inp("wg0_t", [64, D, 128])
    wg1_t = cx.inp("wg1_t", [64, D, 128])
    wl_t = cx.inp("wl_t", [16, I1, 128])
    o_out = cx.out("out", [128, 16 * T])

    with cx.stack:
        c = std_consts(cx)
        xsb = cx.sb("xsbf", [128, 16 * T], F32)
        qx = cx.sb("qxf", [128, 16 * T], FP8)
        qa = cx.sb("qaf", [128, 64 * T], FP8)
        gb = cx.sb("gbf", [128, T], BF16)
        gelb = cx.sb("gelbf", [128, T], BF16)
        ub = cx.sb("ubf", [128, T], BF16)
        actb = cx.sb("actbf", [128, T], BF16)
        dscr = cx.sb("dscrf", [128, T], BF16)

        ld_s = s.dma("sync", xsb[:, :], streami, tag="ldS")
        q_ops = ln_full32(cx, c, xsb, qx, T, [ld_s, c["ld"]], PIECES_T, tagp="f")

        slg0 = [cx.sb(f"slg0{i}", [128, D], FP8) for i in range(2)]
        slg1 = [cx.sb(f"slg1{i}", [128, D], FP8) for i in range(2)]
        scg = ExitStack()
        psG = [cx.psum(f"psGf{i}", [128, 1024], F32, stack=scg) for i in range(2)]
        psU = [cx.psum(f"psUf{i}", [128, 1024], F32, stack=scg) for i in range(2)]
        ev_hist = []
        mm_hist = []
        qa_ops = []
        P = {}
        for m in range(64):
            i = m % 2
            wprev = [mm_hist[m - 2]] if m >= 2 else [q_ops[-1]]
            ldg = s.dma("gpsimd",
                        slg0[i][:, :].rearrange("p (kc m) -> p kc m", kc=16),
                        wg0_t[m].rearrange("(kc p) m -> p kc m", p=128),
                        waits=wprev, tag=f"ldg0{m}")
            ldu = s.dma("gpsimd",
                        slg1[i][:, :].rearrange("p (kc m) -> p kc m", kc=16),
                        wg1_t[m].rearrange("(kc p) m -> p kc m", p=128),
                        waits=wprev, tag=f"ldg1{m}")
            evp = ev_hist[m - 2] if m >= 2 else []
            lastG = lastU = None
            for ps, sl, ldx_ in ((psG[i], slg0[i], ldg), (psU[i], slg1[i], ldu)):
                for col, h0, hw in pieces_cols(PIECES_T):
                    for kc in range(16):
                        wl = [ldx_, q_ops[kc]]
                        if kc == 0 and col == 0:
                            wl.append(evp)
                        last = s.op("tensor", lambda t, ps=ps, sl=sl, kc=kc,
                                    col=col, h0=h0, hw=hw: t.matmul(
                                        ps[:, col:col + hw],
                                        sl[:, kc * 128:(kc + 1) * 128],
                                        qx[:, kc * T + h0:kc * T + h0 + hw],
                                        start=(kc == 0), stop=(kc == 15)),
                                    waits=wl, tag=f"fmm{m}")
                if ps is psG[i]:
                    lastG = last
                else:
                    lastU = last
            mm_hist.append(lastU)
            e1 = s.op("vector", lambda v, i=i: v.tensor_copy(
                gb[:, :], psG[i][:, :]), waits=[lastG, P.get("gb")],
                tag=f"gcp{m}")
            e2 = s.op("scalar", lambda sc: sc.activation(
                gelb[:, :], gb[:, :], AF.Gelu_apprx_tanh),
                waits=[e1, P.get("gelb")], tag=f"gel{m}")
            e3 = s.op("scalar", lambda sc, i=i: sc.copy(
                ub[:, :], psU[i][:, :]), waits=[lastU, P.get("ub"), e2],
                tag=f"ucp{m}")
            e4 = s.op("vector", lambda v: v.tensor_mul(
                actb[:, :], gelb[:, :], ub[:, :]),
                waits=[e2, e3, P.get("actb")], tag=f"amul{m}")
            e5 = s.op("gpsimd", lambda g, m=m: g.tensor_copy(
                qa[:, m * T:(m + 1) * T], actb[:, :]), waits=[e4],
                tag=f"qa{m}")
            P["gb"] = e2
            P["gelb"] = e4
            P["ub"] = e4
            P["actb"] = e5
            ev_hist.append([e1, e3])
            qa_ops.append(e5)
        scg.close()

        pst = [cx.psum(f"ppf{i}", [128, 1024], F32) for i in range(2)]
        st_ops = []

        def evac_dn(m, ps, pieces, waits):
            ops = []
            for col, h0, hw in pieces_cols(pieces):
                o1 = s.op("vector", lambda v, ps=ps, col=col, hw=hw:
                          v.tensor_copy(dscr[:, col:col + hw],
                                        ps[:, col:col + hw]),
                          waits=waits + [P.get("dscr")], tag=f"dcp{m}")
                o2 = s.op("vector", lambda v, m=m, col=col, h0=h0, hw=hw:
                          v.tensor_add(
                              xsb[:, m * T + h0:m * T + h0 + hw],
                              xsb[:, m * T + h0:m * T + h0 + hw],
                              dscr[:, col:col + hw]),
                          waits=[o1, ld_s], tag=f"dadd{m}")
                ops.append(o2)
            P["dscr"] = ops
            st_ops.append(s.dma("sync", o_out[:, m * T:(m + 1) * T],
                                xsb[:, m * T:(m + 1) * T], waits=ops,
                                tag=f"stf{m}"))
            return ops

        stream_matmul(cx, "dnf", wl_t, 16, I1,
                      lambda kc, h0, hw: qa[:, kc * T + h0:kc * T + h0 + hw],
                      lambda kc: qa_ops[kc], evac_dn, PIECES_T, pst,
                      sb_dtype=FP8, first_waits=[qa_ops[1]],
                      psum_prev=[ev_hist[-2] + [qa_ops[-1]], ev_hist[-1]])
        s.op("sync", lambda e: e.nop(), waits=st_ops, tag="fin")
        s.emit(cx.stack)
    return cx.nc




def _phase_barrier(cx, tag):
    s = cx.s
    lasts = [s.ops[e][-1] for e in ENGINES if s.ops[e]]
    t = cx._fence_sb
    ops = []
    ops.append(s.op("vector", lambda e: e.memset(t[0:1, 8:12], 0.0),
                    waits=lasts, tag=f"barV{tag}"))
    ops.append(s.op("scalar", lambda e: e.memzero(t[0:1, 0:4]),
                    waits=lasts, tag=f"barS{tag}"))
    ops.append(s.op("gpsimd", lambda e: e.memset(t[0:1, 4:8], 0.0),
                    waits=lasts, tag=f"barG{tag}"))
    ops.append(s.op("tensor", lambda e: e.nop(), waits=lasts, tag=f"barT{tag}"))
    ops.append(s.op("sync", lambda e: e.nop(), waits=lasts, tag=f"bar0{tag}"))
    cx._last_fence = {}
    return ops


def ln_stream_stats(cx, c, stream, x_ready, scope, tagp):
    s = cx.s
    ntok = T
    brow = cx.sb(f"brow{tagp}", [1, 4 * ntok], F32, stack=scope)
    bc = cx.sb(f"bc{tagp}", [128, 2 * ntok], F32, stack=scope)
    sq = cx.sb(f"sq{tagp}", [128, ntok], BF16, stack=scope)
    scope2 = ExitStack()
    stat_a = cx.psum(f"statA{tagp}", [1, ntok], F32, stack=scope2)
    stat_b = cx.psum(f"statB{tagp}", [1, ntok], F32, stack=scope2)
    mm_sum = []
    mm_sq_last = None
    prev_sq = None
    for ci in range(16):
        xc = stream[:, ci * ntok:(ci + 1) * ntok]
        sop = s.op("scalar", lambda sc, b=xc: sc.activation(
            sq[:, :], b, AF.Square), waits=[x_ready, prev_sq],
            tag=f"lnsq{tagp}{ci}")
        for h0, hw in PIECES_T:
            mm_sum.append(s.op("tensor", lambda t, h0=h0, hw=hw, xc=xc, ci=ci:
                               t.matmul(stat_a[0:1, h0:h0 + hw],
                                        c["ones_col"][:, :], xc[:, h0:h0 + hw],
                                        start=(ci == 0), stop=(ci == 15)),
                               waits=[x_ready, c["ld"]], tag=f"lnA{tagp}"))
        for h0, hw in PIECES_T:
            prev_sq = s.op("tensor", lambda t, h0=h0, hw=hw, ci=ci: t.matmul(
                stat_b[0:1, h0:h0 + hw], c["ones_col"][:, :], sq[:, h0:h0 + hw],
                start=(ci == 0), stop=(ci == 15)), waits=[sop], tag=f"lnB{tagp}")
        mm_sq_last = prev_sq
    inv_d = 1.0 / D
    MEAN = brow[0:1, 0:ntok]
    RSTD = brow[0:1, ntok:2 * ntok]
    MB = brow[0:1, 2 * ntok:3 * ntok]
    SCR = brow[0:1, 3 * ntok:4 * ntok]
    o_mean = s.op("vector", lambda v: v.tensor_scalar_mul(
        MEAN, stat_a[0:1, :], inv_d), waits=[mm_sum[-1]], tag=f"mean{tagp}")
    o_m2 = s.op("scalar", lambda sc: sc.activation(SCR, MEAN, AF.Square),
                waits=[o_mean], tag=f"m2{tagp}")
    o_var = s.op("vector", lambda v: v.scalar_tensor_tensor(
        RSTD, stat_b[0:1, :], inv_d, SCR, ALU.mult, ALU.subtract),
        waits=[o_m2, mm_sq_last], tag=f"var{tagp}")
    scope2.close()
    bc_ps = cx.psum(f"bcps{tagp}", [128, ntok], F32, stack=scope)
    o_sqrt = s.op("scalar", lambda sc: sc.activation(
        RSTD, RSTD, AF.Sqrt, bias=c["eps_col"][0:1, :]), waits=[o_var],
        tag=f"sqrt{tagp}")
    o_rstd = s.op("vector", lambda v: v.reciprocal(RSTD, RSTD), waits=[o_sqrt],
                  tag=f"rstd{tagp}")
    o_mb = s.op("vector", lambda v: v.scalar_tensor_tensor(
        MB, MEAN, -1.0, RSTD, ALU.mult, ALU.mult), waits=[o_rstd], tag=f"mb{tagp}")
    bc1 = [s.op("tensor", lambda t, h0=h0, hw=hw: t.matmul(
        bc_ps[:, h0:h0 + hw], c["ones_row"][0:1, :],
        brow[0:1, ntok + h0:ntok + h0 + hw], start=True, stop=True),
        waits=[o_rstd, mm_sq_last], tag=f"bc1{tagp}") for h0, hw in PIECES_T]
    cp1 = s.op("vector", lambda v: v.tensor_copy(bc[:, 0:ntok], bc_ps[:, :]),
               waits=bc1, tag=f"bcc1{tagp}")
    bc2 = [s.op("tensor", lambda t, h0=h0, hw=hw: t.matmul(
        bc_ps[:, h0:h0 + hw], c["ones_row"][0:1, :],
        brow[0:1, 2 * ntok + h0:2 * ntok + h0 + hw], start=True, stop=True),
        waits=[o_mb, cp1], tag=f"bc2{tagp}") for h0, hw in PIECES_T]
    cp2 = s.op("vector", lambda v: v.tensor_copy(bc[:, ntok:], bc_ps[:, :]),
               waits=bc2, tag=f"bcc2{tagp}")
    return bc, [cp1, cp2]


def ln_apply_sb(cx, stream, bc, scr, out_ap, ci, waits, tagp):
    s = cx.s
    xc = stream[:, ci * T:(ci + 1) * T]
    o1 = s.op("vector", lambda v: v.tensor_mul(scr[:, :], xc, bc[:, 0:T]),
              waits=waits, tag=f"ap1{tagp}{ci}")
    o2 = s.op("vector", lambda v: v.tensor_add(out_ap, scr[:, :], bc[:, T:]),
              waits=[o1], tag=f"ap2{tagp}{ci}")
    return o2


def quant_weight_local(cx, c, wa_dram, wb_dram, rows_a, cols_a, qsb, ws_tm,
                       entry, tagp, r0=0, rw=None, shared=None):
    s = cx.s
    K = cols_a
    if rw is None:
        rw = rows_a
    nrt = rw // 128
    PW = min(1408, rw)
    sc = ExitStack()
    if shared is not None:
        rot, rotb = shared["rot"], shared["rotb"]
        wsrow, bcw = shared["wsrow"], shared["bcw"]
        pr, pb = shared["pr"], shared["pb"]
        carry = shared["carry"]
    else:
        rot = [cx.sb(f"qwra{tagp}{i}", [128, cols_a], F32, stack=sc)
               for i in range(2)]
        rotb = [cx.sb(f"qwrb{tagp}{i}", [128, PW], F32, stack=sc)
                for i in range(2)]
        wsrow = cx.sb(f"wsrow{tagp}", [1, rw], F32, stack=sc)
        bcw = cx.sb(f"bcw{tagp}", [128, PW], F32, stack=sc)
        pr = cx.psum(f"qwp{tagp}", [1, 512], F32, stack=sc)
        pb = cx.psum(f"qwb{tagp}", [128, 512], F32, stack=sc)
        carry = entry
    g_off = r0 // 128
    red_ops = []
    prev = {}
    for g in range(nrt):
        i = g % 2
        ld = s.dma("sync", rot[i][:, :],
                   wa_dram[r0 + g * 128:r0 + (g + 1) * 128, :],
                   waits=[prev.get(i), carry], tag=f"qwld{tagp}{g}")
        red = s.op("vector", lambda v, g=g, i=i: v.reduce_max(
            ws_tm[:, g_off + g:g_off + g + 1], rot[i][:, :], AX.X,
            apply_absolute_value=True), waits=[ld], tag=f"qwrd{tagp}{g}")
        prev[i] = red
        red_ops.append(red)
    prev_cp = None
    tposs = []
    for blk in range(0, nrt, 4):
        tp = []
        for j in range(min(4, nrt - blk)):
            tp.append(s.op("tensor", lambda t, blk=blk, j=j: t.transpose(
                pr[0:1, j * 128:(j + 1) * 128],
                ws_tm[:, g_off + blk + j:g_off + blk + j + 1],
                c["identf"][:, :]),
                waits=[red_ops[blk + j], prev_cp, c["ld"], carry],
                tag=f"qwt{tagp}"))
        prev_cp = s.op("vector", lambda v, blk=blk, n=min(4, nrt - blk):
                       v.tensor_copy(wsrow[0:1, blk * 128:(blk + n) * 128],
                                     pr[0:1, 0:n * 128]), waits=tp,
                       tag=f"qwtc{tagp}")
        tposs.append(prev_cp)
    rrec = s.op("vector", lambda v: v.reciprocal(wsrow[0:1, :], wsrow[0:1, :]),
                waits=tposs, tag=f"qwrec{tagp}")
    rmul = s.op("vector", lambda v: v.tensor_scalar_mul(
        wsrow[0:1, :], wsrow[0:1, :], 224.0), waits=[rrec], tag=f"qwm{tagp}")
    wss = s.op("vector", lambda v: v.tensor_scalar_mul(
        ws_tm[:, g_off:g_off + nrt], ws_tm[:, g_off:g_off + nrt], 1.0 / 224.0),
        waits=[rmul], tag=f"qws{tagp}")
    q_ops = [[] for _ in range(K // 128)]
    prevq = {}
    for p0 in range(0, rw, PW):
        pw = min(PW, rw - p0)
        bc_ops = []
        prevb = prevq.get("bc")
        for h0 in range(0, pw, 512):
            hw2 = min(512, pw - h0)
            mm = s.op("tensor", lambda t, p0=p0, h0=h0, hw2=hw2: t.matmul(
                pb[:, 0:hw2], c["ones_row"][0:1, :],
                wsrow[0:1, p0 + h0:p0 + h0 + hw2],
                start=True, stop=True), waits=[rmul, prevb], tag=f"qwbc{tagp}")
            prevb = s.op("vector", lambda v, h0=h0, hw2=hw2: v.tensor_copy(
                bcw[:, h0:h0 + hw2], pb[:, 0:hw2]),
                waits=[mm, prevq.get("q0"), prevq.get("q1")], tag=f"qwbcc{tagp}")
            bc_ops.append(prevb)
        for kc in range(K // 128):
            i = kc % 2
            ld = s.dma("sync", rotb[i][:, 0:pw],
                       wb_dram[kc * 128:(kc + 1) * 128, r0 + p0:r0 + p0 + pw],
                       waits=[prevq.get(i), carry], tag=f"qwbl{tagp}{kc}")
            qo = s.op("vector", lambda v, kc=kc, i=i, p0=p0, pw=pw: v.tensor_mul(
                qsb[:, kc * rw + p0:kc * rw + p0 + pw],
                rotb[i][:, 0:pw], bcw[:, 0:pw]),
                waits=[ld, bc_ops] + ([entry] if shared is not None else []),
                tag=f"qwq{tagp}{kc}")
            prevq[i] = qo
            prevq[f"q{i}"] = qo
            prevq["bc"] = qo
            q_ops[kc].append(qo)
    if shared is not None:
        shared["carry"] = prevq["bc"]
    sc.close()
    return q_ops, wss




def _fin_store(cx, stream, o_out, waits):
    s = cx.s
    stg = [cx.sb(f"fin{i}", [128, T], F32) for i in range(2)]
    sts = []
    prev = {}
    for m in range(16):
        i = m % 2
        cp = s.op("vector", lambda v, m=m, i=i: v.tensor_copy(
            stg[i][:, :], stream[:, m * T:(m + 1) * T]),
            waits=[waits, prev.get(i)], tag=f"fincp{m}")
        st = s.dma("sync", o_out[:, m * T:(m + 1) * T], stg[i][:, :],
                   waits=[cp], tag=f"finst{m}")
        prev[i] = st
        sts.append(st)
    s.op("sync", lambda e: e.nop(), waits=sts, tag="finfin")


def build_fused(upto=4):
    cx = Ctx()
    s = cx.s
    RG = [[0, 1, 2, 3], [4, 5, 6, 7]]
    x_in = cx.inp("x", [128, 16 * TW])
    wg1t = cx.inp("wg1t", [4, 128, D])
    wg2t = cx.inp("wg2t", [16, 128, 512])
    wut = cx.inp("wut", [16, 128, D])
    wvt = cx.inp("wvt", [16, 128, D])
    bg1 = cx.inp("bg1", [128, 4])
    bg2 = cx.inp("bg2", [128, 16])
    bu = cx.inp("bu", [128, 16])
    bv = cx.inp("bv", [128, 16])
    cw = cx.inp("cw", [128, 48])
    cb = cx.inp("cb", [128, 16])
    pmask = cx.inp("pmask", [128, 4])
    wgba = cx.inp("wgba", [I0, D])
    wgbt = cx.inp("wgbt", [D, I0])
    wuba = cx.inp("wuba", [I0, D])
    wubt = cx.inp("wubt", [D, I0])
    wdna = cx.inp("wdna", [D, I0])
    wdnt = cx.inp("wdnt", [I0, D])
    wq_t = cx.inp("wq_t", [16, 128, D])
    wk_t = cx.inp("wk_t", [16, 128, D])
    wv_nat = cx.inp("wv_nat", [128, 16 * 1024])
    wgk1p = cx.inp("wgk1p", [128, 256])
    wgk2 = cx.inp("wgk2", [16, D])
    bgk2 = cx.inp("bgk2", [128, 16])
    selm = cx.inp("selm", [128, 4])
    wg_nat = cx.inp("wg_nat", [128, 16 * 1024])
    wnorm = cx.inp("wnorm", [1, T])
    wo_t = cx.inp("wo_t", [16, 128, 1024])
    wg0_t = cx.inp("wg0_t", [64, 128, D])
    wg1f_t = cx.inp("wg1f_t", [64, 128, D])
    wl_t = cx.inp("wl_t", [16, 128, I1])
    o_out = cx.out("out", [128, 16 * T])
    tot_in = cx.nc.dram_tensor("tot_in", [128, 16], F32, kind="Internal").ap()
    tot_out = cx.nc.dram_tensor("tot_out", [512, 16], F32, kind="Internal").ap()
    sp_in = cx.nc.dram_tensor("sp_in", [128, 4112], F32, kind="Internal").ap()
    sp_out = cx.nc.dram_tensor("sp_out", [512, 4112], F32, kind="Internal").ap()
    tdram = cx.nc.dram_tensor("tdram", [128, 44 * T], BF16, kind="Internal").ap()
    qedram = cx.nc.dram_tensor("qedram", [128, 16 * T], BF16, kind="Internal").ap()

    with cx.stack:
        c = std_consts(cx, ident=True, mask128=True)
        tril, trl = cx.const_sb("trilc", np.triu(np.ones((128, 128), bf)))
        m8 = np.ones((128, 32), np.float32)
        m8[:, ::8] = 0.0
        mask8, m8l = cx.const_sb("mask8c", m8)
        onec, onl = cx.const_sb("onecol", np.ones((128, 1), np.float32))
        c["ld"] += [trl, m8l, onl]
        stream = cx.sb("stream", [128, 16 * T], BF16)

        p1 = ExitStack()
        cx.cur = p1
        xb1 = cx.sb("xb1", [128, 16 * TW], BF16)
        hb = cx.sb("hb", [128, 4 * T], BF16)
        gates = cx.sb("gat", [128, 16 * T], BF16)
        ug = cx.sb("ug1", [128, 16 * T], BF16)
        tot = cx.sb("tot_sb", [128, 16], F32)
        tots = cx.sb("tots", [128, 64], F32)
        pfx = cx.sb("pfx", [128, 16], F32)
        pm = cx.sb("pm", [128, 4], F32)
        vg = cx.sb("vg", [128, T], BF16)
        kc_t = cx.sb("kc_t", [128, T], BF16)
        zero_b = cx.sb("zerob", [128, T], BF16)
        t32 = cx.sb("t32", [128, T], F32)
        sm, small_ld = cx.smalls([("bg1", bg1), ("bg2", bg2), ("bu", bu),
                                  ("bv", bv), ("cw", cw), ("cb", cb)])
        slab_big = {"tiles": [cx.sb(f"slab_big{i}", [128, D], BF16)
                              for i in range(2)], "lastmm": [None, None]}
        slab_gt = {"tiles": [cx.sb(f"slab_gt{i}", [128, 512], BF16)
                             for i in range(2)], "lastmm": [None, None]}
        ld_pm = s.dma("sync", pm[:, :], pmask, tag="ldpm")
        ld_stream = s.dma("gpsimd",
                          stream[:, :].rearrange("p (ci t) -> p ci t", ci=16),
                          x_in[:, :].rearrange("p (ci t) -> p ci t", ci=16)[:, :, 1:1 + T],
                          tag="ldstream")
        z0 = s.op("vector", lambda v: v.memset(zero_b[:, :], 0.0), tag="z0")
        apps = ln_hbm(cx, c, x_in, xb1, TW, ((0, 512), (512, 512), (1024, 2)),
                      tagp="p1")
        conv_done = []
        prev = None
        for ci in range(16):
            xc = xb1[:, ci * TW:(ci + 1) * TW]
            stc = stream[:, ci * T:(ci + 1) * T]
            o1 = s.op("vector", lambda v, xc=xc, ci=ci: v.tensor_scalar_mul(
                t32[:, :], xc[:, 0:T], sm["cw"][:, 3 * ci:3 * ci + 1]),
                waits=[apps[ci], prev, small_ld], tag=f"cv1_{ci}")
            o2 = s.op("vector", lambda v, xc=xc, ci=ci: v.scalar_tensor_tensor(
                t32[:, :], xc[:, 1:1 + T], sm["cw"][:, 3 * ci + 1:3 * ci + 2],
                t32[:, :], ALU.mult, ALU.add), waits=[o1], tag=f"cv2_{ci}")
            o3 = s.op("vector", lambda v, xc=xc, ci=ci: v.scalar_tensor_tensor(
                t32[:, :], xc[:, 2:2 + T], sm["cw"][:, 3 * ci + 2:3 * ci + 3],
                t32[:, :], ALU.mult, ALU.add), waits=[o2], tag=f"cv3_{ci}")
            o4 = s.op("vector", lambda v, stc=stc, ci=ci: v.scalar_tensor_tensor(
                stc, t32[:, :], sm["cb"][:, ci:ci + 1], stc, ALU.add, ALU.add),
                waits=[o3, ld_stream], tag=f"cv4_{ci}")
            prev = o4
            conv_done.append(o4)

        pst = [cx.psum(f"pp{i}", [128, 1024], F32) for i in range(2)]

        def rhs_x1(kcc, h0, hw):
            return xb1[:, kcc * TW + 1 + h0: kcc * TW + 1 + h0 + hw]

        def evac_h(m, ps, pieces, waits):
            ops = []
            for col, h0, hw in pieces_cols(pieces):
                ops.append(s.op("scalar", lambda sc2, m=m, ps=ps, col=col,
                                h0=h0, hw=hw: sc2.activation(
                                    hb[:, m * T + h0:m * T + h0 + hw],
                                    ps[:, col:col + hw], AF.Relu,
                                    bias=sm["bg1"][:, m:m + 1]),
                                waits=waits + [small_ld], tag=f"evh{m}"))
            return ops

        ev_h = stream_matmul(cx, "h", wg1t, 4, D, rhs_x1,
                             lambda kcc: apps[kcc], evac_h, PIECES_T, pst,
                             slab_pair=slab_big)

        def evac_g(m, ps, pieces, waits):
            ops = []
            for col, h0, hw in pieces_cols(pieces):
                ops.append(s.op("scalar", lambda sc2, m=m, ps=ps, col=col,
                                h0=h0, hw=hw: sc2.activation(
                                    gates[:, m * T + h0:m * T + h0 + hw],
                                    ps[:, col:col + hw], AF.Sigmoid,
                                    bias=sm["bg2"][:, m:m + 1]),
                                waits=waits, tag=f"evg{m}"))
            return ops

        ev_g = stream_matmul(cx, "gt", wg2t, 16, 512,
                             lambda kcc, h0, hw: hb[:, kcc * T + h0:kcc * T + h0 + hw],
                             lambda kcc: ev_h[kcc], evac_g, PIECES_T, pst,
                             psum_prev=[ev_h[2], ev_h[3]],
                             slab_pair=slab_gt, rhs_cover=ev_h[3])

        ug_done = []

        def evac_u(m, ps, pieces, waits):
            ops = []
            for col, h0, hw in pieces_cols(pieces):
                ops.append(s.op("vector", lambda v, m=m, ps=ps, col=col, h0=h0,
                                hw=hw: v.scalar_tensor_tensor(
                                    ug[:, m * T + h0:m * T + h0 + hw],
                                    ps[:, col:col + hw], sm["bu"][:, m:m + 1],
                                    gates[:, m * T + h0:m * T + h0 + hw],
                                    ALU.add, ALU.mult),
                                waits=waits + [ev_g[m], small_ld],
                                tag=f"evu{m}"))
            red = s.op("vector", lambda v, m=m: v.reduce_sum(
                tot[:, m:m + 1], ug[:, m * T:(m + 1) * T], AX.X), waits=ops,
                tag=f"totr{m}")
            ug_done.append(red)
            return ops + [red]

        stream_matmul(cx, "u", wut, 16, D, rhs_x1, lambda kcc: apps[kcc],
                      evac_u, PIECES_T, pst, psum_prev=[ev_g[14], ev_g[15]],
                      slab_pair=slab_big, rhs_cover=apps[-1])

        st_tot = s.dma("sync", tot_in, tot[:, :], waits=ug_done, tag="sttot")
        if SIM1:
            cc1 = [s.dma("gpsimd", tot_out[j * 128:(j + 1) * 128, :], tot_in,
                         waits=[st_tot], tag=f"cc1sim{j}") for j in range(4)]
        else:
            cc1 = s.op("gpsimd", lambda g: g.collective_compute(
                "AllGather", ALU.bypass, replica_groups=RG, ins=[tot_in.opt()],
                outs=[tot_out.opt()]), waits=[st_tot], async_inc=1, tag="cc1")
        ld_t = s.dma("sync", tots[:, :].rearrange("p (g m) -> p g m", g=4),
                     tot_out.rearrange("(g p) m -> p g m", p=128),
                     waits=[cc1], tag="ldtot")
        o = s.op("vector", lambda v: v.memset(pfx[:, :], 0.0), tag="pfx0")
        for j in range(4):
            o = s.op("vector", lambda v, j=j, o=o: v.scalar_tensor_tensor(
                pfx[:, :], tots[:, j * 16:(j + 1) * 16], pm[:, j:j + 1],
                pfx[:, :], ALU.mult, ALU.add), waits=[ld_t, ld_pm, o],
                tag=f"pfx{j}")
        pfx_done = o

        go = gates
        go_ops = [None] * 16

        def evac_v(m, ps, pieces, waits):
            ops = []
            for col, h0, hw in pieces_cols(pieces):
                ops.append(s.op("vector", lambda v, m=m, ps=ps, col=col, h0=h0,
                                hw=hw: v.scalar_tensor_tensor(
                                    vg[:, h0:h0 + hw], ps[:, col:col + hw],
                                    sm["bv"][:, m:m + 1],
                                    gates[:, m * T + h0:m * T + h0 + hw],
                                    ALU.add, ALU.mult),
                                waits=waits + [small_ld,
                                               go_ops[m - 1] if m >= 1 else None],
                                tag=f"evv{m}"))
            scn = s.op("vector", lambda v, m=m: v.tensor_tensor_scan(
                kc_t[:, :], ug[:, m * T:(m + 1) * T], zero_b[:, :], 0.0,
                ALU.add, ALU.add),
                waits=[z0, pfx_done, ops, go_ops[m - 1] if m >= 1 else None],
                tag=f"scan{m}")
            scn2 = s.op("vector", lambda v, m=m: v.tensor_scalar_add(
                kc_t[:, :], kc_t[:, :], pfx[:, m:m + 1]), waits=[scn],
                tag=f"scanp{m}")
            gop = s.op("vector", lambda v, m=m: v.tensor_mul(
                go[:, m * T:(m + 1) * T], kc_t[:, :], vg[:, :]), waits=[scn2],
                tag=f"go{m}")
            go_ops[m] = gop
            return ops + [scn2, gop]

        ev_v = stream_matmul(cx, "v", wvt, 16, D, rhs_x1,
                             lambda kcc: apps[kcc], evac_v, PIECES_T, pst,
                             slab_pair=slab_big, rhs_cover=apps[-1])

        p1_stream = []

        def evac_uv(m, ps, pieces, waits):
            ops = []
            for col, h0, hw in pieces_cols(pieces):
                ops.append(s.op("vector", lambda v, m=m, ps=ps, col=col, h0=h0,
                                hw=hw: v.tensor_add(
                                    stream[:, m * T + h0:m * T + h0 + hw],
                                    stream[:, m * T + h0:m * T + h0 + hw],
                                    ps[:, col:col + hw]),
                                waits=waits + [conv_done], tag=f"evuv{m}"))
            p1_stream.extend(ops)
            return ops

        stream_matmul(cx, "uv", wut, 16, D,
                      lambda kcc, h0, hw: go[:, kcc * T + h0:kcc * T + h0 + hw],
                      lambda kcc: go_ops[kcc], evac_uv, PIECES_T, pst,
                      psum_prev=[ev_v[14], ev_v[15]], slab_pair=slab_big,
                      rhs_cover=go_ops[15])
        cx.cur = None
        p1.close()
        bar1 = _phase_barrier(cx, "b1")
        if upto == 1:
            _fin_store(cx, stream, o_out, bar1)
            s.emit(cx.stack)
            return cx.nc

        p2 = ExitStack()
        cx.cur = p2
        Mt = cx.sb("Mt2", [128, T], F32)
        wsd_tm = cx.sb("wsdtm", [128, 16], F32)
        p2a = ExitStack()
        cx.cur = p2a
        qx2 = cx.sb("qx2", [128, 16 * T], FP8)
        Mx = cx.sb("Mx2", [128, T], F32)
        absx = cx.sb("absx2", [128, T], F32)
        xfr = cx.sb("xfr2", [128, T], F32)
        wsg_tm = cx.sb("wsgtm", [128, 44], F32)
        wsu_tm = cx.sb("wsutm", [128, 44], F32)
        rmx_tm = cx.sb("rmxtm", [128, 32], F32)
        s448x = cx.sb("s448x2", [128, T], F32)
        r448x = cx.sb("r448x2", [128, T], F32)
        ln2 = ExitStack()
        bc2, bc2_ops = ln_stream_stats(cx, c, stream, bar1, ln2, "l2")
        scr2 = cx.sb("scr2l", [128, T], F32, stack=ln2)
        mops = []
        prevm = None
        for ci in range(16):
            a2 = ln_apply_sb(cx, stream, bc2, scr2, xfr[:, :], ci,
                             [bc2_ops, prevm], "x1")
            ab = s.op("scalar", lambda sc2: sc2.activation(
                absx[:, :], xfr[:, :], AF.Abs), waits=[a2, prevm], tag=f"abx{ci}")
            if ci == 0:
                prevm = s.op("vector", lambda v: v.tensor_copy(
                    Mx[:, :], absx[:, :]), waits=[ab], tag="mx0")
            else:
                prevm = s.op("vector", lambda v: v.tensor_max(
                    Mx[:, :], Mx[:, :], absx[:, :]), waits=[ab, prevm],
                    tag=f"mx{ci}")
            mops.append(prevm)
        _, _, bcx_rdy = rowmax_to_bcs(cx, c, Mx, mops[-1], c["identf"],
                                      tagp="x2", pre=(rmx_tm, s448x, r448x))
        q_ops = []
        prevq = None
        for ci in range(16):
            a2 = ln_apply_sb(cx, stream, bc2, scr2, xfr[:, :], ci,
                             [bcx_rdy, prevq], "x2")
            qo = s.op("vector", lambda v, ci=ci: v.tensor_mul(
                qx2[:, ci * T:(ci + 1) * T], xfr[:, :], s448x[:, :]),
                waits=[a2], tag=f"qx{ci}")
            prevq = qo
            q_ops.append(qo)
        ln2.close()

        QW = 1408
        pgu = ExitStack()
        qgsb = cx.sb("qgsb", [128, 16 * QW], FP8, stack=pgu)
        qusb = cx.sb("qusb", [128, 16 * QW], FP8, stack=pgu)
        psG = cx.psum("psG", [128, 1024], F32, stack=pgu)
        psU = cx.psum("psU", [128, 1024], F32, stack=pgu)
        g32 = cx.sb("g32", [128, T], F32, stack=pgu)
        sg32 = cx.sb("sg32", [128, T], F32, stack=pgu)
        u32 = cx.sb("u32", [128, T], F32, stack=pgu)
        tb = [cx.sb(f"tb{i}", [128, T], BF16, stack=pgu) for i in range(2)]
        abst = cx.sb("abst", [128, T], BF16, stack=pgu)
        ev_hist = {}
        mm_hist = {}
        st_hist = {}
        mt_ops = []
        sil_hist = {}
        qprev = [q_ops[-1]]
        qsh = {"rot": [cx.sb(f"qshra{i}", [128, D], F32, stack=pgu)
                       for i in range(2)],
               "rotb": [cx.sb(f"qshrb{i}", [128, QW], F32, stack=pgu)
                        for i in range(2)],
               "wsrow": cx.sb("qshws", [1, QW], F32, stack=pgu),
               "bcw": cx.sb("qshbc", [128, QW], F32, stack=pgu),
               "pr": cx.psum("qshpr", [1, 512], F32, stack=pgu),
               "pb": cx.psum("qshpb", [128, 512], F32, stack=pgu),
               "carry": [q_ops[-1]]}
        for qt4 in range(4):
            r0 = qt4 * QW
            qg_ops, wsg_rdy = quant_weight_local(
                cx, c, wgba, wgbt, I0, D, qgsb, wsg_tm, qprev, f"qg{qt4}",
                r0=r0, rw=QW, shared=qsh)
            qu_ops, wsu_rdy = quant_weight_local(
                cx, c, wuba, wubt, I0, D, qusb, wsu_tm, qprev, f"qu{qt4}",
                r0=r0, rw=QW, shared=qsh)
            qx2r = qx2[:, :].rearrange("p (c t) -> p c t", c=16)
            qgr = qgsb[:, :].rearrange("p (c q) -> p c q", c=16)
            qur = qusb[:, :].rearrange("p (c q) -> p c q", c=16)
            for ml in range(11):
                m = qt4 * 11 + ml
                i = m % 2
                evp = ev_hist.get(m - 1, [])
                lastG = lastU = None
                for ps, qr, qrdy in ((psG, qgr, qg_ops),
                                     (psU, qur, qu_ops)):
                    for col, h0, hw in pieces_cols(PIECES_T):
                        for kp in range(8):
                            wl = ([qrdy[15], evp]
                                  if (kp == 0 and col == 0) else [])
                            last = s.op("tensor", lambda t, ps=ps, qr=qr,
                                        kp=kp, ml=ml, col=col, h0=h0, hw=hw:
                                        t.matmul(
                                            ps[:, col:col + hw],
                                            qr[:, 2 * kp:2 * kp + 2, ml * 128:(ml + 1) * 128],
                                            qx2r[:, 2 * kp:2 * kp + 2, h0:h0 + hw],
                                            start=(kp == 0), stop=(kp == 7),
                                            perf_mode=DR),
                                        waits=wl, tag=f"c1mm{m}")
                    if ps is psG:
                        lastG = last
                    else:
                        lastU = last
                mm_hist[m] = last
                eprev = ev_hist.get(m - 1, [None])[-1]
                sprev = sil_hist.get(m - 1)
                e1 = s.op("vector", lambda v, m=m: v.scalar_tensor_tensor(
                    g32[:, :], psG[:, :], wsg_tm[:, m:m + 1], r448x[:, :],
                    ALU.mult, ALU.mult),
                    waits=[lastG, wsg_rdy, bcx_rdy, eprev, sprev],
                    tag=f"evg2{m}")
                e2 = s.op("scalar", lambda sc2: sc2.activation(
                    sg32[:, :], g32[:, :], AF.Silu), waits=[e1, eprev],
                    tag=f"sil{m}")
                sil_hist[m] = e2
                e3 = s.op("vector", lambda v, m=m: v.scalar_tensor_tensor(
                    u32[:, :], psU[:, :], wsu_tm[:, m:m + 1], r448x[:, :],
                    ALU.mult, ALU.mult), waits=[lastU, wsu_rdy, eprev],
                    tag=f"evu2{m}")
                e4 = s.op("vector", lambda v, i=i: v.tensor_mul(
                    tb[i][:, :], sg32[:, :], u32[:, :]),
                    waits=[e2, e3, st_hist.get(m - 2)], tag=f"tmul{m}")
                ab = s.op("scalar", lambda sc2, i=i: sc2.activation(
                    abst[:, :], tb[i][:, :], AF.Abs),
                    waits=[e4, mt_ops[-1] if mt_ops else None], tag=f"abt{m}")
                if m == 0:
                    e5 = s.op("vector", lambda v: v.tensor_copy(
                        Mt[:, :], abst[:, :]), waits=[ab], tag=f"mt{m}")
                else:
                    e5 = s.op("vector", lambda v: v.tensor_max(
                        Mt[:, :], Mt[:, :], abst[:, :]), waits=[ab, mt_ops[-1]],
                        tag=f"mt{m}")
                mt_ops.append(e5)
                st = s.dma("sync", tdram[:, m * T:(m + 1) * T], tb[i][:, :],
                           waits=[e4], tag=f"stt{m}")
                st_hist[m] = st
                ev_hist[m] = [e1, e3]
            qprev = ev_hist[qt4 * 11 + 10] + [mm_hist[qt4 * 11 + 10]]
        t_stores = list(st_hist.values())
        cx.cur = None
        pgu.close()
        p2a.close()
        cx.cur = p2
        bar2a = _phase_barrier(cx, "b2a")
        s448t = cx.sb("s448t2", [128, T], F32)
        r448t = cx.sb("r448t2", [128, T], F32)
        rmt_tm = cx.sb("rmttm", [128, 32], F32)
        _, _, bct_rdy = rowmax_to_bcs(cx, c, Mt, [mt_ops[-1], bar2a],
                                      c["identf"], tagp="t2",
                                      pre=(rmt_tm, s448t, r448t))
        qt = cx.sb("qt2", [128, 44 * T], FP8)
        trot = [cx.sb(f"trot{i}", [128, T], BF16) for i in range(2)]
        qt_ops = []
        prevt = {}
        for mi in range(44):
            i = mi % 2
            ldt = s.dma("sync", trot[i][:, :], tdram[:, mi * T:(mi + 1) * T],
                        waits=[prevt.get(i), st_hist[mi], bar2a], tag=f"ldt{mi}")
            qo = s.op("vector", lambda v, mi=mi, i=i: v.tensor_mul(
                qt[:, mi * T:(mi + 1) * T], trot[i][:, :], s448t[:, :]),
                waits=[ldt, bct_rdy], tag=f"qt{mi}")
            prevt[i] = qo
            qt_ops.append(qo)
        dscr = cx.sb("dscr2", [128, T], F32)
        p2_stream = []
        DHW = 1024
        pdn = ExitStack()
        qdsb = cx.sb("qdsb", [128, 44 * DHW], FP8, stack=pdn)
        pst2 = [cx.psum(f"pp2{i}", [128, 1024], F32, stack=pdn) for i in range(2)]
        dn_hist = {}
        dprev = [bar2a, qt_ops[-1]]
        for hh in range(2):
            qd_ops, wsd_rdy = quant_weight_local(
                cx, c, wdna, wdnt, D, I0, qdsb, wsd_tm, dprev, f"qd{hh}",
                r0=hh * DHW, rw=DHW)
            qdr = qdsb[:, :].rearrange("p (c t) -> p c t", c=44)
            qtr = qt[:, :].rearrange("p (c t) -> p c t", c=44)
            for ml in range(8):
                m = hh * 8 + ml
                i = m % 2
                evp = dn_hist.get(m - 2, [bar2a])
                last = None
                for col, h0, hw in pieces_cols(PIECES_T):
                    for kp in range(22):
                        wl = ([qd_ops[43], evp]
                              if (kp == 0 and col == 0) else [])
                        last = s.op("tensor", lambda t, i=i, kp=kp, ml=ml,
                                    col=col, h0=h0, hw=hw: t.matmul(
                                        pst2[i][:, col:col + hw],
                                        qdr[:, 2 * kp:2 * kp + 2, ml * 128:(ml + 1) * 128],
                                        qtr[:, 2 * kp:2 * kp + 2, h0:h0 + hw],
                                        start=(kp == 0), stop=(kp == 21),
                                        perf_mode=DR),
                                    waits=wl, tag=f"dnmm{m}")
                ops = []
                for col, h0, hw in pieces_cols(PIECES_T):
                    o1 = s.op("vector", lambda v, i=i, m=m, col=col, hw=hw:
                              v.scalar_tensor_tensor(
                                  dscr[:, col:col + hw],
                                  pst2[i][:, col:col + hw],
                                  wsd_tm[:, m:m + 1], r448t[:, col:col + hw],
                                  ALU.mult, ALU.mult),
                              waits=[last, wsd_rdy, bct_rdy,
                                     dn_hist.get(m - 1, [None])[-1]],
                              tag=f"evd{m}")
                    o2 = s.op("vector", lambda v, m=m, col=col, h0=h0, hw=hw:
                              v.tensor_add(
                                  stream[:, m * T + h0:m * T + h0 + hw],
                                  stream[:, m * T + h0:m * T + h0 + hw],
                                  dscr[:, col:col + hw]),
                              waits=[o1, p1_stream], tag=f"evd2{m}")
                    ops.append(o2)
                dn_hist[m] = ops
                p2_stream.extend(ops)
            dprev = dn_hist[hh * 8 + 7]
        cx.cur = None
        pdn.close()
        p2.close()
        bar2 = _phase_barrier(cx, "b2")
        if upto == 2:
            _fin_store(cx, stream, o_out, bar2)
            s.emit(cx.stack)
            return cx.nc

        p3 = ExitStack()
        cx.cur = p3
        xb3 = cx.sb("xb3", [128, 16 * T], BF16)
        o_sb = cx.sb("o_sb3", [128, 8 * T], BF16)
        spack = cx.sb("spack3", [128, 4096], F32)
        a_sb = cx.sb("a_sb3", [128, 16], F32)
        cc_sb = cx.sb("cc_sb3", [128, 128], F32)
        sel = cx.sb("sel3", [128, 4], F32)
        ld_sel = s.dma("sync", sel[:, :], selm, waits=[bar2], tag="ldsel")
        ln3 = ExitStack()
        bc3, bc3_ops = ln_stream_stats(cx, c, stream, bar2, ln3, "l3")
        scr3 = cx.sb("scr3l", [128, T], F32, stack=ln3)
        apps = []
        prevq = None
        for ci in range(16):
            a2 = ln_apply_sb(cx, stream, bc3, scr3,
                             xb3[:, ci * T:(ci + 1) * T], ci,
                             [bc3_ops, prevq], "x3")
            prevq = a2
            apps.append(a2)
        ln3.close()

        d_sub = ExitStack()
        cx.cur = d_sub
        v_tm = cx.sb("v_tm3", [128, 8 * T], BF16)
        sb_S = cx.sb("sbS3", [128, 1024], BF16)
        fm1 = cx.sb("fm13", [16, T], BF16)
        qe_h = cx.sb("qe_h3", [128, 4 * T], BF16)
        kd_h = cx.sb("kd_h3", [128, 4 * T], BF16)
        kdtm = cx.sb("kdtm3", [128, 8 * 512], BF16)
        ebt = cx.sb("ebt3", [128, T], BF16)
        embt = cx.sb("embt3", [128, T], BF16)
        gkp = cx.sb("gkp3", [128, T], F32)
        sA = cx.sb("sA3", [128, T], F32)
        sB = cx.sb("sB3", [128, T], F32)
        m0t = cx.sb("m0t3", [128, T], F32)
        b_t = cx.sb("b_t3", [128, T], F32)
        zs = cx.sb("zs3", [128, 32], F32)
        bl = cx.sb("bl3", [128, 32], F32)
        zsh = cx.sb("zsh3", [128, 32], F32)
        wexp = cx.sb("wexp3", [128, 32], F32)
        ebias = cx.sb("ebias3", [128, 8], F32)
        asb = cx.sb("asb3", [128, 128], BF16)
        misc = cx.sb("misc3", [128, 64], F32)
        slq = [cx.sb("slq3", [128, D], BF16)] * 2
        wgk1_sb = cx.sb("wgk1sb3", [128, 256], BF16)
        wgk2_sb = cx.sb("wgk2sb3", [16, 512], BF16)
        smv, small_ld3 = cx.smalls([("bgk2", bgk2)])
        ld1 = s.dma("gpsimd", wgk1_sb[:, :], wgk1p, waits=[bar2], tag="ldgk1")
        scope_v = ExitStack()
        wv_sb = cx.sb("wv_sb3", [128, 16 * 1024], BF16, stack=scope_v)
        psv = [cx.psum(f"psv3{i}", [128, 1024], F32, stack=scope_v) for i in range(2)]
        psf = cx.psum("psf3", [16, T], F32, stack=scope_v)
        ldv = s.dma("gpsimd", wv_sb[:, :], wv_nat, waits=[apps], tag="ldwv")
        v_ev = []
        for tc in range(8):
            i = tc % 2
            last = None
            for col, h0, hw in pieces_cols(PIECES_T):
                for kc in range(16):
                    wl = ([ldv, apps[-1], v_ev[tc - 2] if tc >= 2 else None]
                          if (kc == 0 and col == 0) else [])
                    last = s.op("tensor", lambda t, i=i, kc=kc, tc=tc, col=col,
                                h0=h0, hw=hw: t.matmul(
                                    psv[i][:, col:col + hw],
                                    xb3[:, kc * T + tc * 128:kc * T + tc * 128 + 128],
                                    wv_sb[:, kc * 1024 + h0:kc * 1024 + h0 + hw],
                                    start=(kc == 0), stop=(kc == 15)),
                                waits=wl, tag=f"vmm3{tc}")
            ev = s.op("vector", lambda v, i=i, tc=tc: v.tensor_copy(
                v_tm[:, tc * T:(tc + 1) * T], psv[i][:, :]), waits=[last],
                tag=f"vev3{tc}")
            v_ev.append(ev)
        lastf = None
        for col, h0, hw in pieces_cols(PIECES_T):
            for kc in range(16):
                lastf = s.op("tensor", lambda t, kc=kc, col=col, h0=h0, hw=hw:
                             t.matmul(psf[0:16, col:col + hw],
                                      wgk1_sb[:, kc * 16:(kc + 1) * 16],
                                      xb3[:, kc * T + h0:kc * T + h0 + hw],
                                      start=(kc == 0), stop=(kc == 15)),
                             waits=([ld1, apps[-1], v_ev[-1]]
                                    if (kc == 0 and col == 0) else []),
                             tag="fmm3")
        evf = s.op("vector", lambda v: v.tensor_copy(fm1[:, :], psf[0:16, :]),
                   waits=[lastf], tag="fev3")
        scope_v.close()

        psqk = cx.psum("psqk3", [128, 1024], F32)
        psA = cx.psum("psA3", [128, 128], F32)
        psO = cx.psum("psO3", [128, 256], F32)
        psOb = cx.psum("psOb3", [128, 128], BF16)
        psT = cx.psum("psT3", [128, 1024], F32)
        qk_last_ev = [evf]
        P = {}
        st_qe = [None] * 4
        o_cp_last = None
        final_ops = []
        ld2p = None
        for h in range(4):
            ld2 = s.dma("gpsimd", wgk2_sb[:, :], wgk2[:, h * 512:(h + 1) * 512],
                        waits=[bar2, ld2p], tag=f"ldgk2{h}")
            mq_done = []
            for mq in range(4):
                gq = 4 * h + mq
                lastg = None
                for col, h0, hw in pieces_cols(PIECES_T):
                    lastg = s.op("tensor", lambda t, mq=mq, col=col, h0=h0,
                                 hw=hw: t.matmul(
                                     psqk[:, col:col + hw],
                                     wgk2_sb[0:16, mq * 128:(mq + 1) * 128],
                                     fm1[:, h0:h0 + hw], start=True, stop=True),
                                 waits=[ld2, evf, qk_last_ev], tag=f"gkmm{gq}")
                e0 = s.op("vector", lambda v, gq=gq: v.tensor_scalar_add(
                    gkp[:, :], psqk[:, :], smv["bgk2"][:, gq:gq + 1]),
                    waits=[lastg, small_ld3, P.get("gkp")], tag=f"gk0{gq}")
                qk_last_ev = [e0]
                e1 = s.op("vector", lambda v: v.tensor_scalar_min(
                    m0t[:, :], gkp[:, :], 0.0), waits=[e0, P.get("m0")],
                    tag=f"gk1{gq}")
                e2 = s.op("scalar", lambda sc2: sc2.activation(
                    sA[:, :], gkp[:, :], AF.Abs), waits=[e0, P.get("sA")],
                    tag=f"gk2{gq}")
                e3 = s.op("scalar", lambda sc2: sc2.activation(
                    sB[:, :], sA[:, :], AF.Exp, scale=-1.0),
                    waits=[e2, P.get("sB")], tag=f"gk3{gq}")
                e4 = s.op("scalar", lambda sc2: sc2.activation(
                    sA[:, :], sB[:, :], AF.Ln, bias=onec[:, 0:1]),
                    waits=[e3], tag=f"gk4{gq}")
                e5 = s.op("vector", lambda v: v.scalar_tensor_tensor(
                    m0t[:, :], sA[:, :], -1.0, m0t[:, :], ALU.mult, ALU.add),
                    waits=[e4, e1], tag=f"gk5{gq}")
                P["sA"] = e5
                P["sB"] = e4
                P["m0"] = e5
                scn = s.op("vector", lambda v: v.tensor_tensor_scan(
                    b_t[:, :], c["mask128"][:, :], m0t[:, :], 0.0,
                    ALU.mult, ALU.add), waits=[e5, P.get("b")], tag=f"gk6{gq}")
                P["gkp"] = [e1, e2]
                x1 = s.op("vector", lambda v, mq=mq: v.tensor_copy(
                    zs[:, mq * 8:(mq + 1) * 8].rearrange("p (cc o) -> p cc o", o=1),
                    b_t[:, :].rearrange("p (cc t) -> p cc t", cc=8)[:, :, 64:65]),
                    waits=[scn, P.get("zsg")], tag=f"zs{gq}")
                x2 = s.op("vector", lambda v, mq=mq: v.tensor_copy(
                    bl[:, mq * 8:(mq + 1) * 8].rearrange("p (cc o) -> p cc o", o=1),
                    b_t[:, :].rearrange("p (cc t) -> p cc t", cc=8)[:, :, 127:128]),
                    waits=[scn, P.get("blg")], tag=f"bl{gq}")
                x3 = s.op("vector", lambda v, mq=mq: v.tensor_scalar(
                    ebias[:, :], zs[:, mq * 8:(mq + 1) * 8], -1.0 / 16.0,
                    LNSCALE_DK, ALU.mult, ALU.add), waits=[x1, P.get("ebias")],
                    tag=f"ebias{gq}")
                ebo = None
                for ch in range(8):
                    ebo = s.op("scalar", lambda sc2, ch=ch: sc2.activation(
                        ebt[:, ch * 128:(ch + 1) * 128],
                        b_t[:, ch * 128:(ch + 1) * 128], AF.Exp,
                        scale=1.0 / 16.0, bias=ebias[:, ch:ch + 1]),
                        waits=[x3, scn, P.get("ebt")], tag=f"eb{gq}")
                x4 = s.op("vector", lambda v, mq=mq: v.tensor_scalar_mul(
                    ebias[:, :], zs[:, mq * 8:(mq + 1) * 8], 1.0 / 16.0),
                    waits=[ebo], tag=f"embias{gq}")
                emo = None
                for ch in range(8):
                    emo = s.op("scalar", lambda sc2, ch=ch: sc2.activation(
                        embt[:, ch * 128:(ch + 1) * 128],
                        b_t[:, ch * 128:(ch + 1) * 128], AF.Exp,
                        scale=-1.0 / 16.0, bias=ebias[:, ch:ch + 1]),
                        waits=[x4, P.get("embt")], tag=f"em{gq}")
                P["b"] = [emo, x1, x2]
                P["ebias"] = emo
                i2 = mq % 2
                ldq = s.dma("gpsimd", slq[i2][:, :], wq_t[gq],
                            waits=[P.get(("sl", i2)), bar2], tag=f"ldq3{gq}")
                lastq = None
                for col, h0, hw in pieces_cols(PIECES_T):
                    for kc in range(16):
                        wl = ([ldq, apps[-1], qk_last_ev]
                              if (kc == 0 and col == 0) else [])
                        lastq = s.op("tensor", lambda t, i2=i2, kc=kc, col=col,
                                     h0=h0, hw=hw: t.matmul(
                                         psqk[:, col:col + hw],
                                         slq[i2][:, kc * 128:(kc + 1) * 128],
                                         xb3[:, kc * T + h0:kc * T + h0 + hw],
                                         start=(kc == 0), stop=(kc == 15)),
                                     waits=wl, tag=f"qmm{gq}")
                qev = s.op("vector", lambda v, mq=mq: v.tensor_mul(
                    qe_h[:, mq * T:(mq + 1) * T], psqk[:, :], ebt[:, :]),
                    waits=[lastq, ebo, st_qe[h - 1] if h > 0 else None,
                           P.get(("qe", mq))], tag=f"qev{gq}")
                qk_last_ev = [qev]
                ldk = s.dma("gpsimd", slq[i2][:, :], wk_t[gq],
                            waits=[lastq], tag=f"ldk3{gq}")
                lastk = None
                for col, h0, hw in pieces_cols(PIECES_T):
                    for kc in range(16):
                        wl = ([ldk, qk_last_ev]
                              if (kc == 0 and col == 0) else [])
                        lastk = s.op("tensor", lambda t, i2=i2, kc=kc, col=col,
                                     h0=h0, hw=hw: t.matmul(
                                         psqk[:, col:col + hw],
                                         slq[i2][:, kc * 128:(kc + 1) * 128],
                                         xb3[:, kc * T + h0:kc * T + h0 + hw],
                                         start=(kc == 0), stop=(kc == 15)),
                                     waits=wl, tag=f"kmm{gq}")
                kev = s.op("vector", lambda v, mq=mq: v.tensor_mul(
                    kd_h[:, mq * T:(mq + 1) * T], psqk[:, :], embt[:, :]),
                    waits=[lastk, emo, P.get(("kd", mq))], tag=f"kev{gq}")
                qk_last_ev = [kev]
                P[("sl", i2)] = lastk
                P["ebt"] = qev
                P["embt"] = kev
                mq_done.append([qev, kev])
            w2 = s.op("vector", lambda v: v.tensor_sub(
                wexp[:, :], bl[:, :], zs[:, :]),
                waits=[mq_done, P.get("wexpg")], tag=f"wx1{h}")
            w0 = s.op("vector", lambda v: v.tensor_copy(
                zsh[:, 0:31], zs[:, 1:32]), waits=[mq_done, P.get("zshg")],
                tag=f"zsh{h}")
            w1 = s.op("vector", lambda v: v.memset(
                zsh[:, :].rearrange("p (m cc) -> p m cc", m=4)[:, :, 7:8], 0.0),
                waits=[w0], tag=f"zshz{h}")
            w3 = s.op("vector", lambda v: v.tensor_add(
                wexp[:, :], wexp[:, :], zsh[:, :]), waits=[w2, w1], tag=f"wx2{h}")
            w4 = s.op("scalar", lambda sc2: sc2.activation(
                wexp[:, :], wexp[:, :], AF.Exp, scale=1.0 / 16.0),
                waits=[w3], tag=f"wx3{h}")
            P["zshg"] = w3
            tr_cp = []
            prev_cp = P.get("kdtm")
            for ccc in range(8):
                for mq in range(4):
                    tr = s.op("tensor", lambda t, ccc=ccc, mq=mq: t.transpose(
                        psOb[:, :],
                        kd_h[:, mq * T + ccc * 128:mq * T + ccc * 128 + 128],
                        c["ident"][:, :]),
                        waits=[mq_done[mq], prev_cp], tag=f"ktr{h}")
                    cp = s.op("vector", lambda v, ccc=ccc, mq=mq: v.tensor_copy(
                        kdtm[:, ccc * 512 + mq * 128:ccc * 512 + mq * 128 + 128],
                        psOb[:, :]), waits=[tr], tag=f"ktc{h}")
                    prev_cp = cp
                    tr_cp.append(cp)
            z1 = s.op("vector", lambda v, h=h: v.memset(
                spack[:, h * 1024:(h + 1) * 1024], 0.0), waits=[bar2],
                tag=f"sz{h}")
            z2 = s.op("vector", lambda v: v.memset(sb_S[:, :], 0.0),
                      waits=[P.get("sbS")], tag=f"sbz{h}")
            prev_asb = P.get("asbg")
            prev_ocp = None
            prev_upd = [z1] * 4
            prev_cast = [z2] * 4
            for ccc in range(8):
                lastA = None
                for mq in range(4):
                    lastA = s.op("tensor", lambda t, ccc=ccc, mq=mq: t.matmul(
                        psA[:, :],
                        kd_h[:, mq * T + ccc * 128:mq * T + ccc * 128 + 128],
                        qe_h[:, mq * T + ccc * 128:mq * T + ccc * 128 + 128],
                        start=(mq == 0), stop=(mq == 3)),
                        waits=[mq_done, prev_asb], tag=f"Amm{h}{ccc}")
                aev = s.op("vector", lambda v: v.tensor_mul(
                    asb[:, :], psA[:, :], tril[:, :]), waits=[lastA, trl],
                    tag=f"Aev{h}{ccc}")
                prev_asb = aev
                o_mms = [s.op("tensor", lambda t, ccc=ccc, h=h: t.matmul(
                    psO[:, :], asb[:, :],
                    v_tm[:, ccc * T + h * 256:ccc * T + h * 256 + 256],
                    start=True, stop=(ccc == 0)),
                    waits=[aev, v_ev[ccc], prev_ocp], tag=f"Omm{h}{ccc}")]
                if ccc > 0:
                    for mq in range(4):
                        o_mms.append(s.op("tensor", lambda t, ccc=ccc, mq=mq:
                                          t.matmul(
                                              psO[:, :],
                                              qe_h[:, mq * T + ccc * 128:mq * T + ccc * 128 + 128],
                                              sb_S[:, mq * 256:(mq + 1) * 256],
                                              start=False, stop=(mq == 3)),
                                          waits=[prev_cast[mq]],
                                          tag=f"O2mm{h}{ccc}"))
                ocp = s.op("vector", lambda v, ccc=ccc, h=h: v.tensor_copy(
                    o_sb[:, ccc * T + h * 256:ccc * T + h * 256 + 256],
                    psO[:, :]), waits=o_mms, tag=f"Ocp{h}{ccc}")
                prev_ocp = ocp
                o_cp_last = ocp
                t_mms = []
                for mq in range(4):
                    t_mms.append(s.op("tensor", lambda t, ccc=ccc, mq=mq, h=h:
                                      t.matmul(
                                          psT[:, mq * 256:(mq + 1) * 256],
                                          kdtm[:, ccc * 512 + mq * 128:ccc * 512 + mq * 128 + 128],
                                          v_tm[:, ccc * T + h * 256:ccc * T + h * 256 + 256],
                                          start=True, stop=True),
                                      waits=[tr_cp[ccc * 4 + mq], v_ev[ccc],
                                             prev_upd[mq]], tag=f"Tmm{h}{ccc}"))
                for mq in range(4):
                    sl_ = spack[:, h * 1024 + mq * 256:h * 1024 + (mq + 1) * 256]
                    u1 = s.op("vector", lambda v, sl_=sl_, mq=mq: v.tensor_add(
                        sl_, sl_, psT[:, mq * 256:(mq + 1) * 256]),
                        waits=[t_mms[mq], prev_upd[mq]], tag=f"su{h}{ccc}")
                    u2 = s.op("vector", lambda v, sl_=sl_, mq=mq, ccc=ccc:
                              v.tensor_scalar_mul(
                                  sl_, sl_,
                                  wexp[:, mq * 8 + ccc:mq * 8 + ccc + 1]),
                              waits=[u1, w4], tag=f"sm{h}{ccc}")
                    prev_upd[mq] = u2
                    if ccc < 7:
                        u3 = s.op("vector", lambda v, sl_=sl_, mq=mq:
                                  v.tensor_copy(
                                      sb_S[:, mq * 256:(mq + 1) * 256], sl_),
                                  waits=[u2, o_mms], tag=f"scast{h}{ccc}")
                        prev_cast[mq] = u3
            st_qe[h] = s.dma("sync", qedram[:, h * 4096:(h + 1) * 4096],
                             qe_h[:, :], waits=[prev_asb, prev_ocp],
                             tag=f"stqe{h}")
            P["kdtm"] = prev_upd
            for mq in range(4):
                P[("kd", mq)] = [prev_asb, tr_cp[-1]]
                P[("qe", mq)] = [prev_asb, prev_ocp]
            P["sbS"] = prev_cast
            P["asbg"] = prev_asb
            P["wexpg"] = prev_upd
            fo = []
            for mq in range(4):
                fo.append(s.op("vector", lambda v, mq=mq: v.reduce_sum(
                    misc[:, mq:mq + 1], bl[:, mq * 8:(mq + 1) * 8], AX.X),
                    waits=[mq_done, P.get("misc")], tag=f"bls{h}{mq}"))
            a1 = s.op("scalar", lambda sc2, h=h: sc2.activation(
                a_sb[:, h * 4:(h + 1) * 4], misc[:, 0:4], AF.Exp,
                scale=1.0 / 16.0), waits=fo, tag=f"aex{h}")
            P["misc"] = a1
            c1_ = s.op("vector", lambda v: v.tensor_tensor_scan(
                misc[:, 32:64], mask8[:, :], bl[:, 0:32], 0.0,
                ALU.mult, ALU.add), waits=[mq_done, P.get("misc2"), m8l],
                tag=f"ccs{h}")
            c2_ = s.op("vector", lambda v: v.tensor_sub(
                misc[:, 32:64], misc[:, 32:64], bl[:, 0:32]), waits=[c1_],
                tag=f"ccsub{h}")
            c3_ = s.op("vector", lambda v: v.tensor_add(
                misc[:, 32:64], misc[:, 32:64], zs[:, 0:32]), waits=[c2_],
                tag=f"ccadd{h}")
            c4_ = s.op("scalar", lambda sc2, h=h: sc2.activation(
                cc_sb[:, h * 32:(h + 1) * 32], misc[:, 32:64], AF.Exp,
                scale=1.0 / 16.0), waits=[c3_], tag=f"ccex{h}")
            P["misc2"] = c4_
            P["zsg"] = [w4, c4_]
            P["blg"] = [w4, a1, c4_]
            final_ops += [a1, c4_]
            ld2p = lastg
        cx.cur = p3
        d_sub.close()
        sbin = cx.sb("sbin3", [128, 4096], BF16)
        og_fm = cx.sb("og_fm3", [128, 8 * T], BF16)
        st_sp = s.dma("sync", sp_in[:, 0:4096], spack[:, :],
                      waits=[P["kdtm"]], tag="stsp")
        st_a = s.dma("sync", sp_in[:, 4096:4112], a_sb[:, :], waits=final_ops,
                     tag="sta")
        if SIM1:
            cc2 = [s.dma("gpsimd", sp_out[j * 128:(j + 1) * 128, :], sp_in,
                         waits=[st_sp, st_a], tag=f"cc2sim{j}") for j in range(4)]
        else:
            cc2 = s.op("gpsimd", lambda g: g.collective_compute(
                "AllGather", ALU.bypass, replica_groups=RG, ins=[sp_in.opt()],
                outs=[sp_out.opt()]), waits=[st_sp, st_a], async_inc=1,
                tag="cc2")
        bar3a = _phase_barrier(cx, "b3a")
        sc1 = ExitStack()
        T_acc = cx.sb("T_acc3", [128, 4096], F32, stack=sc1)
        S_in = cx.sb("S_in3", [128, 4096], F32, stack=sc1)
        rotS = [cx.sb(f"rotS3{i}", [128, 4096], F32, stack=sc1) for i in range(2)]
        a_gt = cx.sb("a_gt3", [128, 64], F32, stack=sc1)
        ld_a = s.dma("sync", a_gt[:, :].rearrange("p (g m) -> p g m", g=4),
                     sp_out.rearrange("(g p) m -> p g m", p=128)[:, :, 4096:4112],
                     waits=[cc2, bar3a], tag="lda3")
        ld_S = []
        for j in range(4):
            ld_S.append(s.dma("sync", rotS[j % 2][:, :],
                              sp_out[j * 128:(j + 1) * 128, 0:4096],
                              waits=[cc2, bar3a], tag=f"ldS3{j}"))
        cp0 = s.op("vector", lambda v: v.tensor_copy(T_acc[:, :], rotS[0][:, :]),
                   waits=[ld_S[0]], tag="cp03")
        si0 = s.op("vector", lambda v: v.tensor_scalar_mul(
            S_in[:, :], T_acc[:, :], sel[:, 0:1]), waits=[cp0, ld_sel],
            tag="si03")
        ld_S[2].waits.append(cp0)
        prevsi = si0
        prevT = [cp0]
        for j in range(1, 4):
            tops = []
            for hm in range(16):
                tops.append(s.op("vector", lambda v, j=j, hm=hm:
                                 v.scalar_tensor_tensor(
                                     T_acc[:, hm * 256:(hm + 1) * 256],
                                     T_acc[:, hm * 256:(hm + 1) * 256],
                                     a_gt[:, j * 16 + hm:j * 16 + hm + 1],
                                     rotS[j % 2][:, hm * 256:(hm + 1) * 256],
                                     ALU.mult, ALU.add),
                                 waits=[ld_S[j], ld_a, prevT, prevsi],
                                 tag=f"tj3{j}"))
            if j + 2 < 4:
                ld_S[j + 2].waits.extend(tops)
            prevT = tops
            prevsi = s.op("vector", lambda v, j=j: v.scalar_tensor_tensor(
                S_in[:, :], T_acc[:, :], sel[:, j:j + 1], S_in[:, :],
                ALU.mult, ALU.add), waits=[tops, prevsi], tag=f"si3{j}")
        cbin = s.op("vector", lambda v: v.tensor_copy(sbin[:, :], S_in[:, :]),
                    waits=[prevsi], tag="cbin3")
        cx.cur = p3
        sc1.close()
        sc2c = ExitStack()
        qrot = [cx.sb(f"qrot3{i}", [128, 4096], BF16, stack=sc2c)
                for i in range(2)]
        qs = cx.sb("qs3", [128, 128], BF16, stack=sc2c)
        psC = cx.psum("psC3", [128, 256], F32, stack=sc2c)
        prev_oadd = None
        prev_qs = None
        corr_ops = []
        h_last = {}
        for h in range(4):
            ldq_h = s.dma("sync", qrot[h % 2][:, :],
                          qedram[:, h * 4096:(h + 1) * 4096],
                          waits=[h_last.get(h % 2), st_qe[h],
                                 cbin if h < 2 else None], tag=f"ldqc{h}")
            for ccc in range(8):
                cmms = []
                for mq in range(4):
                    qso = s.op("vector", lambda v, h=h, ccc=ccc, mq=mq:
                               v.tensor_scalar_mul(
                                   qs[:, :],
                                   qrot[h % 2][:, mq * T + ccc * 128:mq * T + ccc * 128 + 128],
                                   cc_sb[:, h * 32 + mq * 8 + ccc:h * 32 + mq * 8 + ccc + 1]),
                               waits=[ldq_h, prev_qs], tag=f"qsc{h}{ccc}")
                    cmms.append(s.op("tensor", lambda t, h=h, mq=mq: t.matmul(
                        psC[:, :], qs[:, :],
                        sbin[:, (h * 4 + mq) * 256:(h * 4 + mq + 1) * 256],
                        start=(mq == 0), stop=(mq == 3)),
                        waits=[qso, cbin, prev_oadd], tag=f"cmm3{h}{ccc}"))
                    prev_qs = cmms[-1]
                oadd = s.op("vector", lambda v, h=h, ccc=ccc: v.tensor_add(
                    o_sb[:, ccc * T + h * 256:ccc * T + h * 256 + 256],
                    o_sb[:, ccc * T + h * 256:ccc * T + h * 256 + 256],
                    psC[:, :]), waits=[cmms], tag=f"oadd3{h}{ccc}")
                prev_oadd = oadd
                corr_ops.append(oadd)
            h_last[h % 2] = prev_qs
        cx.cur = p3
        sc2c.close()
        sc4 = ExitStack()
        cx.cur = sc4
        dump = cx.sb("dump3", [128, 256], F32)
        racc = cx.sb("racc3", [128, 32], F32)
        rstd = cx.sb("rstd3", [128, 32], F32)
        wnb = cx.sb("wnb3", [128, T], BF16)
        wrow = cx.sb("wrow3", [1, T], BF16)
        ld_wr = s.dma("gpsimd", wrow[:, :], wnorm, waits=[corr_ops[-1]],
                      tag="ldwr3")
        racc_ops = []
        prev_d = None
        for tc in range(8):
            for h in range(4):
                prev_d = s.op("scalar", lambda sc2, tc=tc, h=h: sc2.activation(
                    dump[:, :],
                    o_sb[:, tc * T + h * 256:tc * T + h * 256 + 256],
                    AF.Square, accum_out=racc[:, tc * 4 + h:tc * 4 + h + 1]),
                    waits=[corr_ops, prev_d], tag=f"sq3{tc}{h}")
                racc_ops.append(prev_d)
        r1 = s.op("scalar", lambda sc2: sc2.activation(
            rstd[:, :], racc[:, :], AF.Sqrt, scale=1.0 / 256.0,
            bias=c["eps_col"][:, 0:1]), waits=racc_ops, tag="rsq3")
        r2 = s.op("vector", lambda v: v.reciprocal(rstd[:, :], rstd[:, :]),
                  waits=[r1], tag="rrec3")
        psW = cx.psum("psW3", [128, 512], F32)
        wb_ops = []
        prevw = None
        for h0 in range(0, T, 512):
            mm = s.op("tensor", lambda t, h0=h0: t.matmul(
                psW[:, :], c["ones_row_b"][0:1, :], wrow[0:1, h0:h0 + 512],
                start=True, stop=True),
                waits=[ld_wr, prevw, c["ld"], corr_ops[-1]], tag="wbc3")
            prevw = s.op("vector", lambda v, h0=h0: v.tensor_copy(
                wnb[:, h0:h0 + 512], psW[:, :]), waits=[mm], tag="wbcc3")
            wb_ops.append(prevw)
        wg_sb = cx.sb("wg_sb3", [128, 16 * 1024], BF16)
        sg_t = cx.sb("sg_t3", [128, T], BF16)
        psg = [cx.psum(f"psg3{i}", [128, 1024], F32) for i in range(2)]
        cpx = None
        ld_wg = s.dma("gpsimd", wg_sb[:, :], wg_nat,
                      waits=[corr_ops[-1]], tag="ldwg3")
        og_ops = []
        prev_sg = None
        for tc in range(8):
            i = tc % 2
            last = None
            for col, h0, hw in pieces_cols(PIECES_T):
                for kc in range(16):
                    wl = ([ld_wg, apps[-1], og_ops[-8:] if tc >= 2 else None]
                          if (kc == 0 and col == 0) else [])
                    last = s.op("tensor", lambda t, i=i, kc=kc, tc=tc, col=col,
                                h0=h0, hw=hw: t.matmul(
                                    psg[i][:, col:col + hw],
                                    xb3[:, kc * T + tc * 128:kc * T + tc * 128 + 128],
                                    wg_sb[:, kc * 1024 + h0:kc * 1024 + h0 + hw],
                                    start=(kc == 0), stop=(kc == 15)),
                                waits=wl, tag=f"gmm3{tc}")
            sg = s.op("scalar", lambda sc2, i=i: sc2.activation(
                sg_t[:, :], psg[i][:, :], AF.Silu), waits=[last, prev_sg],
                tag=f"gsil3{tc}")
            gt = s.op("vector", lambda v: v.tensor_mul(
                sg_t[:, :], sg_t[:, :], wnb[:, :]), waits=[sg, wb_ops],
                tag=f"gwn3{tc}")
            tc_ops = []
            for h in range(4):
                tc_ops.append(s.op("vector", lambda v, tc=tc, h=h:
                                   v.scalar_tensor_tensor(
                                       o_sb[:, tc * T + h * 256:tc * T + h * 256 + 256],
                                       o_sb[:, tc * T + h * 256:tc * T + h * 256 + 256],
                                       rstd[:, tc * 4 + h:tc * 4 + h + 1],
                                       sg_t[:, h * 256:(h + 1) * 256],
                                       ALU.mult, ALU.mult),
                                   waits=[gt, r2, corr_ops, racc_ops],
                                   tag=f"og3{tc}{h}"))
            og_ops.extend(tc_ops)
            prev_sg = tc_ops
        psTr = cx.psum("psTr3", [128, 128], BF16)
        prev_cp = None
        tr_cps = {}
        for tc in range(8):
            for vc in range(8):
                tr = s.op("tensor", lambda t, tc=tc, vc=vc: t.transpose(
                    psTr[:, :],
                    o_sb[:, tc * T + vc * 128:tc * T + vc * 128 + 128],
                    c["ident"][:, :]), waits=[og_ops, prev_cp], tag=f"ogtr3{tc}")
                cp = s.op("vector", lambda v, tc=tc, vc=vc: v.tensor_copy(
                    og_fm[:, vc * T + tc * 128:vc * T + tc * 128 + 128],
                    psTr[:, :]), waits=[tr], tag=f"ogtc3{tc}")
                prev_cp = cp
                tr_cps[vc] = cp
        cx.cur = p3
        sc4.close()
        bar3b = _phase_barrier(cx, "b3b")
        cx.cur = p3
        pstwo = [cx.psum(f"ppwo{i}", [128, 1024], F32) for i in range(2)]
        p3_stream = []

        def evac_wo(m, ps, pieces, waits):
            ops = []
            for col, h0, hw in pieces_cols(pieces):
                ops.append(s.op("vector", lambda v, m=m, ps=ps, col=col, h0=h0,
                                hw=hw: v.tensor_add(
                                    stream[:, m * T + h0:m * T + h0 + hw],
                                    stream[:, m * T + h0:m * T + h0 + hw],
                                    ps[:, col:col + hw]),
                                waits=waits + [p2_stream], tag=f"evwo{m}"))
            p3_stream.extend(ops)
            return ops

        stream_matmul(cx, "wo", wo_t, 16, 1024,
                      lambda kc, h0, hw: og_fm[:, kc * T + h0:kc * T + h0 + hw],
                      lambda kc: [tr_cps[kc], bar3b], evac_wo, PIECES_T, pstwo,
                      first_waits=[bar3b], rhs_cover=[tr_cps[7], bar3b])
        cx.cur = None
        p3.close()
        bar3 = _phase_barrier(cx, "b3")
        if upto == 3:
            _fin_store(cx, stream, o_out, bar3)
            s.emit(cx.stack)
            return cx.nc

        p4 = ExitStack()
        cx.cur = p4
        qx4 = cx.sb("qx4", [128, 16 * T], FP8)
        qa = cx.sb("qa4", [128, 64 * T], FP8)
        gb = cx.sb("gb4", [128, T], BF16)
        gelb = cx.sb("gelb4", [128, T], BF16)
        ub = cx.sb("ub4", [128, T], BF16)
        actb = cx.sb("actb4", [128, T], BF16)
        dscr4 = cx.sb("dscr4", [128, T], BF16)
        outst = [cx.sb(f"outst{i}", [128, T], F32) for i in range(2)]
        ln4 = ExitStack()
        bc4, bc4_ops = ln_stream_stats(cx, c, stream, bar3, ln4, "l4")
        scr4 = cx.sb("scr4l", [128, T], F32, stack=ln4)
        q_ops4 = []
        prevq = None
        for ci in range(16):
            a2 = ln_apply_sb(cx, stream, bc4, scr4,
                             qx4[:, ci * T:(ci + 1) * T], ci,
                             [bc4_ops, prevq], "x4")
            prevq = a2
            q_ops4.append(a2)
        ln4.close()
        slg0 = [cx.sb(f"slg04{i}", [128, D], FP8) for i in range(2)]
        slg1 = [cx.sb(f"slg14{i}", [128, D], FP8) for i in range(2)]
        scg = ExitStack()
        psG4 = [cx.psum(f"psG4{i}", [128, 1024], F32, stack=scg) for i in range(2)]
        psU4 = [cx.psum(f"psU4{i}", [128, 1024], F32, stack=scg) for i in range(2)]
        ev_hist4 = []
        mm_hist4 = []
        qa_ops = []
        P4d = {}
        for m in range(64):
            i = m % 2
            wprev = [mm_hist4[m - 2]] if m >= 2 else [q_ops4[-1], bar3]
            ldg = s.dma("gpsimd", slg0[i][:, :], wg0_t[m],
                        waits=wprev, tag=f"ldg04{m}")
            ldu = s.dma("gpsimd", slg1[i][:, :], wg1f_t[m],
                        waits=wprev, tag=f"ldg14{m}")
            evp = ev_hist4[m - 2] if m >= 2 else []
            lastG = lastU = None
            qx4r = qx4[:, :].rearrange("p (c t) -> p c t", c=16)
            for ps, sl, ldx_ in ((psG4[i], slg0[i], ldg), (psU4[i], slg1[i], ldu)):
                slr = sl[:, :].rearrange("p (c n) -> p c n", c=16)
                for col, h0, hw in pieces_cols(PIECES_T):
                    for kp in range(8):
                        wl = ([ldx_, q_ops4[-1], evp]
                              if (kp == 0 and col == 0) else [])
                        last = s.op("tensor", lambda t, ps=ps, slr=slr, kp=kp,
                                    col=col, h0=h0, hw=hw: t.matmul(
                                        ps[:, col:col + hw],
                                        slr[:, 2 * kp:2 * kp + 2, :],
                                        qx4r[:, 2 * kp:2 * kp + 2, h0:h0 + hw],
                                        start=(kp == 0), stop=(kp == 7),
                                        perf_mode=DR),
                                    waits=wl, tag=f"fmm4{m}")
                if ps is psG4[i]:
                    lastG = last
                else:
                    lastU = last
            mm_hist4.append(lastU)
            e1 = s.op("vector", lambda v, i=i: v.tensor_copy(
                gb[:, :], psG4[i][:, :]), waits=[lastG, P4d.get("gb")],
                tag=f"gcp4{m}")
            e2 = s.op("scalar", lambda sc2: sc2.activation(
                gelb[:, :], gb[:, :], AF.Gelu_apprx_tanh),
                waits=[e1, P4d.get("gelb")], tag=f"gel4{m}")
            e3 = s.op("scalar", lambda sc2, i=i: sc2.copy(
                ub[:, :], psU4[i][:, :]), waits=[lastU, P4d.get("ub"), e2],
                tag=f"ucp4{m}")
            e4 = s.op("vector", lambda v: v.tensor_mul(
                actb[:, :], gelb[:, :], ub[:, :]),
                waits=[e2, e3, P4d.get("actb")], tag=f"amul4{m}")
            e5 = s.op("gpsimd", lambda g, m=m: g.tensor_copy(
                qa[:, m * T:(m + 1) * T], actb[:, :]), waits=[e4],
                tag=f"qa4{m}")
            P4d["gb"] = e2
            P4d["gelb"] = e4
            P4d["ub"] = e4
            P4d["actb"] = e5
            ev_hist4.append([e1, e3])
            qa_ops.append(e5)
        scg.close()
        pst4 = [cx.psum(f"ppf4{i}", [128, 1024], F32) for i in range(2)]
        st_ops4 = []
        P4e = {}

        def evac_dn4(m, ps, pieces, waits):
            ops = []
            for col, h0, hw in pieces_cols(pieces):
                o1 = s.op("vector", lambda v, ps=ps, col=col, hw=hw:
                          v.tensor_copy(dscr4[:, col:col + hw],
                                        ps[:, col:col + hw]),
                          waits=waits + [P4d.get("dscr")], tag=f"dcp4{m}")
                o2 = s.op("vector", lambda v, m=m, col=col, h0=h0, hw=hw:
                          v.tensor_add(
                              outst[m % 2][:, h0:h0 + hw],
                              stream[:, m * T + h0:m * T + h0 + hw],
                              dscr4[:, col:col + hw]),
                          waits=[o1, p3_stream, P4e.get(("st", m % 2))],
                          tag=f"dadd4{m}")
                ops.append(o2)
            P4d["dscr"] = ops
            st = s.dma("sync", o_out[:, m * T:(m + 1) * T], outst[m % 2][:, :],
                       waits=ops, tag=f"stf4{m}")
            P4e[("st", m % 2)] = st
            st_ops4.append(st)
            return ops

        qar = qa[:, :].rearrange("p (c t) -> p c t", c=64)
        stream_matmul(cx, "dnf", wl_t, 16, I1,
                      lambda kc, h0, hw: qa[:, kc * T + h0:kc * T + h0 + hw],
                      lambda kc: qa_ops[kc], evac_dn4, PIECES_T, pst4,
                      sb_dtype=FP8, first_waits=[qa_ops[1]],
                      psum_prev=[ev_hist4[-2] + [qa_ops[-1]], ev_hist4[-1]],
                      rhs_cover=qa_ops[-1],
                      pair_rhs3=lambda kp, h0, hw:
                      qar[:, 2 * kp:2 * kp + 2, h0:h0 + hw])
        s.op("sync", lambda e: e.nop(), waits=st_ops4, tag="fin4")
        cx.cur = None
        p4.close()
        s.emit(cx.stack)
        return cx.nc


_BUILT = {}


def _get(name, fn):
    if name not in _BUILT:
        _BUILT[name] = fn()
    return _BUILT[name]


def kernel(**inputs):
    inp = {k: np.asarray(v) for k, v in inputs.items()}
    cores = list(range(NCORE))
    ranges = shard_ranges()
    x = inp["L0_attn_x"]
    xsh = prep_x_shards(x)
    Pidx = np.asarray(inp["L0_ffn_down_reorder_index"]).astype(np.int64)
    _up_idx_unused = np.asarray(inp["L0_ffn_up_reorder_index"])
    wg_p = np.ascontiguousarray(np.asarray(inp["L0_ffn_w_gate"])[Pidx, :])
    wu_p = np.ascontiguousarray(np.asarray(inp["L0_ffn_w_up"])[Pidx, :])
    wd = np.ascontiguousarray(np.asarray(inp["L0_ffn_w_down"]))
    com = dict(
        wg1t=tile_lhsT_p(np.ascontiguousarray(inp["L0_attn_Wg1"].T)),
        wg2t=tile_lhsT_p(np.ascontiguousarray(inp["L0_attn_Wg2"].T)),
        wut=tile_lhsT_p(np.ascontiguousarray(inp["L0_attn_Wu"].T)),
        wvt=tile_lhsT_p(np.ascontiguousarray(inp["L0_attn_Wv"].T)),
        bg1=col16(inp["L0_attn_bg1"]), bg2=col16(inp["L0_attn_bg2"]),
        bu=col16(inp["L0_attn_bu"]), bv=col16(inp["L0_attn_bv"]),
        cw=np.ascontiguousarray(
            inp["L0_attn_conv_w"][:, 0, :].reshape(16, 128, 3)
            .transpose(1, 0, 2).reshape(128, 48)),
        cb=col16(inp["L0_attn_conv_b"]),
        wgba=wg_p, wgbt=np.ascontiguousarray(wg_p.T),
        wuba=wu_p, wubt=np.ascontiguousarray(wu_p.T),
        wdna=wd, wdnt=np.ascontiguousarray(wd.T),
        wq_t=tile_lhsT_p(np.ascontiguousarray(inp["L1_attn_Wq"])),
        wk_t=tile_lhsT_p(np.ascontiguousarray(inp["L1_attn_Wk"])),
        wv_nat=fm_layout(np.ascontiguousarray(inp["L1_attn_Wv"])),
        wgk1p=np.ascontiguousarray(
            inp["L1_attn_Wgk1"].reshape(16, 128, 16).transpose(1, 0, 2)
            .reshape(128, 256)),
        wgk2=np.ascontiguousarray(inp["L1_attn_Wgk2"]),
        bgk2=col16(inp["L1_attn_bgk2"]),
        wg_nat=fm_layout(np.ascontiguousarray(inp["L1_attn_Wg"])),
        wnorm=np.ascontiguousarray(
            np.tile(np.asarray(inp["L1_attn_Wnorm"]), 4)[None, :]
            .astype(np.float32)),
        wo_t=tile_lhsT_p(np.ascontiguousarray(inp["L1_attn_Wo"])),
        wg0_t=tile_lhsT_p(np.ascontiguousarray(inp["L1_ffn_w_gating"][0])),
        wg1f_t=tile_lhsT_p(np.ascontiguousarray(inp["L1_ffn_w_gating"][1])),
        wl_t=tile_lhsT_p(np.ascontiguousarray(inp["L1_ffn_w_linear"])),
    )
    in_maps = []
    for c in cores:
        pmask = np.zeros((128, 4), np.float32)
        pmask[:, :c % 4] = 1.0
        sel = np.zeros((128, 4), np.float32)
        if c % 4 > 0:
            sel[:, c % 4 - 1] = 1.0
        in_maps.append(dict(com, x=xsh[c], pmask=pmask, selm=sel))
    nc = _get("fused", build_fused)
    res = run_bass_kernel_spmd(nc, in_maps, core_ids=cores)
    out = np.zeros((2, 4096, 2048), np.float32)
    for c, (b, t0, t1) in enumerate(ranges):
        out[b, t0:t1] = fm_unlayout(np.asarray(res.results[c]["out"]), T).T
    return out

